# version 09
# baseline (speedup 1.0000x reference)
import numpy as np
import contextlib
import concourse.bass as bass
import concourse.mybir as mybir
from concourse.bass_utils import run_bass_kernel_spmd

F32 = mybir.dt.float32
BF16 = mybir.dt.bfloat16
AF = mybir.ActivationFunctionType
ALU = mybir.AluOpType
AX = mybir.AxisListType

NCORES = 8
DM = 1024
H = 16
DH = 64
E = 2048
NEXP = 64
PSLOT = 32
HID = 1024
TT = 512
LN_EPS = 1e-5
QSCALE = 0.125
CLIP_LO = -33000.0
CLIP_HI = 65000.0
EPC = NEXP // NCORES
DEBUG_STOP = None
DEBUG_SCRATCH = False
ESL = E // NCORES


class Sem:
    __slots__ = ("h", "n")

    def __init__(self, h):
        self.h = h
        self.n = 0


def inc1(ins, sem):
    sem.n += 1
    ins.then_inc(sem.h, 1)
    return sem.n


def inc16(ins, sem):
    sem.n += 16
    ins.then_inc(sem.h, 16)
    return sem.n


def dma_tr(sy, out_ap, in_ap, sem):
    F = out_ap.shape[1]
    tgt = sem.n
    for f0 in range(0, F, 512):
        f1 = min(f0 + 512, F)
        ins = sy.dma_start(out=out_ap[:, f0:f1], in_=in_ap[f0:f1, :], transpose=True)
        tgt = inc16(ins, sem)
    return tgt




def _debug_dump(nc, block, dhw, dsw, d_sp, out_h, T, waits):
    @block.gpsimd
    def _(g):
        for s, tgt in waits:
            g.wait_ge(s.h, tgt)
        g.dma_start(out=out_h[:, :], in_=d_sp[0:T, 0:DM]).then_inc(dsw.h, 16)
        dsw.n += 16
        g.wait_ge(dsw.h, dsw.n)

def build_program(T=8192):
    assert T % TT == 0
    NT = T // TT
    NSUB = TT // 128
    NSUBT = T // 128

    nc = bass.Bass("TRN2", target_bir_lowering=False, detect_race_conditions=False)

    x_h = nc.dram_tensor("x", [T, DM], F32, kind="ExternalInput")
    wqkv_h = nc.dram_tensor("wqkv", [3 * DM, DM], F32, kind="ExternalInput")
    wo_h = nc.dram_tensor("wo", [DM, DM], F32, kind="ExternalInput")
    phiw_h = nc.dram_tensor("phiw", [E, DM], F32, kind="ExternalInput")
    phib_h = nc.dram_tensor("phib", [1, E], F32, kind="ExternalInput")
    ew_h = nc.dram_tensor("ew", [EPC, DM, HID], F32, kind="ExternalInput")
    eb_h = nc.dram_tensor("eb", [1, EPC * HID], F32, kind="ExternalInput")
    out_h = nc.dram_tensor("out", [T, DM], F32, kind="ExternalOutput")

    wqkv_bs = nc.dram_tensor("wqkv_bs", [3 * DM, DM], BF16)
    phiw_bs = nc.dram_tensor("phiw_bs", [E, DM], BF16)
    wo_bs = nc.dram_tensor("wo_bs", [DM, DM], BF16)
    xn_sp = nc.dram_tensor("xn_sp", [T, DM], BF16)
    k_sp = nc.dram_tensor("k_sp", [DM, T], BF16)
    attn_sp = nc.dram_tensor("attn_sp", [DM, T], BF16)
    c_sp = nc.dram_tensor("c_sp", [T, E], BF16)
    d_sp = nc.dram_tensor("d_sp", [T, E], BF16)
    x_bounce = nc.dram_tensor("x_bounce", [E, DM], F32)
    x_rs = nc.dram_tensor("x_rs", [ESL, DM], F32)
    yt_slice = nc.dram_tensor("yt_slice", [ESL, HID], BF16)
    yt_full = nc.dram_tensor("yt_full", [E, HID], BF16)

    st = contextlib.ExitStack()
    with st:
        dhw = Sem(st.enter_context(nc.semaphore("dhw")))
        dtr = Sem(st.enter_context(nc.semaphore("dtr")))
        dsc = Sem(st.enter_context(nc.semaphore("dsc")))
        dsw = Sem(st.enter_context(nc.semaphore("dsw")))
        cc = Sem(st.enter_context(nc.semaphore("cc")))
        ts = Sem(st.enter_context(nc.semaphore("ts")))
        vs = Sem(st.enter_context(nc.semaphore("vs")))
        ss = Sem(st.enter_context(nc.semaphore("ss")))
        gs = Sem(st.enter_context(nc.semaphore("gs")))

        pS = st.enter_context(nc.psum_tensor("pS", [128, 512], F32))
        pPool = st.enter_context(nc.psum_tensor("pPool", [128, 2560], F32))
        pT = st.enter_context(nc.psum_tensor("pT", [128, 1024], BF16))

        S_sb = st.enter_context(nc.sbuf_tensor("S_sb", [128, 1024], BF16))
        maskT = st.enter_context(nc.sbuf_tensor("maskT", [128, 128], BF16))
        ident = st.enter_context(nc.sbuf_tensor("ident", [128, 128], BF16))
        ones1 = st.enter_context(nc.sbuf_tensor("ones1", [1, 128], BF16))
        zeros = st.enter_context(nc.sbuf_tensor("zerosb", [1, 512], BF16))
        zk = st.enter_context(nc.sbuf_tensor("zk", [128, 512], BF16))
        phib_sb = st.enter_context(nc.sbuf_tensor("phib_sb", [1, E], BF16))
        eps_sb = st.enter_context(nc.sbuf_tensor("eps_sb", [128, 1], F32))

        block = st.enter_context(nc.Block())

        b01_free = (ss, 0)
        b23_free = (vs, 0)

        @block.gpsimd
        def _(g):
            g.memset(maskT[:], 1.0)
            g.memset(ident[:], 0.0)
            g.memset(ones1[:], 1.0)
            g.memset(zeros[:], 0.0)
            g.memset(zk[:], 0.0)
            g.memset(S_sb[:], 0.0)
            g.memset(eps_sb[:], LN_EPS)
            g.affine_select(out=maskT[:], in_=maskT[:], compare_op=ALU.is_ge,
                            fill=0.0, base=0, pattern=[[1, 128]], channel_multiplier=-1)
            ins_last = g.affine_select(out=ident[:], in_=ident[:], compare_op=ALU.not_equal,
                            fill=1.0, base=0, pattern=[[-1, 128]], channel_multiplier=1)
            g_consts = inc1(ins_last, gs)
            g.dma_start(out=wqkv_bs[:, :], in_=wqkv_h[:, :]).then_inc(dsw.h, 16)
            g.dma_start(out=phiw_bs[:, :], in_=phiw_h[:, :]).then_inc(dsw.h, 16)
            g.dma_start(out=wo_bs[:, :], in_=wo_h[:, :]).then_inc(dsw.h, 16)
            g.dma_start(out=phib_sb[:, :], in_=phib_h[:, :]).then_inc(dsw.h, 16)
            dsw.n += 64
        g_consts = gs.n
        d_wcast = dsw.n
        d_wcast1 = d_wcast - 48

        @block.tensor
        def _(t):
            t.wait_ge(gs.h, g_consts)
            t.matmul(pS[:, :], zk[:, 0:128], zk[:, 0:512],
                     start=True, stop=False, skip_group_check=True)

        with contextlib.ExitStack() as st1:
            wqkvT = st1.enter_context(nc.sbuf_tensor("wqkvT", [128, 8 * 3 * DM], BF16))
            phiT = st1.enter_context(nc.sbuf_tensor("phiT", [128, 8 * E], BF16))
            attn_buf = st1.enter_context(nc.sbuf_tensor("attn_buf", [128, 8 * TT], BF16))
            x_sub = st1.enter_context(nc.sbuf_tensor("x_sub", [128, 4 * DM], F32))

            d_wqkv = [0]

            @block.sync
            def _(sy):
                sy.wait_ge(dsw.h, d_wcast1)
                for dc in range(8):
                    dma_tr(sy, wqkvT[:, dc * 3 * DM:(dc + 1) * 3 * DM],
                           wqkv_bs[:, dc * 128:(dc + 1) * 128], dtr)
                d_wqkv[0] = dtr.n
                sy.wait_ge(dsw.h, d_wcast)
                for dc in range(8):
                    dma_tr(sy, phiT[:, dc * E:(dc + 1) * E],
                           phiw_bs[:, dc * 128:(dc + 1) * 128], dtr)
            d_wT = dtr.n
            d_wqkv = d_wqkv[0]
            xn_st2 = st1.enter_context(nc.sbuf_tensor("xn_st2", [128, 4 * DM], BF16))
            sqt = st1.enter_context(nc.sbuf_tensor("sqt", [128, 1024], F32))
            s1 = st1.enter_context(nc.sbuf_tensor("s1", [128, 4], F32))
            s2 = st1.enter_context(nc.sbuf_tensor("s2", [128, 4], F32))
            m_sb = st1.enter_context(nc.sbuf_tensor("m_sb", [128, 4], F32))
            vneg = st1.enter_context(nc.sbuf_tensor("vneg", [128, 4], F32))
            junk1 = st1.enter_context(nc.sbuf_tensor("junk1", [128, 1], F32))
            rstd = st1.enter_context(nc.sbuf_tensor("rstd", [128, 4], F32))
            xn_fm = st1.enter_context(nc.sbuf_tensor("xn_fm", [128, 2 * 8 * TT], BF16))
            q_fm = st1.enter_context(nc.sbuf_tensor("q_fm", [128, 8 * TT], BF16))
            k_fm = st1.enter_context(nc.sbuf_tensor("k_fm", [128, 8 * TT], BF16))
            k_tm = st1.enter_context(nc.sbuf_tensor("k_tm", [128, 2 * DM], BF16))
            v_tm = st1.enter_context(nc.sbuf_tensor("v_tm", [128, 2 * DM], BF16))
            scM = st1.enter_context(nc.sbuf_tensor("scM", [128, 16 * 128], BF16))
            tinv = st1.enter_context(nc.sbuf_tensor("tinv", [128, 2 * 128], F32))
            e1b = st1.enter_context(nc.sbuf_tensor("e1b", [128, 2 * E], BF16))
            c_ob = st1.enter_context(nc.sbuf_tensor("c_ob", [128, 2 * E], BF16))
            sD = st1.enter_context(nc.sbuf_tensor("sD", [128, 64], F32))
            sC = st1.enter_context(nc.sbuf_tensor("sC", [128, 32], F32))
            rD = st1.enter_context(nc.sbuf_tensor("rD", [128, 64], F32))
            rC = st1.enter_context(nc.sbuf_tensor("rC", [128, 32], F32))

            x_done = [0, 0, 0, 0]
            x_used = [0]
            xst_done = [0]
            e1_free = [0, 0]
            c_free = [0, 0]
            kv_used = [0, 0]
            v_evac = [0, 0]
            ktm_done = [0, 0]
            tinv_done = [0, 0]
            tinv_used = [0, 0]
            ic_evac_prev = 0
            scM_used_prev = 0
            attn_stored_prev = 0
            xnfm_used_slot = [0, 0]
            xnfm_done_slot = [0, 0]
            qk_used_prev = 0
            s_scopy = 0

            def emit_ln_stats(lt, s_i):
                tok0 = lt * TT
                r0 = tok0 + s_i * 128

                @block.sync
                def _(sy, s_i=s_i, r0=r0):
                    if x_used[0]:
                        sy.wait_ge(vs.h, x_used[0])
                    ins = sy.dma_start(out=x_sub[:, s_i * DM:(s_i + 1) * DM],
                                       in_=x_h[r0:r0 + 128, :])
                    x_done[s_i] = inc16(ins, dhw)

                @block.vector
                def _(v, s_i=s_i):
                    v.wait_ge(dhw.h, x_done[s_i])
                    xs = x_sub[:, s_i * DM:(s_i + 1) * DM]
                    v.tensor_mul(sqt[:], xs, xs)
                    v.reduce_sum(s1[:, s_i:s_i + 1], xs, axis=AX.X)
                    v.reduce_sum(s2[:, s_i:s_i + 1], sqt[:], axis=AX.X)
                    v.tensor_scalar_mul(m_sb[:, s_i:s_i + 1], s1[:, s_i:s_i + 1],
                                        1.0 / DM)
                    v.tensor_copy(junk1[:], s1[:, 0:1])
                    ins = v.scalar_tensor_tensor(vneg[:, s_i:s_i + 1],
                                                 s1[:, s_i:s_i + 1],
                                                 m_sb[:, s_i:s_i + 1],
                                                 s2[:, s_i:s_i + 1],
                                                 ALU.mult, ALU.subtract)
                    inc1(ins, vs)

            def emit_ln_tail(lt):
                tok0 = lt * TT
                xo = (lt % 2) * 8 * TT
                v_stats = vs.n

                @block.scalar
                def _(s, v_stats=v_stats):
                    s.wait_ge(vs.h, v_stats)
                    ins = s.activation(rstd[:], vneg[:], AF.Sqrt,
                                       bias=eps_sb[:, 0:1], scale=-1.0 / DM)
                    inc1(ins, ss)
                s_sqrt = ss.n

                for s_i in range(NSUB):
                    @block.vector
                    def _(v, s_i=s_i, s_sqrt=s_sqrt):
                        if s_i == 0:
                            v.wait_ge(ss.h, s_sqrt)
                            if xst_done[0]:
                                v.wait_ge(dhw.h, xst_done[0])
                            v.reciprocal(rstd[:], rstd[:])
                            v.tensor_copy(junk1[:], m_sb[:, 0:1])
                        xs = x_sub[:, s_i * DM:(s_i + 1) * DM]
                        ins = v.tensor_scalar(xn_st2[:, s_i * DM:(s_i + 1) * DM], xs,
                                              m_sb[:, s_i:s_i + 1],
                                              rstd[:, s_i:s_i + 1],
                                              ALU.subtract, ALU.mult)
                        if s_i == NSUB - 1:
                            x_used[0] = inc1(ins, vs)
                        else:
                            inc1(ins, vs)
                    v_n = vs.n

                    @block.sync
                    def _(sy, s_i=s_i, v_n=v_n, tok0=tok0):
                        sy.wait_ge(vs.h, v_n)
                        r0 = tok0 + s_i * 128
                        ins = sy.dma_start(
                            out=xn_sp[r0:r0 + 128, :],
                            in_=xn_st2[:, s_i * DM:(s_i + 1) * DM])
                        xst_done[0] = inc16(ins, dhw)
                xn_all_stored = dhw.n

                @block.sync
                def _(sy, tok0=tok0, lt=lt, xo=xo, xn_all_stored=xn_all_stored):
                    sy.wait_ge(dhw.h, xn_all_stored)
                    if xnfm_used_slot[lt % 2]:
                        sy.wait_ge(ts.h, xnfm_used_slot[lt % 2])
                    for dc in range(8):
                        ins = sy.dma_start(
                            out=xn_fm[:, xo + dc * TT:xo + (dc + 1) * TT],
                            in_=xn_sp[tok0:tok0 + TT, dc * 128:(dc + 1) * 128],
                            transpose=True)
                        inc16(ins, dtr)
                xnfm_done_slot[lt % 2] = dtr.n

            def emit_ln(lt):
                for s_i in range(NSUB):
                    emit_ln_stats(lt, s_i)
                emit_ln_tail(lt)

            emit_ln(0)
            for t_i in range(NT):
                tok0 = t_i * TT
                xo = (t_i % 2) * 8 * TT
                xnfm_done = xnfm_done_slot[t_i % 2]

                rot_free = [b01_free, b01_free]
                for which in range(2):
                    sec0 = which * DM
                    for rc in range(8):
                        rot = rc % 2
                        pslice = pPool[:, rot * 512:rot * 512 + 512]

                        @block.tensor
                        def _(t, which=which, rc=rc, rot=rot, pslice=pslice, sec0=sec0, xo=xo):
                            if which == 0 and rc == 0:
                                t.wait_ge(dtr.h, xnfm_done)
                                if t_i == 0:
                                    t.wait_ge(dtr.h, d_wqkv)
                                if qk_used_prev:
                                    t.wait_ge(ts.h, qk_used_prev)
                            fsem, ftgt = rot_free[rot]
                            if ftgt:
                                t.wait_ge(fsem.h, ftgt)
                            for dc in range(8):
                                lhs = wqkvT[:, dc * 3 * DM + sec0 + rc * 128:
                                            dc * 3 * DM + sec0 + rc * 128 + 128]
                                rhs = xn_fm[:, xo + dc * TT:xo + dc * TT + TT]
                                ins_last = t.matmul(pslice, lhs, rhs,
                                         start=(dc == 0), stop=(dc == 7),
                                         skip_group_check=True)
                            inc1(ins_last, ts)
                        t_proj = ts.n

                        @block.scalar
                        def _(s, which=which, rc=rc, pslice=pslice, t_proj=t_proj):
                            s.wait_ge(ts.h, t_proj)
                            if which == 0:
                                ins = s.activation(q_fm[:, rc * TT:(rc + 1) * TT],
                                                   pslice, AF.Identity, scale=QSCALE)
                            else:
                                ins = s.activation(k_fm[:, rc * TT:(rc + 1) * TT],
                                                   pslice, AF.Sign)
                            inc1(ins, ss)
                        rot_free[rot] = (ss, ss.n)

                        if which == 1:
                            k_ev = ss.n

                            @block.sync
                            def _(sy, rc=rc, tok0=tok0, k_ev=k_ev):
                                sy.wait_ge(ss.h, k_ev)
                                ins = sy.dma_start(
                                    out=k_sp[rc * 128:(rc + 1) * 128, tok0:tok0 + TT],
                                    in_=k_fm[:, rc * TT:(rc + 1) * TT])
                                inc16(ins, dhw)
                k_stored = dhw.n
                b01_free = rot_free[1] if rot_free[1][1] >= rot_free[0][1] else rot_free[0]

                for s_i in range(NSUB):
                    sl = s_i % 2
                    ci = tok0 // 128 + s_i

                    @block.sync
                    def _(sy, sl=sl, ci=ci):
                        sy.wait_ge(dhw.h, k_stored)
                        if kv_used[sl]:
                            sy.wait_ge(ts.h, kv_used[sl])
                        ktm_done[sl] = dma_tr(
                            sy, k_tm[:, sl * DM:(sl + 1) * DM],
                            k_sp[:, ci * 128:(ci + 1) * 128], dtr)

                    @block.gpsimd
                    def _(g, sl=sl, ci=ci):
                        if tinv_used[sl]:
                            g.wait_ge(vs.h, tinv_used[sl])
                        ins = g.iota(tinv[:, sl * 128:(sl + 1) * 128], [[1, 128]],
                                     base=ci * 128 + 1, channel_multiplier=0,
                                     allow_small_or_imprecise_dtypes=True)
                        inc1(ins, gs)
                    g_iota = gs.n

                    @block.vector
                    def _(v, sl=sl, g_iota=g_iota):
                        v.wait_ge(gs.h, g_iota)
                        ins = v.reciprocal(tinv[:, sl * 128:(sl + 1) * 128],
                                           tinv[:, sl * 128:(sl + 1) * 128])
                        tinv_done[sl] = inc1(ins, vs)

                    @block.tensor
                    def _(t, s_i=s_i, sl=sl, xo=xo):
                        fsem, ftgt = b23_free
                        if ftgt:
                            t.wait_ge(fsem.h, ftgt)
                        if kv_used[sl]:
                            t.wait_ge(ts.h, 0)
                        for hh in range(2):
                            po = pPool[:, 1024 + hh * 512:1024 + hh * 512 + 512]
                            for dc in range(8):
                                lhs = xn_fm[:, xo + dc * TT + s_i * 128:
                                            xo + dc * TT + s_i * 128 + 128]
                                rhs = wqkvT[:, dc * 3 * DM + 2 * DM + hh * 512:
                                            dc * 3 * DM + 2 * DM + hh * 512 + 512]
                                ins_last = t.matmul(po, lhs, rhs, start=(dc == 0), stop=(dc == 7),
                                         skip_group_check=True)
                        inc1(ins_last, ts)
                    t_vproj = ts.n

                    @block.scalar
                    def _(s, sl=sl, t_vproj=t_vproj):
                        s.wait_ge(ts.h, t_vproj)
                        if kv_used[sl]:
                            s.wait_ge(ts.h, kv_used[sl])
                        ins = s.activation(v_tm[:, sl * DM:(sl + 1) * DM],
                                           pPool[:, 1024:2048], AF.Sign)
                        v_evac[sl] = inc1(ins, ss)
                    v_ready = ss.n

                    sc_wave_v = 0
                    for w in range(2):
                        @block.tensor
                        def _(t, w=w, s_i=s_i, sc_wave_v=sc_wave_v):
                            if w == 0:
                                fsem, ftgt = b01_free
                                if ftgt:
                                    t.wait_ge(fsem.h, ftgt)
                                if scM_used_prev:
                                    t.wait_ge(ts.h, 0)
                            else:
                                t.wait_ge(vs.h, sc_wave_v)
                            for b in range(2):
                                t.matmul(pPool[:, b * 512:b * 512 + 512],
                                         zk[:, 0:128], zk[:, 0:512],
                                         start=True, stop=False,
                                         skip_group_check=True)
                            ro = 64 * w
                            for hh in range(8):
                                h = 2 * hh + w
                                rc = h // 2
                                lhs = k_fm[ro:ro + 64, rc * TT + s_i * 128:
                                           rc * TT + s_i * 128 + 128]
                                rhs = q_fm[ro:ro + 64, rc * TT + s_i * 128:
                                           rc * TT + s_i * 128 + 128]
                                ins_last = t.matmul(pPool[:, hh * 128:hh * 128 + 128], lhs, rhs,
                                         start=False, stop=(hh % 4 == 3),
                                         skip_group_check=True,
                                         tile_position=(ro, 0))
                            inc1(ins_last, ts)
                        t_sc = ts.n

                        @block.vector
                        def _(v, w=w, t_sc=t_sc):
                            v.wait_ge(ts.h, t_sc)
                            if scM_used_prev:
                                v.wait_ge(ts.h, scM_used_prev)
                            src = pPool[:, 0:1024].rearrange("p (h i) -> p h i", h=8)
                            msk = maskT[:].rearrange("p (o i) -> p o i", o=1
                                                     ).broadcast_to([128, 8, 128])
                            dst = scM[:].rearrange("p (h q) -> p h q", h=8)[
                                :, :, w * 128:(w + 1) * 128]
                            inc1(v.tensor_tensor(dst, src, msk, ALU.mult), vs)
                        sc_wave_v = vs.n
                    sc_done = vs.n
                    b01_free = (vs, sc_done)

                    @block.tensor
                    def _(t, sl=sl, s_i=s_i, sc_done=sc_done):
                        t.wait_ge(vs.h, sc_done)
                        t.wait_ge(ss.h, v_ready)
                        t.wait_ge(ss.h, s_scopy)
                        if ic_evac_prev:
                            t.wait_ge(vs.h, ic_evac_prev)
                        for b in range(2):
                            t.matmul(pPool[:, 1024 + b * 512:1024 + b * 512 + 512],
                                     zk[:, 0:128], zk[:, 0:512],
                                     start=True, stop=False,
                                     skip_group_check=True)
                        for p in range(8):
                            pic = pPool[:, 1024 + p * 128:1024 + p * 128 + 128]
                            for par in range(2):
                                ro = 64 * par
                                sM = scM[:, (2 * p + par) * 128:(2 * p + par) * 128 + 128]
                                t.matmul(pic[ro:ro + 64, :],
                                         v_tm[:, sl * DM + p * 128 + ro:
                                              sl * DM + p * 128 + ro + 64],
                                         sM, start=False, stop=False,
                                         skip_group_check=True, tile_position=(0, ro))
                            ins_last = t.matmul(pic[:, :],
                                     S_sb[:, p * 128:p * 128 + 128],
                                     q_fm[:, p * TT + s_i * 128:
                                          p * TT + s_i * 128 + 128],
                                     start=False, stop=(p % 4 == 3),
                                     skip_group_check=True)
                        inc1(ins_last, ts)
                    t_ic = ts.n
                    scM_used_prev = t_ic

                    @block.tensor
                    def _(t, sl=sl):
                        t.wait_ge(dtr.h, ktm_done[sl])
                        for p in range(8):
                            for par in range(2):
                                ro = 64 * par
                                ins_last = t.matmul(pS[ro:ro + 64, p * 64:p * 64 + 64],
                                         k_tm[:, sl * DM + p * 128 + ro:
                                              sl * DM + p * 128 + ro + 64],
                                         v_tm[:, sl * DM + p * 128 + ro:
                                              sl * DM + p * 128 + ro + 64],
                                         start=False, stop=False,
                                         skip_group_check=True, tile_position=(0, ro))
                        kv_used[sl] = inc1(ins_last, ts)
                    t_ckv = ts.n

                    @block.scalar
                    def _(s, t_ckv=t_ckv):
                        s.wait_ge(ts.h, t_ckv)
                        dste = S_sb[0:64, :].rearrange("p (q c) -> p q c", q=8)[
                            :, :, 0:64]
                        srce = pS[0:64, :].rearrange("p (q c) -> p q c", q=8)
                        s.activation(dste, srce, AF.Identity)
                        dsto = S_sb[64:128, :].rearrange("p (q c) -> p q c", q=8)[
                            :, :, 64:128]
                        srco = pS[64:128, :].rearrange("p (q c) -> p q c", q=8)
                        inc1(s.activation(dsto, srco, AF.Identity), ss)
                    s_scopy = ss.n

                    @block.vector
                    def _(v, s_i=s_i, sl=sl, t_ic=t_ic):
                        v.wait_ge(ts.h, t_ic)
                        v.wait_ge(vs.h, tinv_done[sl])
                        if attn_stored_prev:
                            v.wait_ge(dhw.h, attn_stored_prev)
                        src = pPool[:, 1024:2048].rearrange("p (q i) -> p q i", q=8)
                        tv = tinv[:, sl * 128:(sl + 1) * 128].rearrange(
                            "p (o i) -> p o i", o=1).broadcast_to([128, 8, 128])
                        dst = attn_buf[:].rearrange("p (q c) -> p q c", q=8)[
                            :, :, s_i * 128:s_i * 128 + 128]
                        ins = v.tensor_tensor(dst, src, tv, ALU.mult)
                        tinv_used[sl] = inc1(ins, vs)
                    ic_evac_prev = vs.n
                    b23_free = (vs, vs.n)

                    if t_i + 1 < NT:
                        emit_ln_stats(t_i + 1, s_i)

                @block.sync
                def _(sy, tok0=tok0, ic_evac_prev=ic_evac_prev):
                    sy.wait_ge(vs.h, ic_evac_prev)
                    for p in range(8):
                        ins = sy.dma_start(
                            out=attn_sp[p * 128:(p + 1) * 128, tok0:tok0 + TT],
                            in_=attn_buf[:, p * TT:(p + 1) * TT])
                        inc16(ins, dhw)
                attn_stored_prev = dhw.n

                if t_i + 1 < NT:
                    emit_ln_tail(t_i + 1)

                rot_free = [b01_free, b01_free]
                for s_i in range(NSUB):
                    sl = s_i % 2
                    r0 = tok0 + s_i * 128
                    for eg in range(4):
                        rot = eg % 2
                        pslice = pPool[:, rot * 512:rot * 512 + 512]

                        @block.tensor
                        def _(t, s_i=s_i, eg=eg, rot=rot, pslice=pslice, xo=xo,
                              t_i=t_i):
                            if t_i == 0 and s_i == 0 and eg == 0:
                                t.wait_ge(dtr.h, d_wT)
                            fsem, ftgt = rot_free[rot]
                            if ftgt:
                                t.wait_ge(fsem.h, ftgt)
                            for dc in range(8):
                                lhs = xn_fm[:, xo + dc * TT + s_i * 128:
                                            xo + dc * TT + s_i * 128 + 128]
                                rhs = phiT[:, dc * E + eg * 512:dc * E + eg * 512 + 512]
                                t.matmul(pslice, lhs, rhs, start=(dc == 0), stop=False,
                                         skip_group_check=True)
                            ins_last = t.matmul(pslice, ones1[0:1, 0:128],
                                     phib_sb[0:1, eg * 512:eg * 512 + 512],
                                     start=False, stop=True, skip_group_check=True)
                            inc1(ins_last, ts)
                        t_lg = ts.n

                        @block.scalar
                        def _(s, sl=sl, eg=eg, pslice=pslice, t_lg=t_lg):
                            s.wait_ge(ts.h, t_lg)
                            if eg == 0 and e1_free[sl]:
                                s.wait_ge(dhw.h, e1_free[sl])
                            ins = s.activation(
                                e1b[:, sl * E + eg * 512:sl * E + eg * 512 + 512],
                                pslice, AF.Exp)
                            inc1(ins, ss)
                        rot_free[rot] = (ss, ss.n)
                    e1_ready = ss.n

                    @block.vector
                    def _(v, sl=sl, e1_ready=e1_ready):
                        v.wait_ge(ss.h, e1_ready)
                        e3 = e1b[:, sl * E:(sl + 1) * E].rearrange(
                            "p (n q) -> p n q", n=64, q=32)
                        e3s = e1b[:, sl * E:(sl + 1) * E].rearrange(
                            "p (n q) -> p q n", n=64, q=32)
                        v.tensor_reduce(sC[:].rearrange("p (q o) -> p q o", o=1),
                                        e3s, axis=AX.X, op=ALU.add)
                        v.tensor_reduce(sD[:].rearrange("p (n o) -> p n o", o=1),
                                        e3, axis=AX.X, op=ALU.add)
                        v.reciprocal(rC[:], sC[:])
                        v.reciprocal(rD[:], sD[:])
                        if c_free[sl]:
                            v.wait_ge(dhw.h, c_free[sl])
                        e3 = e1b[:, sl * E:(sl + 1) * E].rearrange(
                            "p (n q) -> p n q", n=64, q=32)
                        co = c_ob[:, sl * E:(sl + 1) * E].rearrange(
                            "p (n q) -> p n q", n=64, q=32)
                        rc_b = rC[:].rearrange("p (o q) -> p o q", o=1
                                               ).broadcast_to([128, 64, 32])
                        v.tensor_tensor(co, e3, rc_b, ALU.mult)
                        rd_b = rD[:].rearrange("p (n o) -> p n o", o=1
                                               ).broadcast_to([128, 64, 32])
                        inc1(v.tensor_tensor(e3, e3, rd_b, ALU.mult), vs)
                    cd_done = vs.n

                    @block.sync
                    def _(sy, sl=sl, r0=r0, cd_done=cd_done):
                        sy.wait_ge(vs.h, cd_done)
                        ins = sy.dma_start(out=c_sp[r0:r0 + 128, :],
                                           in_=c_ob[:, sl * E:(sl + 1) * E])
                        c_free[sl] = inc16(ins, dhw)
                        ins = sy.dma_start(out=d_sp[r0:r0 + 128, :],
                                           in_=e1b[:, sl * E:(sl + 1) * E])
                        e1_free[sl] = inc16(ins, dhw)
                xnfm_used_slot[t_i % 2] = ts.n
                qk_used_prev = scM_used_prev
                b01_free = rot_free[0] if rot_free[0][1] >= rot_free[1][1] else rot_free[1]

            p1_dhw, p1_ts, p1_vs, p1_ss, p1_gs = dhw.n, ts.n, vs.n, ss.n, gs.n

        if DEBUG_STOP == "pass1":
            _debug_dump(nc, block, dhw, dsw, d_sp, out_h, T,
                        [(dhw, p1_dhw), (ts, p1_ts), (vs, p1_vs), (ss, p1_ss), (gs, p1_gs)])
            return nc

        with contextlib.ExitStack() as st2:
            xn_full = st2.enter_context(
                nc.sbuf_tensor("xn_full", [128, NSUBT * DM], BF16))
            d_col = st2.enter_context(
                nc.sbuf_tensor("d_col", [128, 2 * NSUBT * 128], BF16))
            x_stage = st2.enter_context(nc.sbuf_tensor("x_stage", [128, 2 * DM], F32))

            @block.sync
            def _(sy):
                sy.wait_ge(dhw.h, p1_dhw)
                sy.wait_ge(ts.h, p1_ts)
                sy.wait_ge(vs.h, p1_vs)
                sy.wait_ge(ss.h, p1_ss)
                sy.wait_ge(gs.h, p1_gs)
                nq = min(8, T // 128)
                rows = T // nq
                for i in range(nq):
                    src = xn_sp[i * rows:(i + 1) * rows, :].rearrange(
                        "(s p) d -> p s d", p=128)
                    dst = xn_full[:, i * (rows // 128) * DM:
                                  (i + 1) * (rows // 128) * DM].rearrange(
                        "p (s d) -> p s d", d=DM)
                    inc16(sy.dma_start(out=dst, in_=src), dhw)
            xnfull_done = dhw.n

            dcol_done = [0, 0]
            dcol_used = [0, 0]
            xev = [0, 0]
            xst_t = [0]
            for ec in range(16):
                sl = ec % 2

                @block.sync
                def _(sy, ec=ec, sl=sl):
                    if dcol_used[sl]:
                        sy.wait_ge(ts.h, dcol_used[sl])
                    nq = min(8, T // 128)
                    rows = T // nq
                    for i in range(nq):
                        src = d_sp[i * rows:(i + 1) * rows, ec * 128:(ec + 1) * 128
                                   ].rearrange("(s p) e -> p s e", p=128)
                        dst = d_col[:, sl * NSUBT * 128 + i * rows:
                                    sl * NSUBT * 128 + (i + 1) * rows
                                    ].rearrange("p (s e) -> p s e", e=128)
                        inc16(sy.dma_start(out=dst, in_=src), dhw)
                    dcol_done[sl] = dhw.n

                for dh_i in range(2):
                    pslice = pPool[:, dh_i * 512:dh_i * 512 + 512]

                    @block.tensor
                    def _(t, ec=ec, sl=sl, dh_i=dh_i, pslice=pslice):
                        if ec == 0 and dh_i == 0:
                            t.wait_ge(dhw.h, xnfull_done)
                        t.wait_ge(dhw.h, dcol_done[sl])
                        if xev[dh_i]:
                            t.wait_ge(vs.h, xev[dh_i])
                        for sub in range(NSUBT):
                            lhs = d_col[:, sl * NSUBT * 128 + sub * 128:
                                        sl * NSUBT * 128 + sub * 128 + 128]
                            rhs = xn_full[:, sub * DM + dh_i * 512:
                                          sub * DM + dh_i * 512 + 512]
                            ins_last = t.matmul(pslice, lhs, rhs, start=(sub == 0),
                                     stop=(sub == NSUBT - 1), skip_group_check=True)
                        if dh_i == 1:
                            dcol_used[sl] = ts.n + 1
                        inc1(ins_last, ts)
                    t_x = ts.n

                    @block.vector
                    def _(v, dh_i=dh_i, pslice=pslice, t_x=t_x):
                        v.wait_ge(ts.h, t_x)
                        if xst_t[0]:
                            v.wait_ge(dhw.h, xst_t[0])
                        ins = v.tensor_copy(x_stage[:, dh_i * 512:dh_i * 512 + 512],
                                            pslice)
                        xev[dh_i] = inc1(ins, vs)
                v_x = vs.n

                @block.sync
                def _(sy, ec=ec, v_x=v_x):
                    sy.wait_ge(vs.h, v_x)
                    ins = sy.dma_start(out=x_bounce[ec * 128:(ec + 1) * 128, :],
                                       in_=x_stage[:, 0:DM])
                    xst_t[0] = inc16(ins, dhw)
            xb_done = dhw.n

            if DEBUG_STOP == "xphase":
                _debug_dump(nc, block, dhw, dsw, d_sp, out_h, T,
                            [(dhw, xb_done), (ts, ts.n), (vs, vs.n)])
                return nc

            @block.gpsimd
            def _(g):
                g.wait_ge(dhw.h, xb_done)
                ins = g.collective_compute(
                    "ReduceScatter", ALU.add,
                    replica_groups=[list(range(NCORES))],
                    ins=[x_bounce.ap().opt()], outs=[x_rs.ap().opt()])
                inc1(ins, cc)
            rs_done = cc.n

            if DEBUG_STOP == "exchange":
                _debug_dump(nc, block, dhw, dsw, d_sp, out_h, T, [(cc, rs_done)])
                return nc

        with contextlib.ExitStack() as st3:
            xt_f = st3.enter_context(nc.sbuf_tensor("xt_f", [128, 2 * DM], F32))
            xt_b = st3.enter_context(nc.sbuf_tensor("xt_b", [128, 2 * DM], BF16))
            xtT = st3.enter_context(nc.sbuf_tensor("xtT", [128, 8 * ESL], BF16))
            ew_sb = st3.enter_context(nc.sbuf_tensor("ew_sb", [128, 2 * 8 * HID], BF16))
            eb_sb = st3.enter_context(nc.sbuf_tensor("eb_sb", [1, EPC * HID], BF16))
            yt_st = st3.enter_context(nc.sbuf_tensor("yt_st", [128, 2 * HID], BF16))
            sqtb = st3.enter_context(nc.sbuf_tensor("sqtb", [128, 1024], F32))
            s1b = st3.enter_context(nc.sbuf_tensor("s1b", [128, 1], F32))
            s2b = st3.enter_context(nc.sbuf_tensor("s2b", [128, 1], F32))
            m_sbb = st3.enter_context(nc.sbuf_tensor("m_sbb", [128, 1], F32))
            vnegb = st3.enter_context(nc.sbuf_tensor("vnegb", [128, 1], F32))
            junkb = st3.enter_context(nc.sbuf_tensor("junkb", [128, 1], F32))
            rstdb = st3.enter_context(nc.sbuf_tensor("rstdb", [128, 1], F32))

            ew_done = {}
            ew_used = [0, 0]

            @block.gpsimd
            def _(g):
                g.wait_ge(dhw.h, xb_done)
                inc16(g.dma_start(out=eb_sb[:, :], in_=eb_h[:, :]), dsw)
                for n in range(2):
                    src = ew_h[n, :, :].rearrange("(c p) h -> p c h", p=128)
                    dst = ew_sb[:, n * 8 * HID:(n + 1) * 8 * HID].rearrange(
                        "p (c h) -> p c h", h=HID)
                    ew_done[n] = inc16(g.dma_start(out=dst, in_=src), dsw)
            eb_done = dsw.n

            @block.sync
            def _(sy):
                sy.wait_ge(cc.h, rs_done)
                for i in range(2):
                    ins = sy.dma_start(out=xt_f[:, i * DM:(i + 1) * DM],
                                       in_=x_rs[i * 128:(i + 1) * 128, :])
                    inc16(ins, dhw)
            xrs_loaded = dhw.n

            for i in range(2):
                @block.vector
                def _(v, i=i):
                    v.wait_ge(dhw.h, xrs_loaded)
                    xs = xt_f[:, i * DM:(i + 1) * DM]
                    v.tensor_scalar(xs, xs, CLIP_LO, CLIP_HI, ALU.max, ALU.min)
                    v.tensor_copy(junkb[:], eps_sb[:])
                    v.tensor_mul(sqtb[:], xs, xs)
                    v.reduce_sum(s1b[:], xs, axis=AX.X)
                    v.reduce_sum(s2b[:], sqtb[:], axis=AX.X)
                    v.tensor_scalar_mul(m_sbb[:], s1b[:], 1.0 / DM)
                    v.tensor_copy(junkb[:], s1b[:])
                    ins = v.scalar_tensor_tensor(vnegb[:], s1b[:], m_sbb[:, 0:1],
                                                 s2b[:], ALU.mult, ALU.subtract)
                    inc1(ins, vs)
                v_st = vs.n

                @block.scalar
                def _(s, v_st=v_st):
                    s.wait_ge(vs.h, v_st)
                    ins = s.activation(rstdb[:], vnegb[:], AF.Sqrt,
                                       bias=eps_sb[:, 0:1], scale=-1.0 / DM)
                    inc1(ins, ss)
                s_sq = ss.n

                @block.vector
                def _(v, i=i, s_sq=s_sq):
                    v.wait_ge(ss.h, s_sq)
                    v.reciprocal(rstdb[:], rstdb[:])
                    v.tensor_copy(junkb[:], m_sbb[:])
                    xs = xt_f[:, i * DM:(i + 1) * DM]
                    ins = v.tensor_scalar(xs, xs, m_sbb[:, 0:1], rstdb[:, 0:1],
                                          ALU.subtract, ALU.mult)
                    inc1(ins, vs)
            v_xt = vs.n

            @block.scalar
            def _(s):
                s.wait_ge(vs.h, v_xt)
                inc1(s.activation(xt_b[:], xt_f[:], AF.Tanh), ss)
            s_tanh = ss.n

            xtT_prev = [0]
            for i in range(2):
                for dc in range(8):
                    @block.tensor
                    def _(t, i=i, dc=dc):
                        if i == 0 and dc == 0:
                            t.wait_ge(ss.h, s_tanh)
                        else:
                            t.wait_ge(ss.h, xtT_prev[0])
                        ins = t.transpose(
                            pT[:, 0:128],
                            xt_b[:, i * DM + dc * 128:i * DM + dc * 128 + 128],
                            ident[:])
                        inc1(ins, ts)
                    t_tr = ts.n

                    @block.scalar
                    def _(s, i=i, dc=dc, t_tr=t_tr):
                        s.wait_ge(ts.h, t_tr)
                        ins = s.activation(
                            xtT[:, dc * ESL + i * 128:dc * ESL + i * 128 + 128],
                            pT[:, 0:128], AF.Identity)
                        inc1(ins, ss)
                    xtT_prev[0] = ss.n
            s_xtT = ss.n

            yt_ev = [0]
            for n in range(EPC):
                sl = n % 2
                if n >= 2:
                    @block.gpsimd
                    def _(g, n=n, sl=sl):
                        g.wait_ge(ts.h, ew_used[sl])
                        src = ew_h[n, :, :].rearrange("(c p) h -> p c h", p=128)
                        dst = ew_sb[:, sl * 8 * HID:(sl + 1) * 8 * HID].rearrange(
                            "p (c h) -> p c h", h=HID)
                        ew_done[n] = inc16(g.dma_start(out=dst, in_=src), dsw)

                @block.tensor
                def _(t, n=n, sl=sl):
                    if n == 0:
                        t.wait_ge(ss.h, s_xtT)
                        t.wait_ge(dsw.h, eb_done)
                    t.wait_ge(dsw.h, ew_done[n])
                    if yt_ev[0]:
                        t.wait_ge(ss.h, yt_ev[0])
                    for hh in range(2):
                        po = pPool[0:32, hh * 512:hh * 512 + 512]
                        for dc in range(8):
                            lhs = xtT[:, dc * ESL + n * 32:dc * ESL + n * 32 + 32]
                            rhs = ew_sb[:, sl * 8 * HID + dc * HID + hh * 512:
                                        sl * 8 * HID + dc * HID + hh * 512 + 512]
                            t.matmul(po, lhs, rhs, start=(dc == 0), stop=False,
                                     skip_group_check=True)
                        ins_last = t.matmul(po, ones1[0:1, 0:32],
                                 eb_sb[0:1, n * HID + hh * 512:n * HID + hh * 512 + 512],
                                 start=False, stop=True, skip_group_check=True)
                    ew_used[sl] = inc1(ins_last, ts)
                t_exp = ts.n

                @block.scalar
                def _(s, n=n, t_exp=t_exp):
                    s.wait_ge(ts.h, t_exp)
                    half = n // 4
                    row = 32 * (n % 4)
                    ins = s.activation(
                        yt_st[row:row + 32, half * HID:(half + 1) * HID],
                        pPool[0:32, 0:1024], AF.Identity)
                    yt_ev[0] = inc1(ins, ss)
            s_yt = ss.n

            @block.sync
            def _(sy):
                sy.wait_ge(ss.h, s_yt)
                for half in range(2):
                    ins = sy.dma_start(
                        out=yt_slice[half * 128:(half + 1) * 128, :],
                        in_=yt_st[:, half * HID:(half + 1) * HID])
                    inc16(ins, dhw)
            yts_done = dhw.n

            @block.gpsimd
            def _(g):
                g.wait_ge(dhw.h, yts_done)
                ins = g.collective_compute(
                    "AllGather", ALU.bypass,
                    replica_groups=[list(range(NCORES))],
                    ins=[yt_slice.ap().opt()], outs=[yt_full.ap().opt()])
                inc1(ins, cc)
            ag_done = cc.n
            exp_ts, exp_ss, exp_vs = ts.n, ss.n, vs.n

            if DEBUG_STOP == "experts":
                _debug_dump(nc, block, dhw, dsw, d_sp, out_h, T,
                            [(cc, ag_done), (ts, exp_ts), (ss, exp_ss)])
                return nc

        with contextlib.ExitStack() as st4:
            woT = st4.enter_context(nc.sbuf_tensor("woT", [128, 8 * DM], BF16))
            yt_sb = st4.enter_context(nc.sbuf_tensor("yt_sb", [128, 16 * HID], BF16))
            attn_t = st4.enter_context(nc.sbuf_tensor("attn_t", [128, 2 * 8 * TT], BF16))
            c_t = st4.enter_context(nc.sbuf_tensor("c_t", [128, 2 * 16 * TT], BF16))
            x2 = st4.enter_context(nc.sbuf_tensor("x2", [128, 2 * NSUB * DM], F32))
            o_st = st4.enter_context(nc.sbuf_tensor("o_st", [128, 2 * DM], F32))

            @block.sync
            def _(sy):
                sy.wait_ge(ts.h, exp_ts)
                sy.wait_ge(ss.h, exp_ss)
                for rc in range(8):
                    dma_tr(sy, woT[:, rc * DM:(rc + 1) * DM],
                           wo_bs[:, rc * 128:(rc + 1) * 128], dtr)
            p2_w = dhw.n
            p2_w_tr = dtr.n

            tile_loaded = [0, 0]
            tile_loaded_tr = [0, 0]
            tile_loaded_sc = [0, 0]
            yt_loaded = [0]
            tile_used = [0, 0]
            ost_done = [0, 0]
            p2_ev = [0, 0]
            for t_i in range(NT):
                tok0 = t_i * TT
                tl = t_i % 2


                @block.sync
                def _(sy, tok0=tok0, tl=tl, t_i=t_i):
                    if t_i == 0:
                        sy.wait_ge(dhw.h, p2_w)
                        sy.wait_ge(dtr.h, p2_w_tr)
                        sy.wait_ge(vs.h, exp_vs)
                    if tile_used[tl]:
                        sy.wait_ge(ts.h, tile_used[tl])
                    for rc in range(8):
                        ins = sy.dma_start(
                            out=attn_t[:, tl * 8 * TT + rc * TT:
                                       tl * 8 * TT + (rc + 1) * TT],
                            in_=attn_sp[rc * 128:(rc + 1) * 128, tok0:tok0 + TT])
                        inc16(ins, dhw)
                    for npc in range(16):
                        ins = sy.dma_start(
                            out=c_t[:, tl * 16 * TT + npc * TT:
                                    tl * 16 * TT + (npc + 1) * TT],
                            in_=c_sp[tok0:tok0 + TT, npc * 128:(npc + 1) * 128],
                            transpose=True)
                        inc16(ins, dtr)
                    src = x_h[tok0:tok0 + TT, :].rearrange("(s p) d -> p s d", p=128)
                    dst = x2[:, tl * NSUB * DM:(tl + 1) * NSUB * DM].rearrange(
                        "p (s d) -> p s d", d=DM)
                    tile_loaded[tl] = inc16(sy.dma_start(out=dst, in_=src), dhw)
                    tile_loaded_tr[tl] = dtr.n
                    if t_i == 0:
                        sy.wait_ge(cc.h, ag_done)
                        srcy = yt_full[:, :].rearrange("(c p) h -> p c h", p=128)
                        dsty = yt_sb[:].rearrange("p (c h) -> p c h", h=HID)
                        yt_loaded[0] = inc16(sy.dma_start(out=dsty, in_=srcy), dhw)

                for s_i in range(NSUB):
                    rot = s_i % 2
                    pslice = pPool[:, rot * 1024:rot * 1024 + 1024]

                    @block.tensor
                    def _(t, s_i=s_i, tl=tl, rot=rot, pslice=pslice):
                        t.wait_ge(dhw.h, tile_loaded[tl])
                        t.wait_ge(dtr.h, tile_loaded_tr[tl])
                        if p2_ev[rot]:
                            t.wait_ge(vs.h, p2_ev[rot])
                        for mh in range(2):
                            po = pslice[:, mh * 512:mh * 512 + 512]
                            for rc in range(8):
                                lhs = attn_t[:, tl * 8 * TT + rc * TT + s_i * 128:
                                             tl * 8 * TT + rc * TT + s_i * 128 + 128]
                                rhs = woT[:, rc * DM + mh * 512:rc * DM + mh * 512 + 512]
                                t.matmul(po, lhs, rhs, start=(rc == 0), stop=False,
                                         skip_group_check=True)
                            if t_i == 0 and s_i == 0 and mh == 0:
                                t.wait_ge(dhw.h, yt_loaded[0])
                            for npc in range(16):
                                lhs = c_t[:, tl * 16 * TT + npc * TT + s_i * 128:
                                          tl * 16 * TT + npc * TT + s_i * 128 + 128]
                                rhs = yt_sb[:, npc * HID + mh * 512:
                                            npc * HID + mh * 512 + 512]
                                ins_last = t.matmul(po, lhs, rhs, start=False, stop=(npc == 15),
                                         skip_group_check=True)
                        tgt = inc1(ins_last, ts)
                        if s_i == NSUB - 1:
                            tile_used[tl] = tgt
                    t_p2 = ts.n

                    @block.vector
                    def _(v, s_i=s_i, tl=tl, rot=rot, pslice=pslice, t_p2=t_p2):
                        v.wait_ge(ts.h, t_p2)
                        if ost_done[rot]:
                            v.wait_ge(dsc.h, ost_done[rot])
                        ins = v.tensor_tensor(
                            o_st[:, rot * DM:(rot + 1) * DM],
                            x2[:, tl * NSUB * DM + s_i * DM:
                               tl * NSUB * DM + (s_i + 1) * DM],
                            pslice, ALU.add)
                        p2_ev[rot] = inc1(ins, vs)
                    v_p2 = vs.n

                    @block.scalar
                    def _(s, tok0=tok0, s_i=s_i, rot=rot, v_p2=v_p2):
                        s.wait_ge(vs.h, v_p2)
                        r0 = tok0 + s_i * 128
                        ins = s.dma_start(out=out_h[r0:r0 + 128, :],
                                          in_=o_st[:, rot * DM:(rot + 1) * DM])
                        ost_done[rot] = inc16(ins, dsc)

            @block.sync
            def _(sy):
                sy.wait_ge(dhw.h, dhw.n)
                sy.wait_ge(dsc.h, dsc.n)
                sy.wait_ge(dtr.h, dtr.n)

        if DEBUG_SCRATCH:
            dbg = {
                "xn_d": (xn_sp, [T, DM], BF16),
                "k_d": (k_sp, [DM, T], BF16),
                "attn_d": (attn_sp, [DM, T], BF16),
                "c_d": (c_sp, [T, E], BF16),
                "dd_d": (d_sp, [T, E], BF16),
                "xb_d": (x_bounce, [E, DM], F32),
            }

            @block.gpsimd
            def _(g):
                g.wait_ge(dhw.h, dhw.n)
                for nm, (srct, shp, dt) in dbg.items():
                    oh = nc.dram_tensor(nm, shp, dt, kind="ExternalOutput")
                    inc16(g.dma_start(out=oh[:, :], in_=srct[:, :]), dsw)
                g.wait_ge(dsw.h, dsw.n)

    return nc


def kernel(L0_attn_x, L0_attn_W_qkv, L0_attn_W_o, L0_ffn_phi_w, L0_ffn_phi_b,
           L0_ffn_expert_w, L0_ffn_expert_b):
    x = np.asarray(L0_attn_x)
    B, T, D = x.shape
    nc = build_program(T=T)
    in_maps = []
    for c in range(NCORES):
        in_maps.append({
            "x": np.ascontiguousarray(x[c]),
            "wqkv": np.ascontiguousarray(L0_attn_W_qkv),
            "wo": np.ascontiguousarray(L0_attn_W_o),
            "phiw": np.ascontiguousarray(L0_ffn_phi_w),
            "phib": np.ascontiguousarray(np.asarray(L0_ffn_phi_b).reshape(1, E)),
            "ew": np.ascontiguousarray(L0_ffn_expert_w[c * EPC:(c + 1) * EPC]),
            "eb": np.ascontiguousarray(
                np.asarray(L0_ffn_expert_b)[c * EPC:(c + 1) * EPC].reshape(1, EPC * HID)),
        })
    res = run_bass_kernel_spmd(nc, in_maps, core_ids=list(range(NCORES)))
    return np.stack([r["out"] for r in res.results], axis=0)


# version 10
# speedup vs baseline: 1.0049x; 1.0049x over previous
import numpy as np
import contextlib
import concourse.bass as bass
import concourse.mybir as mybir
from concourse.bass_utils import run_bass_kernel_spmd

F32 = mybir.dt.float32
BF16 = mybir.dt.bfloat16
AF = mybir.ActivationFunctionType
ALU = mybir.AluOpType
AX = mybir.AxisListType

NCORES = 8
DM = 1024
H = 16
DH = 64
E = 2048
NEXP = 64
PSLOT = 32
HID = 1024
TT = 512
LN_EPS = 1e-5
QSCALE = 0.125
CLIP_LO = -33000.0
CLIP_HI = 65000.0
EPC = NEXP // NCORES
DEBUG_STOP = None
DEBUG_SCRATCH = False
ESL = E // NCORES


class Sem:
    __slots__ = ("h", "n")

    def __init__(self, h):
        self.h = h
        self.n = 0


def inc1(ins, sem):
    sem.n += 1
    ins.then_inc(sem.h, 1)
    return sem.n


def inc16(ins, sem):
    sem.n += 16
    ins.then_inc(sem.h, 16)
    return sem.n


def dma_tr(sy, out_ap, in_ap, sem):
    F = out_ap.shape[1]
    tgt = sem.n
    for f0 in range(0, F, 512):
        f1 = min(f0 + 512, F)
        ins = sy.dma_start(out=out_ap[:, f0:f1], in_=in_ap[f0:f1, :], transpose=True)
        tgt = inc16(ins, sem)
    return tgt




def _debug_dump(nc, block, dhw, dsw, d_sp, out_h, T, waits):
    @block.gpsimd
    def _(g):
        for s, tgt in waits:
            g.wait_ge(s.h, tgt)
        g.dma_start(out=out_h[:, :], in_=d_sp[0:T, 0:DM]).then_inc(dsw.h, 16)
        dsw.n += 16
        g.wait_ge(dsw.h, dsw.n)

def build_program(T=8192):
    assert T % TT == 0
    NT = T // TT
    NSUB = TT // 128
    NSUBT = T // 128

    nc = bass.Bass("TRN2", target_bir_lowering=False, detect_race_conditions=False)

    x_h = nc.dram_tensor("x", [T, DM], F32, kind="ExternalInput")
    wqkv_h = nc.dram_tensor("wqkv", [3 * DM, DM], F32, kind="ExternalInput")
    wo_h = nc.dram_tensor("wo", [DM, DM], F32, kind="ExternalInput")
    phiw_h = nc.dram_tensor("phiw", [E, DM], F32, kind="ExternalInput")
    phib_h = nc.dram_tensor("phib", [1, E], F32, kind="ExternalInput")
    ew_h = nc.dram_tensor("ew", [EPC, DM, HID], F32, kind="ExternalInput")
    eb_h = nc.dram_tensor("eb", [1, EPC * HID], F32, kind="ExternalInput")
    out_h = nc.dram_tensor("out", [T, DM], F32, kind="ExternalOutput")

    wqkv_bs = nc.dram_tensor("wqkv_bs", [3 * DM, DM], BF16)
    phiw_bs = nc.dram_tensor("phiw_bs", [E, DM], BF16)
    wo_bs = nc.dram_tensor("wo_bs", [DM, DM], BF16)
    xn_sp = nc.dram_tensor("xn_sp", [T, DM], BF16)
    k_sp = nc.dram_tensor("k_sp", [DM, T], BF16)
    attn_sp = nc.dram_tensor("attn_sp", [DM, T], BF16)
    c_sp = nc.dram_tensor("c_sp", [T, E], BF16)
    d_sp = nc.dram_tensor("d_sp", [T, E], BF16)
    x_bounce = nc.dram_tensor("x_bounce", [E, DM], F32)
    x_rs = nc.dram_tensor("x_rs", [ESL, DM], F32)
    yt_slice = nc.dram_tensor("yt_slice", [ESL, HID], BF16)
    yt_full = nc.dram_tensor("yt_full", [E, HID], BF16)

    st = contextlib.ExitStack()
    with st:
        dhw = Sem(st.enter_context(nc.semaphore("dhw")))
        dtr = Sem(st.enter_context(nc.semaphore("dtr")))
        dsc = Sem(st.enter_context(nc.semaphore("dsc")))
        dsw = Sem(st.enter_context(nc.semaphore("dsw")))
        cc = Sem(st.enter_context(nc.semaphore("cc")))
        ts = Sem(st.enter_context(nc.semaphore("ts")))
        vs = Sem(st.enter_context(nc.semaphore("vs")))
        ss = Sem(st.enter_context(nc.semaphore("ss")))
        gs = Sem(st.enter_context(nc.semaphore("gs")))

        pS = st.enter_context(nc.psum_tensor("pS", [128, 512], F32))
        pPool = st.enter_context(nc.psum_tensor("pPool", [128, 2560], F32))
        pT = st.enter_context(nc.psum_tensor("pT", [128, 1024], BF16))

        S_sb = st.enter_context(nc.sbuf_tensor("S_sb", [128, 1024], BF16))
        maskT = st.enter_context(nc.sbuf_tensor("maskT", [128, 128], BF16))
        ident = st.enter_context(nc.sbuf_tensor("ident", [128, 128], BF16))
        ones1 = st.enter_context(nc.sbuf_tensor("ones1", [1, 128], BF16))
        zeros = st.enter_context(nc.sbuf_tensor("zerosb", [1, 512], BF16))
        zk = st.enter_context(nc.sbuf_tensor("zk", [128, 512], BF16))
        phib_sb = st.enter_context(nc.sbuf_tensor("phib_sb", [1, E], BF16))
        eps_sb = st.enter_context(nc.sbuf_tensor("eps_sb", [128, 1], F32))

        block = st.enter_context(nc.Block())

        b01_free = (ss, 0)
        b23_free = (vs, 0)

        @block.gpsimd
        def _(g):
            g.memset(maskT[:], 1.0)
            g.memset(ident[:], 0.0)
            g.memset(ones1[:], 1.0)
            g.memset(zeros[:], 0.0)
            g.memset(zk[:], 0.0)
            g.memset(S_sb[:], 0.0)
            g.memset(eps_sb[:], LN_EPS)
            g.affine_select(out=maskT[:], in_=maskT[:], compare_op=ALU.is_ge,
                            fill=0.0, base=0, pattern=[[1, 128]], channel_multiplier=-1)
            ins_last = g.affine_select(out=ident[:], in_=ident[:], compare_op=ALU.not_equal,
                            fill=1.0, base=0, pattern=[[-1, 128]], channel_multiplier=1)
            g_consts = inc1(ins_last, gs)
            g.dma_start(out=wqkv_bs[:, :], in_=wqkv_h[:, :]).then_inc(dsw.h, 16)
            g.dma_start(out=phiw_bs[:, :], in_=phiw_h[:, :]).then_inc(dsw.h, 16)
            g.dma_start(out=wo_bs[:, :], in_=wo_h[:, :]).then_inc(dsw.h, 16)
            g.dma_start(out=phib_sb[:, :], in_=phib_h[:, :]).then_inc(dsw.h, 16)
            dsw.n += 64
        g_consts = gs.n
        d_wcast = dsw.n
        d_wcast1 = d_wcast - 48

        @block.tensor
        def _(t):
            t.wait_ge(gs.h, g_consts)
            t.matmul(pS[:, :], zk[:, 0:128], zk[:, 0:512],
                     start=True, stop=False, skip_group_check=True)

        with contextlib.ExitStack() as st1:
            wqkvT = st1.enter_context(nc.sbuf_tensor("wqkvT", [128, 8 * 3 * DM], BF16))
            phiT = st1.enter_context(nc.sbuf_tensor("phiT", [128, 8 * E], BF16))
            attn_buf = st1.enter_context(nc.sbuf_tensor("attn_buf", [128, 8 * TT], BF16))
            x_sub = st1.enter_context(nc.sbuf_tensor("x_sub", [128, 4 * DM], F32))

            d_parts = [0, 0, 0]

            @block.sync
            def _(sy):
                sy.wait_ge(dsw.h, d_wcast1)
                for sec in range(3):
                    for dc in range(8):
                        for f0 in (sec * DM, sec * DM + 512):
                            ins = sy.dma_start(
                                out=wqkvT[:, dc * 3 * DM + f0:
                                          dc * 3 * DM + f0 + 512],
                                in_=wqkv_bs[f0:f0 + 512,
                                            dc * 128:(dc + 1) * 128],
                                transpose=True)
                            inc16(ins, dtr)
                    d_parts[sec] = dtr.n
                sy.wait_ge(dsw.h, d_wcast)
                for dc in range(8):
                    dma_tr(sy, phiT[:, dc * E:(dc + 1) * E],
                           phiw_bs[:, dc * 128:(dc + 1) * 128], dtr)
            d_wT = dtr.n
            d_wq, d_wk, d_wv = d_parts
            xn_st2 = st1.enter_context(nc.sbuf_tensor("xn_st2", [128, 4 * DM], BF16))
            sqt = st1.enter_context(nc.sbuf_tensor("sqt", [128, 1024], F32))
            s1 = st1.enter_context(nc.sbuf_tensor("s1", [128, 4], F32))
            s2 = st1.enter_context(nc.sbuf_tensor("s2", [128, 4], F32))
            m_sb = st1.enter_context(nc.sbuf_tensor("m_sb", [128, 4], F32))
            vneg = st1.enter_context(nc.sbuf_tensor("vneg", [128, 4], F32))
            junk1 = st1.enter_context(nc.sbuf_tensor("junk1", [128, 1], F32))
            rstd = st1.enter_context(nc.sbuf_tensor("rstd", [128, 4], F32))
            xn_fm = st1.enter_context(nc.sbuf_tensor("xn_fm", [128, 2 * 8 * TT], BF16))
            q_fm = st1.enter_context(nc.sbuf_tensor("q_fm", [128, 8 * TT], BF16))
            k_fm = st1.enter_context(nc.sbuf_tensor("k_fm", [128, 8 * TT], BF16))
            k_tm = st1.enter_context(nc.sbuf_tensor("k_tm", [128, 2 * DM], BF16))
            v_tm = st1.enter_context(nc.sbuf_tensor("v_tm", [128, 2 * DM], BF16))
            scM = st1.enter_context(nc.sbuf_tensor("scM", [128, 16 * 128], BF16))
            tinv = st1.enter_context(nc.sbuf_tensor("tinv", [128, 2 * 128], F32))
            e1b = st1.enter_context(nc.sbuf_tensor("e1b", [128, 2 * E], BF16))
            c_ob = st1.enter_context(nc.sbuf_tensor("c_ob", [128, 2 * E], BF16))
            sD = st1.enter_context(nc.sbuf_tensor("sD", [128, 64], F32))
            sC = st1.enter_context(nc.sbuf_tensor("sC", [128, 32], F32))
            rD = st1.enter_context(nc.sbuf_tensor("rD", [128, 64], F32))
            rC = st1.enter_context(nc.sbuf_tensor("rC", [128, 32], F32))

            x_done = [0, 0, 0, 0]
            x_used = [0]
            xst_done = [0]
            e1_free = [0, 0]
            c_free = [0, 0]
            kv_used = [0, 0]
            v_evac = [0, 0]
            ktm_done = [0, 0]
            tinv_done = [0, 0]
            tinv_used = [0, 0]
            ic_evac_prev = 0
            scM_used_prev = 0
            attn_stored_prev = 0
            xnfm_used_slot = [0, 0]
            xnfm_done_slot = [0, 0]
            qk_used_prev = 0
            s_scopy = 0

            def emit_ln_stats(lt, s_i):
                tok0 = lt * TT
                r0 = tok0 + s_i * 128

                @block.sync
                def _(sy, s_i=s_i, r0=r0):
                    if x_used[0]:
                        sy.wait_ge(vs.h, x_used[0])
                    ins = sy.dma_start(out=x_sub[:, s_i * DM:(s_i + 1) * DM],
                                       in_=x_h[r0:r0 + 128, :])
                    x_done[s_i] = inc16(ins, dhw)

                @block.vector
                def _(v, s_i=s_i):
                    v.wait_ge(dhw.h, x_done[s_i])
                    xs = x_sub[:, s_i * DM:(s_i + 1) * DM]
                    v.tensor_mul(sqt[:], xs, xs)
                    v.reduce_sum(s1[:, s_i:s_i + 1], xs, axis=AX.X)
                    v.reduce_sum(s2[:, s_i:s_i + 1], sqt[:], axis=AX.X)
                    v.tensor_scalar_mul(m_sb[:, s_i:s_i + 1], s1[:, s_i:s_i + 1],
                                        1.0 / DM)
                    v.tensor_copy(junk1[:], s1[:, 0:1])
                    ins = v.scalar_tensor_tensor(vneg[:, s_i:s_i + 1],
                                                 s1[:, s_i:s_i + 1],
                                                 m_sb[:, s_i:s_i + 1],
                                                 s2[:, s_i:s_i + 1],
                                                 ALU.mult, ALU.subtract)
                    inc1(ins, vs)

            def emit_ln_tail(lt):
                tok0 = lt * TT
                xo = (lt % 2) * 8 * TT
                v_stats = vs.n

                @block.scalar
                def _(s, v_stats=v_stats):
                    s.wait_ge(vs.h, v_stats)
                    ins = s.activation(rstd[:], vneg[:], AF.Sqrt,
                                       bias=eps_sb[:, 0:1], scale=-1.0 / DM)
                    inc1(ins, ss)
                s_sqrt = ss.n

                for s_i in range(NSUB):
                    @block.vector
                    def _(v, s_i=s_i, s_sqrt=s_sqrt):
                        if s_i == 0:
                            v.wait_ge(ss.h, s_sqrt)
                            if xst_done[0]:
                                v.wait_ge(dhw.h, xst_done[0])
                            v.reciprocal(rstd[:], rstd[:])
                            v.tensor_copy(junk1[:], m_sb[:, 0:1])
                        xs = x_sub[:, s_i * DM:(s_i + 1) * DM]
                        ins = v.tensor_scalar(xn_st2[:, s_i * DM:(s_i + 1) * DM], xs,
                                              m_sb[:, s_i:s_i + 1],
                                              rstd[:, s_i:s_i + 1],
                                              ALU.subtract, ALU.mult)
                        if s_i == NSUB - 1:
                            x_used[0] = inc1(ins, vs)
                        else:
                            inc1(ins, vs)
                    v_n = vs.n

                    @block.sync
                    def _(sy, s_i=s_i, v_n=v_n, tok0=tok0):
                        sy.wait_ge(vs.h, v_n)
                        r0 = tok0 + s_i * 128
                        ins = sy.dma_start(
                            out=xn_sp[r0:r0 + 128, :],
                            in_=xn_st2[:, s_i * DM:(s_i + 1) * DM])
                        xst_done[0] = inc16(ins, dhw)
                xn_all_stored = dhw.n

                @block.sync
                def _(sy, tok0=tok0, lt=lt, xo=xo, xn_all_stored=xn_all_stored):
                    sy.wait_ge(dhw.h, xn_all_stored)
                    if xnfm_used_slot[lt % 2]:
                        sy.wait_ge(ts.h, xnfm_used_slot[lt % 2])
                    for dc in range(8):
                        ins = sy.dma_start(
                            out=xn_fm[:, xo + dc * TT:xo + (dc + 1) * TT],
                            in_=xn_sp[tok0:tok0 + TT, dc * 128:(dc + 1) * 128],
                            transpose=True)
                        inc16(ins, dtr)
                xnfm_done_slot[lt % 2] = dtr.n

            def emit_ln(lt):
                for s_i in range(NSUB):
                    emit_ln_stats(lt, s_i)
                emit_ln_tail(lt)

            emit_ln(0)
            for t_i in range(NT):
                tok0 = t_i * TT
                xo = (t_i % 2) * 8 * TT
                xnfm_done = xnfm_done_slot[t_i % 2]

                rot_free = [b01_free, b01_free]
                for which in range(2):
                    sec0 = which * DM
                    for rc in range(8):
                        rot = rc % 2
                        pslice = pPool[:, rot * 512:rot * 512 + 512]

                        @block.tensor
                        def _(t, which=which, rc=rc, rot=rot, pslice=pslice, sec0=sec0, xo=xo):
                            if which == 0 and rc == 0:
                                t.wait_ge(dtr.h, xnfm_done)
                                if t_i == 0:
                                    t.wait_ge(dtr.h, d_wq)
                            if which == 1 and rc == 0 and t_i == 0:
                                t.wait_ge(dtr.h, d_wk)
                                if qk_used_prev:
                                    t.wait_ge(ts.h, qk_used_prev)
                            fsem, ftgt = rot_free[rot]
                            if ftgt:
                                t.wait_ge(fsem.h, ftgt)
                            for dc in range(8):
                                lhs = wqkvT[:, dc * 3 * DM + sec0 + rc * 128:
                                            dc * 3 * DM + sec0 + rc * 128 + 128]
                                rhs = xn_fm[:, xo + dc * TT:xo + dc * TT + TT]
                                ins_last = t.matmul(pslice, lhs, rhs,
                                         start=(dc == 0), stop=(dc == 7),
                                         skip_group_check=True)
                            inc1(ins_last, ts)
                        t_proj = ts.n

                        @block.scalar
                        def _(s, which=which, rc=rc, pslice=pslice, t_proj=t_proj):
                            s.wait_ge(ts.h, t_proj)
                            if which == 0:
                                ins = s.activation(q_fm[:, rc * TT:(rc + 1) * TT],
                                                   pslice, AF.Identity, scale=QSCALE)
                            else:
                                ins = s.activation(k_fm[:, rc * TT:(rc + 1) * TT],
                                                   pslice, AF.Sign)
                            inc1(ins, ss)
                        rot_free[rot] = (ss, ss.n)

                        if which == 1:
                            k_ev = ss.n

                            @block.sync
                            def _(sy, rc=rc, tok0=tok0, k_ev=k_ev):
                                sy.wait_ge(ss.h, k_ev)
                                ins = sy.dma_start(
                                    out=k_sp[rc * 128:(rc + 1) * 128, tok0:tok0 + TT],
                                    in_=k_fm[:, rc * TT:(rc + 1) * TT])
                                inc16(ins, dhw)
                k_stored = dhw.n
                b01_free = rot_free[1] if rot_free[1][1] >= rot_free[0][1] else rot_free[0]

                for s_i in range(NSUB):
                    sl = s_i % 2
                    ci = tok0 // 128 + s_i

                    @block.sync
                    def _(sy, sl=sl, ci=ci):
                        sy.wait_ge(dhw.h, k_stored)
                        if kv_used[sl]:
                            sy.wait_ge(ts.h, kv_used[sl])
                        ktm_done[sl] = dma_tr(
                            sy, k_tm[:, sl * DM:(sl + 1) * DM],
                            k_sp[:, ci * 128:(ci + 1) * 128], dtr)

                    @block.gpsimd
                    def _(g, sl=sl, ci=ci):
                        if tinv_used[sl]:
                            g.wait_ge(vs.h, tinv_used[sl])
                        ins = g.iota(tinv[:, sl * 128:(sl + 1) * 128], [[1, 128]],
                                     base=ci * 128 + 1, channel_multiplier=0,
                                     allow_small_or_imprecise_dtypes=True)
                        inc1(ins, gs)
                    g_iota = gs.n

                    @block.vector
                    def _(v, sl=sl, g_iota=g_iota):
                        v.wait_ge(gs.h, g_iota)
                        ins = v.reciprocal(tinv[:, sl * 128:(sl + 1) * 128],
                                           tinv[:, sl * 128:(sl + 1) * 128])
                        tinv_done[sl] = inc1(ins, vs)

                    @block.tensor
                    def _(t, s_i=s_i, sl=sl, xo=xo, t_i=t_i):
                        if t_i == 0 and s_i == 0:
                            t.wait_ge(dtr.h, d_wv)
                        fsem, ftgt = b23_free
                        if ftgt:
                            t.wait_ge(fsem.h, ftgt)
                        if kv_used[sl]:
                            t.wait_ge(ts.h, 0)
                        for hh in range(2):
                            po = pPool[:, 1024 + hh * 512:1024 + hh * 512 + 512]
                            for dc in range(8):
                                lhs = xn_fm[:, xo + dc * TT + s_i * 128:
                                            xo + dc * TT + s_i * 128 + 128]
                                rhs = wqkvT[:, dc * 3 * DM + 2 * DM + hh * 512:
                                            dc * 3 * DM + 2 * DM + hh * 512 + 512]
                                ins_last = t.matmul(po, lhs, rhs, start=(dc == 0), stop=(dc == 7),
                                         skip_group_check=True)
                        inc1(ins_last, ts)
                    t_vproj = ts.n

                    @block.scalar
                    def _(s, sl=sl, t_vproj=t_vproj):
                        s.wait_ge(ts.h, t_vproj)
                        if kv_used[sl]:
                            s.wait_ge(ts.h, kv_used[sl])
                        ins = s.activation(v_tm[:, sl * DM:(sl + 1) * DM],
                                           pPool[:, 1024:2048], AF.Sign)
                        v_evac[sl] = inc1(ins, ss)
                    v_ready = ss.n

                    sc_wave_v = 0
                    for w in range(2):
                        @block.tensor
                        def _(t, w=w, s_i=s_i, sc_wave_v=sc_wave_v):
                            if w == 0:
                                fsem, ftgt = b01_free
                                if ftgt:
                                    t.wait_ge(fsem.h, ftgt)
                                if scM_used_prev:
                                    t.wait_ge(ts.h, 0)
                            else:
                                t.wait_ge(vs.h, sc_wave_v)
                            for b in range(2):
                                t.matmul(pPool[:, b * 512:b * 512 + 512],
                                         zk[:, 0:128], zk[:, 0:512],
                                         start=True, stop=False,
                                         skip_group_check=True)
                            ro = 64 * w
                            for hh in range(8):
                                h = 2 * hh + w
                                rc = h // 2
                                lhs = k_fm[ro:ro + 64, rc * TT + s_i * 128:
                                           rc * TT + s_i * 128 + 128]
                                rhs = q_fm[ro:ro + 64, rc * TT + s_i * 128:
                                           rc * TT + s_i * 128 + 128]
                                ins_last = t.matmul(pPool[:, hh * 128:hh * 128 + 128], lhs, rhs,
                                         start=False, stop=(hh % 4 == 3),
                                         skip_group_check=True,
                                         tile_position=(ro, 0))
                            inc1(ins_last, ts)
                        t_sc = ts.n

                        @block.vector
                        def _(v, w=w, t_sc=t_sc):
                            v.wait_ge(ts.h, t_sc)
                            if scM_used_prev:
                                v.wait_ge(ts.h, scM_used_prev)
                            src = pPool[:, 0:1024].rearrange("p (h i) -> p h i", h=8)
                            msk = maskT[:].rearrange("p (o i) -> p o i", o=1
                                                     ).broadcast_to([128, 8, 128])
                            dst = scM[:].rearrange("p (h q) -> p h q", h=8)[
                                :, :, w * 128:(w + 1) * 128]
                            inc1(v.tensor_tensor(dst, src, msk, ALU.mult), vs)
                        sc_wave_v = vs.n
                    sc_done = vs.n
                    b01_free = (vs, sc_done)

                    @block.tensor
                    def _(t, sl=sl, s_i=s_i, sc_done=sc_done):
                        t.wait_ge(vs.h, sc_done)
                        t.wait_ge(ss.h, v_ready)
                        t.wait_ge(ss.h, s_scopy)
                        if ic_evac_prev:
                            t.wait_ge(vs.h, ic_evac_prev)
                        for b in range(2):
                            t.matmul(pPool[:, 1024 + b * 512:1024 + b * 512 + 512],
                                     zk[:, 0:128], zk[:, 0:512],
                                     start=True, stop=False,
                                     skip_group_check=True)
                        for p in range(8):
                            pic = pPool[:, 1024 + p * 128:1024 + p * 128 + 128]
                            for par in range(2):
                                ro = 64 * par
                                sM = scM[:, (2 * p + par) * 128:(2 * p + par) * 128 + 128]
                                t.matmul(pic[ro:ro + 64, :],
                                         v_tm[:, sl * DM + p * 128 + ro:
                                              sl * DM + p * 128 + ro + 64],
                                         sM, start=False, stop=False,
                                         skip_group_check=True, tile_position=(0, ro))
                            ins_last = t.matmul(pic[:, :],
                                     S_sb[:, p * 128:p * 128 + 128],
                                     q_fm[:, p * TT + s_i * 128:
                                          p * TT + s_i * 128 + 128],
                                     start=False, stop=(p % 4 == 3),
                                     skip_group_check=True)
                        inc1(ins_last, ts)
                    t_ic = ts.n
                    scM_used_prev = t_ic

                    @block.tensor
                    def _(t, sl=sl):
                        t.wait_ge(dtr.h, ktm_done[sl])
                        for p in range(8):
                            for par in range(2):
                                ro = 64 * par
                                ins_last = t.matmul(pS[ro:ro + 64, p * 64:p * 64 + 64],
                                         k_tm[:, sl * DM + p * 128 + ro:
                                              sl * DM + p * 128 + ro + 64],
                                         v_tm[:, sl * DM + p * 128 + ro:
                                              sl * DM + p * 128 + ro + 64],
                                         start=False, stop=False,
                                         skip_group_check=True, tile_position=(0, ro))
                        kv_used[sl] = inc1(ins_last, ts)
                    t_ckv = ts.n

                    @block.scalar
                    def _(s, t_ckv=t_ckv):
                        s.wait_ge(ts.h, t_ckv)
                        dste = S_sb[0:64, :].rearrange("p (q c) -> p q c", q=8)[
                            :, :, 0:64]
                        srce = pS[0:64, :].rearrange("p (q c) -> p q c", q=8)
                        s.activation(dste, srce, AF.Identity)
                        dsto = S_sb[64:128, :].rearrange("p (q c) -> p q c", q=8)[
                            :, :, 64:128]
                        srco = pS[64:128, :].rearrange("p (q c) -> p q c", q=8)
                        inc1(s.activation(dsto, srco, AF.Identity), ss)
                    s_scopy = ss.n

                    @block.vector
                    def _(v, s_i=s_i, sl=sl, t_ic=t_ic):
                        v.wait_ge(ts.h, t_ic)
                        v.wait_ge(vs.h, tinv_done[sl])
                        if attn_stored_prev:
                            v.wait_ge(dhw.h, attn_stored_prev)
                        src = pPool[:, 1024:2048].rearrange("p (q i) -> p q i", q=8)
                        tv = tinv[:, sl * 128:(sl + 1) * 128].rearrange(
                            "p (o i) -> p o i", o=1).broadcast_to([128, 8, 128])
                        dst = attn_buf[:].rearrange("p (q c) -> p q c", q=8)[
                            :, :, s_i * 128:s_i * 128 + 128]
                        ins = v.tensor_tensor(dst, src, tv, ALU.mult)
                        tinv_used[sl] = inc1(ins, vs)
                    ic_evac_prev = vs.n
                    b23_free = (vs, vs.n)

                    if t_i + 1 < NT:
                        emit_ln_stats(t_i + 1, s_i)

                @block.sync
                def _(sy, tok0=tok0, ic_evac_prev=ic_evac_prev):
                    sy.wait_ge(vs.h, ic_evac_prev)
                    for p in range(8):
                        ins = sy.dma_start(
                            out=attn_sp[p * 128:(p + 1) * 128, tok0:tok0 + TT],
                            in_=attn_buf[:, p * TT:(p + 1) * TT])
                        inc16(ins, dhw)
                attn_stored_prev = dhw.n

                if t_i + 1 < NT:
                    emit_ln_tail(t_i + 1)

                rot_free = [b01_free, b01_free]
                for s_i in range(NSUB):
                    sl = s_i % 2
                    r0 = tok0 + s_i * 128
                    for eg in range(4):
                        rot = eg % 2
                        pslice = pPool[:, rot * 512:rot * 512 + 512]

                        @block.tensor
                        def _(t, s_i=s_i, eg=eg, rot=rot, pslice=pslice, xo=xo,
                              t_i=t_i):
                            if t_i == 0 and s_i == 0 and eg == 0:
                                t.wait_ge(dtr.h, d_wT)
                            fsem, ftgt = rot_free[rot]
                            if ftgt:
                                t.wait_ge(fsem.h, ftgt)
                            for dc in range(8):
                                lhs = xn_fm[:, xo + dc * TT + s_i * 128:
                                            xo + dc * TT + s_i * 128 + 128]
                                rhs = phiT[:, dc * E + eg * 512:dc * E + eg * 512 + 512]
                                t.matmul(pslice, lhs, rhs, start=(dc == 0), stop=False,
                                         skip_group_check=True)
                            ins_last = t.matmul(pslice, ones1[0:1, 0:128],
                                     phib_sb[0:1, eg * 512:eg * 512 + 512],
                                     start=False, stop=True, skip_group_check=True)
                            inc1(ins_last, ts)
                        t_lg = ts.n

                        @block.scalar
                        def _(s, sl=sl, eg=eg, pslice=pslice, t_lg=t_lg):
                            s.wait_ge(ts.h, t_lg)
                            if eg == 0 and e1_free[sl]:
                                s.wait_ge(dhw.h, e1_free[sl])
                            ins = s.activation(
                                e1b[:, sl * E + eg * 512:sl * E + eg * 512 + 512],
                                pslice, AF.Exp)
                            inc1(ins, ss)
                        rot_free[rot] = (ss, ss.n)
                    e1_ready = ss.n

                    @block.vector
                    def _(v, sl=sl, e1_ready=e1_ready):
                        v.wait_ge(ss.h, e1_ready)
                        e3 = e1b[:, sl * E:(sl + 1) * E].rearrange(
                            "p (n q) -> p n q", n=64, q=32)
                        e3s = e1b[:, sl * E:(sl + 1) * E].rearrange(
                            "p (n q) -> p q n", n=64, q=32)
                        v.tensor_reduce(sC[:].rearrange("p (q o) -> p q o", o=1),
                                        e3s, axis=AX.X, op=ALU.add)
                        v.tensor_reduce(sD[:].rearrange("p (n o) -> p n o", o=1),
                                        e3, axis=AX.X, op=ALU.add)
                        v.reciprocal(rC[:], sC[:])
                        v.reciprocal(rD[:], sD[:])
                        if c_free[sl]:
                            v.wait_ge(dhw.h, c_free[sl])
                        e3 = e1b[:, sl * E:(sl + 1) * E].rearrange(
                            "p (n q) -> p n q", n=64, q=32)
                        co = c_ob[:, sl * E:(sl + 1) * E].rearrange(
                            "p (n q) -> p n q", n=64, q=32)
                        rc_b = rC[:].rearrange("p (o q) -> p o q", o=1
                                               ).broadcast_to([128, 64, 32])
                        v.tensor_tensor(co, e3, rc_b, ALU.mult)
                        rd_b = rD[:].rearrange("p (n o) -> p n o", o=1
                                               ).broadcast_to([128, 64, 32])
                        inc1(v.tensor_tensor(e3, e3, rd_b, ALU.mult), vs)
                    cd_done = vs.n

                    @block.sync
                    def _(sy, sl=sl, r0=r0, cd_done=cd_done):
                        sy.wait_ge(vs.h, cd_done)
                        ins = sy.dma_start(out=c_sp[r0:r0 + 128, :],
                                           in_=c_ob[:, sl * E:(sl + 1) * E])
                        c_free[sl] = inc16(ins, dhw)
                        ins = sy.dma_start(out=d_sp[r0:r0 + 128, :],
                                           in_=e1b[:, sl * E:(sl + 1) * E])
                        e1_free[sl] = inc16(ins, dhw)
                xnfm_used_slot[t_i % 2] = ts.n
                qk_used_prev = scM_used_prev
                b01_free = rot_free[0] if rot_free[0][1] >= rot_free[1][1] else rot_free[1]

            p1_dhw, p1_ts, p1_vs, p1_ss, p1_gs = dhw.n, ts.n, vs.n, ss.n, gs.n

        if DEBUG_STOP == "pass1":
            _debug_dump(nc, block, dhw, dsw, d_sp, out_h, T,
                        [(dhw, p1_dhw), (ts, p1_ts), (vs, p1_vs), (ss, p1_ss), (gs, p1_gs)])
            return nc

        with contextlib.ExitStack() as st2:
            xn_full = st2.enter_context(
                nc.sbuf_tensor("xn_full", [128, NSUBT * DM], BF16))
            d_col = st2.enter_context(
                nc.sbuf_tensor("d_col", [128, 3 * NSUBT * 128], BF16))
            x_stage = st2.enter_context(nc.sbuf_tensor("x_stage", [128, 2 * 2 * DM], F32))

            @block.sync
            def _(sy):
                sy.wait_ge(dhw.h, p1_dhw)
                sy.wait_ge(ts.h, p1_ts)
                sy.wait_ge(vs.h, p1_vs)
                sy.wait_ge(ss.h, p1_ss)
                sy.wait_ge(gs.h, p1_gs)
                nq = min(8, T // 128)
                rows = T // nq
                for i in range(nq):
                    src = xn_sp[i * rows:(i + 1) * rows, :].rearrange(
                        "(s p) d -> p s d", p=128)
                    dst = xn_full[:, i * (rows // 128) * DM:
                                  (i + 1) * (rows // 128) * DM].rearrange(
                        "p (s d) -> p s d", d=DM)
                    inc16(sy.dma_start(out=dst, in_=src), dhw)
            xnfull_done = dhw.n

            dcol_done = [0, 0, 0]
            dcol_used = [0, 0, 0]
            xev = [0, 0, 0, 0]
            xst_t = [0, 0]
            for ec in range(16):
                sl = ec % 3

                @block.sync
                def _(sy, ec=ec, sl=sl):
                    if dcol_used[sl]:
                        sy.wait_ge(ts.h, dcol_used[sl])
                    nq = min(8, T // 128)
                    rows = T // nq
                    for i in range(nq):
                        src = d_sp[i * rows:(i + 1) * rows, ec * 128:(ec + 1) * 128
                                   ].rearrange("(s p) e -> p s e", p=128)
                        dst = d_col[:, sl * NSUBT * 128 + i * rows:
                                    sl * NSUBT * 128 + (i + 1) * rows
                                    ].rearrange("p (s e) -> p s e", e=128)
                        inc16(sy.dma_start(out=dst, in_=src), dhw)
                    dcol_done[sl] = dhw.n

                for dh_i in range(2):
                    pslice = pPool[:, dh_i * 512:dh_i * 512 + 512]

                    @block.tensor
                    def _(t, ec=ec, sl=sl, dh_i=dh_i, pslice=pslice):
                        if ec == 0 and dh_i == 0:
                            t.wait_ge(dhw.h, xnfull_done)
                        t.wait_ge(dhw.h, dcol_done[sl])
                        if xev[(ec % 2) * 2 + dh_i]:
                            t.wait_ge(vs.h, xev[(ec % 2) * 2 + dh_i])
                        for sub in range(NSUBT):
                            lhs = d_col[:, sl * NSUBT * 128 + sub * 128:
                                        sl * NSUBT * 128 + sub * 128 + 128]
                            rhs = xn_full[:, sub * DM + dh_i * 512:
                                          sub * DM + dh_i * 512 + 512]
                            ins_last = t.matmul(pslice, lhs, rhs, start=(sub == 0),
                                     stop=(sub == NSUBT - 1), skip_group_check=True)
                        if dh_i == 1:
                            dcol_used[sl] = ts.n + 1
                        inc1(ins_last, ts)
                    t_x = ts.n

                    @block.vector
                    def _(v, ec=ec, dh_i=dh_i, pslice=pslice, t_x=t_x):
                        v.wait_ge(ts.h, t_x)
                        if xst_t[ec % 2]:
                            v.wait_ge(dhw.h, xst_t[ec % 2])
                        so_ = (ec % 2) * 2 * 512
                        ins = v.tensor_copy(
                            x_stage[:, so_ + dh_i * 512:so_ + dh_i * 512 + 512],
                            pslice)
                        xev[(ec % 2) * 2 + dh_i] = inc1(ins, vs)
                v_x = vs.n

                @block.sync
                def _(sy, ec=ec, v_x=v_x):
                    sy.wait_ge(vs.h, v_x)
                    so_ = (ec % 2) * DM
                    ins = sy.dma_start(out=x_bounce[ec * 128:(ec + 1) * 128, :],
                                       in_=x_stage[:, so_:so_ + DM])
                    xst_t[ec % 2] = inc16(ins, dhw)
            xb_done = dhw.n

            if DEBUG_STOP == "xphase":
                _debug_dump(nc, block, dhw, dsw, d_sp, out_h, T,
                            [(dhw, xb_done), (ts, ts.n), (vs, vs.n)])
                return nc

            @block.gpsimd
            def _(g):
                g.wait_ge(dhw.h, xb_done)
                ins = g.collective_compute(
                    "ReduceScatter", ALU.add,
                    replica_groups=[list(range(NCORES))],
                    ins=[x_bounce.ap().opt()], outs=[x_rs.ap().opt()])
                inc1(ins, cc)
            rs_done = cc.n

            if DEBUG_STOP == "exchange":
                _debug_dump(nc, block, dhw, dsw, d_sp, out_h, T, [(cc, rs_done)])
                return nc

        with contextlib.ExitStack() as st3:
            xt_f = st3.enter_context(nc.sbuf_tensor("xt_f", [128, 2 * DM], F32))
            xt_b = st3.enter_context(nc.sbuf_tensor("xt_b", [128, 2 * DM], BF16))
            xtT = st3.enter_context(nc.sbuf_tensor("xtT", [128, 8 * ESL], BF16))
            ew_sb = st3.enter_context(nc.sbuf_tensor("ew_sb", [128, 2 * 8 * HID], BF16))
            eb_sb = st3.enter_context(nc.sbuf_tensor("eb_sb", [1, EPC * HID], BF16))
            yt_st = st3.enter_context(nc.sbuf_tensor("yt_st", [128, 2 * HID], BF16))
            sqtb = st3.enter_context(nc.sbuf_tensor("sqtb", [128, 1024], F32))
            s1b = st3.enter_context(nc.sbuf_tensor("s1b", [128, 1], F32))
            s2b = st3.enter_context(nc.sbuf_tensor("s2b", [128, 1], F32))
            m_sbb = st3.enter_context(nc.sbuf_tensor("m_sbb", [128, 1], F32))
            vnegb = st3.enter_context(nc.sbuf_tensor("vnegb", [128, 1], F32))
            junkb = st3.enter_context(nc.sbuf_tensor("junkb", [128, 1], F32))
            rstdb = st3.enter_context(nc.sbuf_tensor("rstdb", [128, 1], F32))

            ew_done = {}
            ew_used = [0, 0]

            @block.gpsimd
            def _(g):
                g.wait_ge(dhw.h, xb_done)
                inc16(g.dma_start(out=eb_sb[:, :], in_=eb_h[:, :]), dsw)
                for n in range(2):
                    src = ew_h[n, :, :].rearrange("(c p) h -> p c h", p=128)
                    dst = ew_sb[:, n * 8 * HID:(n + 1) * 8 * HID].rearrange(
                        "p (c h) -> p c h", h=HID)
                    ew_done[n] = inc16(g.dma_start(out=dst, in_=src), dsw)
            eb_done = dsw.n

            @block.sync
            def _(sy):
                sy.wait_ge(cc.h, rs_done)
                for i in range(2):
                    ins = sy.dma_start(out=xt_f[:, i * DM:(i + 1) * DM],
                                       in_=x_rs[i * 128:(i + 1) * 128, :])
                    inc16(ins, dhw)
            xrs_loaded = dhw.n

            for i in range(2):
                @block.vector
                def _(v, i=i):
                    v.wait_ge(dhw.h, xrs_loaded)
                    xs = xt_f[:, i * DM:(i + 1) * DM]
                    v.tensor_scalar(xs, xs, CLIP_LO, CLIP_HI, ALU.max, ALU.min)
                    v.tensor_copy(junkb[:], eps_sb[:])
                    v.tensor_mul(sqtb[:], xs, xs)
                    v.reduce_sum(s1b[:], xs, axis=AX.X)
                    v.reduce_sum(s2b[:], sqtb[:], axis=AX.X)
                    v.tensor_scalar_mul(m_sbb[:], s1b[:], 1.0 / DM)
                    v.tensor_copy(junkb[:], s1b[:])
                    ins = v.scalar_tensor_tensor(vnegb[:], s1b[:], m_sbb[:, 0:1],
                                                 s2b[:], ALU.mult, ALU.subtract)
                    inc1(ins, vs)
                v_st = vs.n

                @block.scalar
                def _(s, v_st=v_st):
                    s.wait_ge(vs.h, v_st)
                    ins = s.activation(rstdb[:], vnegb[:], AF.Sqrt,
                                       bias=eps_sb[:, 0:1], scale=-1.0 / DM)
                    inc1(ins, ss)
                s_sq = ss.n

                @block.vector
                def _(v, i=i, s_sq=s_sq):
                    v.wait_ge(ss.h, s_sq)
                    v.reciprocal(rstdb[:], rstdb[:])
                    v.tensor_copy(junkb[:], m_sbb[:])
                    xs = xt_f[:, i * DM:(i + 1) * DM]
                    ins = v.tensor_scalar(xs, xs, m_sbb[:, 0:1], rstdb[:, 0:1],
                                          ALU.subtract, ALU.mult)
                    inc1(ins, vs)
            v_xt = vs.n

            @block.scalar
            def _(s):
                s.wait_ge(vs.h, v_xt)
                inc1(s.activation(xt_b[:], xt_f[:], AF.Tanh), ss)
            s_tanh = ss.n

            xtT_prev = [0]
            for i in range(2):
                for dc in range(8):
                    @block.tensor
                    def _(t, i=i, dc=dc):
                        if i == 0 and dc == 0:
                            t.wait_ge(ss.h, s_tanh)
                        else:
                            t.wait_ge(ss.h, xtT_prev[0])
                        ins = t.transpose(
                            pT[:, 0:128],
                            xt_b[:, i * DM + dc * 128:i * DM + dc * 128 + 128],
                            ident[:])
                        inc1(ins, ts)
                    t_tr = ts.n

                    @block.scalar
                    def _(s, i=i, dc=dc, t_tr=t_tr):
                        s.wait_ge(ts.h, t_tr)
                        ins = s.activation(
                            xtT[:, dc * ESL + i * 128:dc * ESL + i * 128 + 128],
                            pT[:, 0:128], AF.Identity)
                        inc1(ins, ss)
                    xtT_prev[0] = ss.n
            s_xtT = ss.n

            yt_ev = [0]
            for n in range(EPC):
                sl = n % 2
                if n >= 2:
                    @block.gpsimd
                    def _(g, n=n, sl=sl):
                        g.wait_ge(ts.h, ew_used[sl])
                        src = ew_h[n, :, :].rearrange("(c p) h -> p c h", p=128)
                        dst = ew_sb[:, sl * 8 * HID:(sl + 1) * 8 * HID].rearrange(
                            "p (c h) -> p c h", h=HID)
                        ew_done[n] = inc16(g.dma_start(out=dst, in_=src), dsw)

                @block.tensor
                def _(t, n=n, sl=sl):
                    if n == 0:
                        t.wait_ge(ss.h, s_xtT)
                        t.wait_ge(dsw.h, eb_done)
                    t.wait_ge(dsw.h, ew_done[n])
                    if yt_ev[0]:
                        t.wait_ge(ss.h, yt_ev[0])
                    for hh in range(2):
                        po = pPool[0:32, hh * 512:hh * 512 + 512]
                        for dc in range(8):
                            lhs = xtT[:, dc * ESL + n * 32:dc * ESL + n * 32 + 32]
                            rhs = ew_sb[:, sl * 8 * HID + dc * HID + hh * 512:
                                        sl * 8 * HID + dc * HID + hh * 512 + 512]
                            t.matmul(po, lhs, rhs, start=(dc == 0), stop=False,
                                     skip_group_check=True)
                        ins_last = t.matmul(po, ones1[0:1, 0:32],
                                 eb_sb[0:1, n * HID + hh * 512:n * HID + hh * 512 + 512],
                                 start=False, stop=True, skip_group_check=True)
                    ew_used[sl] = inc1(ins_last, ts)
                t_exp = ts.n

                @block.scalar
                def _(s, n=n, t_exp=t_exp):
                    s.wait_ge(ts.h, t_exp)
                    half = n // 4
                    row = 32 * (n % 4)
                    ins = s.activation(
                        yt_st[row:row + 32, half * HID:(half + 1) * HID],
                        pPool[0:32, 0:1024], AF.Identity)
                    yt_ev[0] = inc1(ins, ss)
            s_yt = ss.n

            @block.sync
            def _(sy):
                sy.wait_ge(ss.h, s_yt)
                for half in range(2):
                    ins = sy.dma_start(
                        out=yt_slice[half * 128:(half + 1) * 128, :],
                        in_=yt_st[:, half * HID:(half + 1) * HID])
                    inc16(ins, dhw)
            yts_done = dhw.n

            @block.gpsimd
            def _(g):
                g.wait_ge(dhw.h, yts_done)
                ins = g.collective_compute(
                    "AllGather", ALU.bypass,
                    replica_groups=[list(range(NCORES))],
                    ins=[yt_slice.ap().opt()], outs=[yt_full.ap().opt()])
                inc1(ins, cc)
            ag_done = cc.n
            exp_ts, exp_ss, exp_vs = ts.n, ss.n, vs.n

            if DEBUG_STOP == "experts":
                _debug_dump(nc, block, dhw, dsw, d_sp, out_h, T,
                            [(cc, ag_done), (ts, exp_ts), (ss, exp_ss)])
                return nc

        with contextlib.ExitStack() as st4:
            woT = st4.enter_context(nc.sbuf_tensor("woT", [128, 8 * DM], BF16))
            yt_sb = st4.enter_context(nc.sbuf_tensor("yt_sb", [128, 16 * HID], BF16))
            attn_t = st4.enter_context(nc.sbuf_tensor("attn_t", [128, 2 * 8 * TT], BF16))
            c_t = st4.enter_context(nc.sbuf_tensor("c_t", [128, 2 * 16 * TT], BF16))
            x2 = st4.enter_context(nc.sbuf_tensor("x2", [128, 2 * NSUB * DM], F32))
            o_st = st4.enter_context(nc.sbuf_tensor("o_st", [128, 2 * DM], F32))

            @block.sync
            def _(sy):
                sy.wait_ge(ts.h, exp_ts)
                sy.wait_ge(ss.h, exp_ss)
                for rc in range(8):
                    dma_tr(sy, woT[:, rc * DM:(rc + 1) * DM],
                           wo_bs[:, rc * 128:(rc + 1) * 128], dtr)
            p2_w = dhw.n
            p2_w_tr = dtr.n

            tile_loaded = [0, 0]
            tile_loaded_tr = [0, 0]
            tile_loaded_sc = [0, 0]
            yt_loaded = [0]
            tile_used = [0, 0]
            ost_done = [0, 0]
            p2_ev = [0, 0]
            for t_i in range(NT):
                tok0 = t_i * TT
                tl = t_i % 2


                @block.sync
                def _(sy, tok0=tok0, tl=tl, t_i=t_i):
                    if t_i == 0:
                        sy.wait_ge(dhw.h, p2_w)
                        sy.wait_ge(dtr.h, p2_w_tr)
                        sy.wait_ge(vs.h, exp_vs)
                    if tile_used[tl]:
                        sy.wait_ge(ts.h, tile_used[tl])
                    for rc in range(8):
                        ins = sy.dma_start(
                            out=attn_t[:, tl * 8 * TT + rc * TT:
                                       tl * 8 * TT + (rc + 1) * TT],
                            in_=attn_sp[rc * 128:(rc + 1) * 128, tok0:tok0 + TT])
                        inc16(ins, dhw)
                    for npc in range(16):
                        ins = sy.dma_start(
                            out=c_t[:, tl * 16 * TT + npc * TT:
                                    tl * 16 * TT + (npc + 1) * TT],
                            in_=c_sp[tok0:tok0 + TT, npc * 128:(npc + 1) * 128],
                            transpose=True)
                        inc16(ins, dtr)
                    src = x_h[tok0:tok0 + TT, :].rearrange("(s p) d -> p s d", p=128)
                    dst = x2[:, tl * NSUB * DM:(tl + 1) * NSUB * DM].rearrange(
                        "p (s d) -> p s d", d=DM)
                    tile_loaded[tl] = inc16(sy.dma_start(out=dst, in_=src), dhw)
                    tile_loaded_tr[tl] = dtr.n
                    if t_i == 0:
                        sy.wait_ge(cc.h, ag_done)
                        srcy = yt_full[:, :].rearrange("(c p) h -> p c h", p=128)
                        dsty = yt_sb[:].rearrange("p (c h) -> p c h", h=HID)
                        yt_loaded[0] = inc16(sy.dma_start(out=dsty, in_=srcy), dhw)

                for s_i in range(NSUB):
                    rot = s_i % 2
                    pslice = pPool[:, rot * 1024:rot * 1024 + 1024]

                    @block.tensor
                    def _(t, s_i=s_i, tl=tl, rot=rot, pslice=pslice):
                        t.wait_ge(dhw.h, tile_loaded[tl])
                        t.wait_ge(dtr.h, tile_loaded_tr[tl])
                        if p2_ev[rot]:
                            t.wait_ge(vs.h, p2_ev[rot])
                        for mh in range(2):
                            po = pslice[:, mh * 512:mh * 512 + 512]
                            for rc in range(8):
                                lhs = attn_t[:, tl * 8 * TT + rc * TT + s_i * 128:
                                             tl * 8 * TT + rc * TT + s_i * 128 + 128]
                                rhs = woT[:, rc * DM + mh * 512:rc * DM + mh * 512 + 512]
                                t.matmul(po, lhs, rhs, start=(rc == 0), stop=False,
                                         skip_group_check=True)
                            if t_i == 0 and s_i == 0 and mh == 0:
                                t.wait_ge(dhw.h, yt_loaded[0])
                            for npc in range(16):
                                lhs = c_t[:, tl * 16 * TT + npc * TT + s_i * 128:
                                          tl * 16 * TT + npc * TT + s_i * 128 + 128]
                                rhs = yt_sb[:, npc * HID + mh * 512:
                                            npc * HID + mh * 512 + 512]
                                ins_last = t.matmul(po, lhs, rhs, start=False, stop=(npc == 15),
                                         skip_group_check=True)
                        tgt = inc1(ins_last, ts)
                        if s_i == NSUB - 1:
                            tile_used[tl] = tgt
                    t_p2 = ts.n

                    @block.vector
                    def _(v, s_i=s_i, tl=tl, rot=rot, pslice=pslice, t_p2=t_p2):
                        v.wait_ge(ts.h, t_p2)
                        if ost_done[rot]:
                            v.wait_ge(dsc.h, ost_done[rot])
                        ins = v.tensor_tensor(
                            o_st[:, rot * DM:(rot + 1) * DM],
                            x2[:, tl * NSUB * DM + s_i * DM:
                               tl * NSUB * DM + (s_i + 1) * DM],
                            pslice, ALU.add)
                        p2_ev[rot] = inc1(ins, vs)
                    v_p2 = vs.n

                    @block.scalar
                    def _(s, tok0=tok0, s_i=s_i, rot=rot, v_p2=v_p2):
                        s.wait_ge(vs.h, v_p2)
                        r0 = tok0 + s_i * 128
                        ins = s.dma_start(out=out_h[r0:r0 + 128, :],
                                          in_=o_st[:, rot * DM:(rot + 1) * DM])
                        ost_done[rot] = inc16(ins, dsc)

            @block.sync
            def _(sy):
                sy.wait_ge(dhw.h, dhw.n)
                sy.wait_ge(dsc.h, dsc.n)
                sy.wait_ge(dtr.h, dtr.n)

        if DEBUG_SCRATCH:
            dbg = {
                "xn_d": (xn_sp, [T, DM], BF16),
                "k_d": (k_sp, [DM, T], BF16),
                "attn_d": (attn_sp, [DM, T], BF16),
                "c_d": (c_sp, [T, E], BF16),
                "dd_d": (d_sp, [T, E], BF16),
                "xb_d": (x_bounce, [E, DM], F32),
            }

            @block.gpsimd
            def _(g):
                g.wait_ge(dhw.h, dhw.n)
                for nm, (srct, shp, dt) in dbg.items():
                    oh = nc.dram_tensor(nm, shp, dt, kind="ExternalOutput")
                    inc16(g.dma_start(out=oh[:, :], in_=srct[:, :]), dsw)
                g.wait_ge(dsw.h, dsw.n)

    return nc


def kernel(L0_attn_x, L0_attn_W_qkv, L0_attn_W_o, L0_ffn_phi_w, L0_ffn_phi_b,
           L0_ffn_expert_w, L0_ffn_expert_b):
    x = np.asarray(L0_attn_x)
    B, T, D = x.shape
    nc = build_program(T=T)
    in_maps = []
    for c in range(NCORES):
        in_maps.append({
            "x": np.ascontiguousarray(x[c]),
            "wqkv": np.ascontiguousarray(L0_attn_W_qkv),
            "wo": np.ascontiguousarray(L0_attn_W_o),
            "phiw": np.ascontiguousarray(L0_ffn_phi_w),
            "phib": np.ascontiguousarray(np.asarray(L0_ffn_phi_b).reshape(1, E)),
            "ew": np.ascontiguousarray(L0_ffn_expert_w[c * EPC:(c + 1) * EPC]),
            "eb": np.ascontiguousarray(
                np.asarray(L0_ffn_expert_b)[c * EPC:(c + 1) * EPC].reshape(1, EPC * HID)),
        })
    res = run_bass_kernel_spmd(nc, in_maps, core_ids=list(range(NCORES)))
    return np.stack([r["out"] for r in res.results], axis=0)


# version 11 (final)
# speedup vs baseline: 1.0538x; 1.0486x over previous
import numpy as np
import contextlib
import concourse.bass as bass
import concourse.mybir as mybir
from concourse.bass_utils import run_bass_kernel_spmd

F32 = mybir.dt.float32
BF16 = mybir.dt.bfloat16
AF = mybir.ActivationFunctionType
ALU = mybir.AluOpType
AX = mybir.AxisListType

NCORES = 8
DM = 1024
H = 16
DH = 64
E = 2048
NEXP = 64
PSLOT = 32
HID = 1024
TT = 512
LN_EPS = 1e-5
QSCALE = 0.125
CLIP_LO = -33000.0
CLIP_HI = 65000.0
EPC = NEXP // NCORES
DEBUG_STOP = None
DEBUG_SCRATCH = False
ESL = E // NCORES


class Sem:
    __slots__ = ("h", "n")

    def __init__(self, h):
        self.h = h
        self.n = 0


def inc1(ins, sem):
    sem.n += 1
    ins.then_inc(sem.h, 1)
    return sem.n


def inc16(ins, sem):
    sem.n += 16
    ins.then_inc(sem.h, 16)
    return sem.n


def dma_tr(sy, out_ap, in_ap, sem):
    F = out_ap.shape[1]
    tgt = sem.n
    for f0 in range(0, F, 512):
        f1 = min(f0 + 512, F)
        ins = sy.dma_start(out=out_ap[:, f0:f1], in_=in_ap[f0:f1, :], transpose=True)
        tgt = inc16(ins, sem)
    return tgt




def _debug_dump(nc, block, dhw, dsw, d_sp, out_h, T, waits):
    @block.gpsimd
    def _(g):
        for s, tgt in waits:
            g.wait_ge(s.h, tgt)
        g.dma_start(out=out_h[:, :], in_=d_sp[0:T, 0:DM]).then_inc(dsw.h, 16)
        dsw.n += 16
        g.wait_ge(dsw.h, dsw.n)

def build_program(T=8192):
    assert T % TT == 0
    NT = T // TT
    NSUB = TT // 128
    NSUBT = T // 128

    nc = bass.Bass("TRN2", target_bir_lowering=False, detect_race_conditions=False)

    x_h = nc.dram_tensor("x", [T, DM], F32, kind="ExternalInput")
    wqkv_h = nc.dram_tensor("wqkv", [3 * DM, DM], F32, kind="ExternalInput")
    wo_h = nc.dram_tensor("wo", [DM, DM], F32, kind="ExternalInput")
    phiw_h = nc.dram_tensor("phiw", [E, DM], F32, kind="ExternalInput")
    phib_h = nc.dram_tensor("phib", [1, E], F32, kind="ExternalInput")
    ew_h = nc.dram_tensor("ew", [EPC, DM, HID], F32, kind="ExternalInput")
    eb_h = nc.dram_tensor("eb", [1, EPC * HID], F32, kind="ExternalInput")
    out_h = nc.dram_tensor("out", [T, DM], F32, kind="ExternalOutput")

    wqkv_bs = nc.dram_tensor("wqkv_bs", [3 * DM, DM], BF16)
    phiw_bs = nc.dram_tensor("phiw_bs", [E, DM], BF16)
    wo_bs = nc.dram_tensor("wo_bs", [DM, DM], BF16)
    xn_sp = nc.dram_tensor("xn_sp", [T, DM], BF16)
    k_sp = nc.dram_tensor("k_sp", [DM, T], BF16)
    attn_sp = nc.dram_tensor("attn_sp", [DM, T], BF16)
    c_sp = nc.dram_tensor("c_sp", [T, E], BF16)
    d_sp = nc.dram_tensor("d_sp", [T, E], BF16)
    x_bounce = nc.dram_tensor("x_bounce", [E, DM], F32)
    x_rs = nc.dram_tensor("x_rs", [ESL, DM], F32)
    yt_slice = nc.dram_tensor("yt_slice", [ESL, HID], BF16)
    yt_full = nc.dram_tensor("yt_full", [E, HID], BF16)

    st = contextlib.ExitStack()
    with st:
        dhw = Sem(st.enter_context(nc.semaphore("dhw")))
        dtr = Sem(st.enter_context(nc.semaphore("dtr")))
        dsc = Sem(st.enter_context(nc.semaphore("dsc")))
        dsw = Sem(st.enter_context(nc.semaphore("dsw")))
        cc = Sem(st.enter_context(nc.semaphore("cc")))
        ts = Sem(st.enter_context(nc.semaphore("ts")))
        vs = Sem(st.enter_context(nc.semaphore("vs")))
        ss = Sem(st.enter_context(nc.semaphore("ss")))
        gs = Sem(st.enter_context(nc.semaphore("gs")))

        pS = st.enter_context(nc.psum_tensor("pS", [128, 512], F32))
        pPool = st.enter_context(nc.psum_tensor("pPool", [128, 2560], F32))
        pT = st.enter_context(nc.psum_tensor("pT", [128, 1024], BF16))

        S_sb = st.enter_context(nc.sbuf_tensor("S_sb", [128, 1024], BF16))
        maskT = st.enter_context(nc.sbuf_tensor("maskT", [128, 128], BF16))
        ident = st.enter_context(nc.sbuf_tensor("ident", [128, 128], BF16))
        ones1 = st.enter_context(nc.sbuf_tensor("ones1", [1, 128], BF16))
        zeros = st.enter_context(nc.sbuf_tensor("zerosb", [1, 512], BF16))
        zk = st.enter_context(nc.sbuf_tensor("zk", [128, 512], BF16))
        phib_sb = st.enter_context(nc.sbuf_tensor("phib_sb", [1, E], BF16))
        eps_sb = st.enter_context(nc.sbuf_tensor("eps_sb", [128, 1], F32))

        block = st.enter_context(nc.Block())

        b01_free = (ss, 0)
        b23_free = (vs, 0)

        @block.gpsimd
        def _(g):
            g.memset(maskT[:], 1.0)
            g.memset(ident[:], 0.0)
            g.memset(ones1[:], 1.0)
            g.memset(zeros[:], 0.0)
            g.memset(zk[:], 0.0)
            g.memset(S_sb[:], 0.0)
            g.memset(eps_sb[:], LN_EPS)
            g.affine_select(out=maskT[:], in_=maskT[:], compare_op=ALU.is_ge,
                            fill=0.0, base=0, pattern=[[1, 128]], channel_multiplier=-1)
            ins_last = g.affine_select(out=ident[:], in_=ident[:], compare_op=ALU.not_equal,
                            fill=1.0, base=0, pattern=[[-1, 128]], channel_multiplier=1)
            g_consts = inc1(ins_last, gs)
            g.dma_start(out=wqkv_bs[:, :], in_=wqkv_h[:, :]).then_inc(dsw.h, 16)
            g.dma_start(out=phiw_bs[:, :], in_=phiw_h[:, :]).then_inc(dsw.h, 16)
            g.dma_start(out=wo_bs[:, :], in_=wo_h[:, :]).then_inc(dsw.h, 16)
            g.dma_start(out=phib_sb[:, :], in_=phib_h[:, :]).then_inc(dsw.h, 16)
            dsw.n += 64
        g_consts = gs.n
        d_wcast = dsw.n
        d_wcast1 = d_wcast - 48

        @block.tensor
        def _(t):
            t.wait_ge(gs.h, g_consts)
            t.matmul(pS[:, :], zk[:, 0:128], zk[:, 0:512],
                     start=True, stop=False, skip_group_check=True)

        with contextlib.ExitStack() as st1:
            wqkvT = st1.enter_context(nc.sbuf_tensor("wqkvT", [128, 8 * 3 * DM], BF16))
            phiT = st1.enter_context(nc.sbuf_tensor("phiT", [128, 8 * E], BF16))
            attn_buf = st1.enter_context(nc.sbuf_tensor("attn_buf", [128, 8 * TT], BF16))
            x_sub = st1.enter_context(nc.sbuf_tensor("x_sub", [128, 4 * DM], F32))

            d_parts = [0, 0, 0]

            @block.sync
            def _(sy):
                sy.wait_ge(dsw.h, d_wcast1)
                for sec in range(3):
                    for dc in range(8):
                        for f0 in (sec * DM, sec * DM + 512):
                            ins = sy.dma_start(
                                out=wqkvT[:, dc * 3 * DM + f0:
                                          dc * 3 * DM + f0 + 512],
                                in_=wqkv_bs[f0:f0 + 512,
                                            dc * 128:(dc + 1) * 128],
                                transpose=True)
                            inc16(ins, dtr)
                    d_parts[sec] = dtr.n
                sy.wait_ge(dsw.h, d_wcast)
                for dc in range(8):
                    dma_tr(sy, phiT[:, dc * E:(dc + 1) * E],
                           phiw_bs[:, dc * 128:(dc + 1) * 128], dtr)
            d_wT = dtr.n
            d_wq, d_wk, d_wv = d_parts
            xn_st2 = st1.enter_context(nc.sbuf_tensor("xn_st2", [128, 4 * DM], BF16))
            sqt = st1.enter_context(nc.sbuf_tensor("sqt", [128, 1024], F32))
            s1 = st1.enter_context(nc.sbuf_tensor("s1", [128, 4], F32))
            s2 = st1.enter_context(nc.sbuf_tensor("s2", [128, 4], F32))
            m_sb = st1.enter_context(nc.sbuf_tensor("m_sb", [128, 4], F32))
            vneg = st1.enter_context(nc.sbuf_tensor("vneg", [128, 4], F32))
            junk1 = st1.enter_context(nc.sbuf_tensor("junk1", [128, 1], F32))
            rstd = st1.enter_context(nc.sbuf_tensor("rstd", [128, 4], F32))
            xn_fm = st1.enter_context(nc.sbuf_tensor("xn_fm", [128, 2 * 8 * TT], BF16))
            q_fm = st1.enter_context(nc.sbuf_tensor("q_fm", [128, 8 * TT], BF16))
            k_fm = st1.enter_context(nc.sbuf_tensor("k_fm", [128, 8 * TT], BF16))
            k_tm = st1.enter_context(nc.sbuf_tensor("k_tm", [128, 2 * DM], BF16))
            v_tm = st1.enter_context(nc.sbuf_tensor("v_tm", [128, 2 * DM], BF16))
            scM = st1.enter_context(nc.sbuf_tensor("scM", [128, 16 * 128], BF16))
            tinv = st1.enter_context(nc.sbuf_tensor("tinv", [128, 2 * 128], F32))
            e1b = st1.enter_context(nc.sbuf_tensor("e1b", [128, 2 * E], BF16))
            c_ob = st1.enter_context(nc.sbuf_tensor("c_ob", [128, 2 * E], BF16))
            sD = st1.enter_context(nc.sbuf_tensor("sD", [128, 64], F32))
            sC = st1.enter_context(nc.sbuf_tensor("sC", [128, 32], F32))
            rD = st1.enter_context(nc.sbuf_tensor("rD", [128, 64], F32))
            rC = st1.enter_context(nc.sbuf_tensor("rC", [128, 32], F32))

            x_done = [0, 0, 0, 0]
            x_used = [0]
            xst_done = [0]
            e1_free = [0, 0]
            c_free = [0, 0]
            kv_used = [0, 0]
            v_evac = [0, 0]
            ktm_done = [0, 0]
            tinv_done = [0, 0]
            tinv_used = [0, 0]
            ic_evac_prev = 0
            scM_used_prev = 0
            attn_stored_prev = 0
            xnfm_used_slot = [0, 0]
            xnfm_done_slot = [0, 0]
            qk_used_prev = 0
            s_scopy = 0

            def emit_ln_stats(lt, s_i):
                tok0 = lt * TT
                r0 = tok0 + s_i * 128

                @block.sync
                def _(sy, s_i=s_i, r0=r0):
                    if x_used[0]:
                        sy.wait_ge(vs.h, x_used[0])
                    ins = sy.dma_start(out=x_sub[:, s_i * DM:(s_i + 1) * DM],
                                       in_=x_h[r0:r0 + 128, :])
                    x_done[s_i] = inc16(ins, dhw)

                @block.vector
                def _(v, s_i=s_i):
                    v.wait_ge(dhw.h, x_done[s_i])
                    xs = x_sub[:, s_i * DM:(s_i + 1) * DM]
                    v.tensor_mul(sqt[:], xs, xs)
                    v.reduce_sum(s1[:, s_i:s_i + 1], xs, axis=AX.X)
                    v.reduce_sum(s2[:, s_i:s_i + 1], sqt[:], axis=AX.X)
                    v.tensor_scalar_mul(m_sb[:, s_i:s_i + 1], s1[:, s_i:s_i + 1],
                                        1.0 / DM)
                    v.tensor_copy(junk1[:], s1[:, 0:1])
                    ins = v.scalar_tensor_tensor(vneg[:, s_i:s_i + 1],
                                                 s1[:, s_i:s_i + 1],
                                                 m_sb[:, s_i:s_i + 1],
                                                 s2[:, s_i:s_i + 1],
                                                 ALU.mult, ALU.subtract)
                    inc1(ins, vs)

            def emit_ln_tail(lt):
                tok0 = lt * TT
                xo = (lt % 2) * 8 * TT
                v_stats = vs.n

                @block.scalar
                def _(s, v_stats=v_stats):
                    s.wait_ge(vs.h, v_stats)
                    ins = s.activation(rstd[:], vneg[:], AF.Sqrt,
                                       bias=eps_sb[:, 0:1], scale=-1.0 / DM)
                    inc1(ins, ss)
                s_sqrt = ss.n

                for s_i in range(NSUB):
                    @block.vector
                    def _(v, s_i=s_i, s_sqrt=s_sqrt):
                        if s_i == 0:
                            v.wait_ge(ss.h, s_sqrt)
                            if xst_done[0]:
                                v.wait_ge(dhw.h, xst_done[0])
                            v.reciprocal(rstd[:], rstd[:])
                            v.tensor_copy(junk1[:], m_sb[:, 0:1])
                        xs = x_sub[:, s_i * DM:(s_i + 1) * DM]
                        ins = v.tensor_scalar(xn_st2[:, s_i * DM:(s_i + 1) * DM], xs,
                                              m_sb[:, s_i:s_i + 1],
                                              rstd[:, s_i:s_i + 1],
                                              ALU.subtract, ALU.mult)
                        if s_i == NSUB - 1:
                            x_used[0] = inc1(ins, vs)
                        else:
                            inc1(ins, vs)
                    v_n = vs.n

                    @block.sync
                    def _(sy, s_i=s_i, v_n=v_n, tok0=tok0):
                        sy.wait_ge(vs.h, v_n)
                        r0 = tok0 + s_i * 128
                        ins = sy.dma_start(
                            out=xn_sp[r0:r0 + 128, :],
                            in_=xn_st2[:, s_i * DM:(s_i + 1) * DM])
                        xst_done[0] = inc16(ins, dhw)
                xn_all_stored = dhw.n

                @block.sync
                def _(sy, tok0=tok0, lt=lt, xo=xo, xn_all_stored=xn_all_stored):
                    sy.wait_ge(dhw.h, xn_all_stored)
                    if xnfm_used_slot[lt % 2]:
                        sy.wait_ge(ts.h, xnfm_used_slot[lt % 2])
                    for dc in range(8):
                        ins = sy.dma_start(
                            out=xn_fm[:, xo + dc * TT:xo + (dc + 1) * TT],
                            in_=xn_sp[tok0:tok0 + TT, dc * 128:(dc + 1) * 128],
                            transpose=True)
                        inc16(ins, dtr)
                xnfm_done_slot[lt % 2] = dtr.n

            def emit_ln(lt):
                for s_i in range(NSUB):
                    emit_ln_stats(lt, s_i)
                emit_ln_tail(lt)

            emit_ln(0)
            for t_i in range(NT):
                tok0 = t_i * TT
                xo = (t_i % 2) * 8 * TT
                xnfm_done = xnfm_done_slot[t_i % 2]

                rot_free = [b01_free, b01_free]
                for which in range(2):
                    sec0 = which * DM
                    for rc in range(8):
                        rot = rc % 2
                        pslice = pPool[:, rot * 512:rot * 512 + 512]

                        @block.tensor
                        def _(t, which=which, rc=rc, rot=rot, pslice=pslice, sec0=sec0, xo=xo):
                            if which == 0 and rc == 0:
                                t.wait_ge(dtr.h, xnfm_done)
                                if t_i == 0:
                                    t.wait_ge(dtr.h, d_wq)
                            if which == 1 and rc == 0 and t_i == 0:
                                t.wait_ge(dtr.h, d_wk)
                                if qk_used_prev:
                                    t.wait_ge(ts.h, qk_used_prev)
                            fsem, ftgt = rot_free[rot]
                            if ftgt:
                                t.wait_ge(fsem.h, ftgt)
                            for dc in range(8):
                                lhs = wqkvT[:, dc * 3 * DM + sec0 + rc * 128:
                                            dc * 3 * DM + sec0 + rc * 128 + 128]
                                rhs = xn_fm[:, xo + dc * TT:xo + dc * TT + TT]
                                ins_last = t.matmul(pslice, lhs, rhs,
                                         start=(dc == 0), stop=(dc == 7),
                                         skip_group_check=True)
                            inc1(ins_last, ts)
                        t_proj = ts.n

                        @block.scalar
                        def _(s, which=which, rc=rc, pslice=pslice, t_proj=t_proj):
                            s.wait_ge(ts.h, t_proj)
                            if which == 0:
                                ins = s.activation(q_fm[:, rc * TT:(rc + 1) * TT],
                                                   pslice, AF.Identity, scale=QSCALE)
                            else:
                                ins = s.activation(k_fm[:, rc * TT:(rc + 1) * TT],
                                                   pslice, AF.Sign)
                            inc1(ins, ss)
                        rot_free[rot] = (ss, ss.n)

                        if which == 1:
                            k_ev = ss.n

                            @block.sync
                            def _(sy, rc=rc, tok0=tok0, k_ev=k_ev):
                                sy.wait_ge(ss.h, k_ev)
                                ins = sy.dma_start(
                                    out=k_sp[rc * 128:(rc + 1) * 128, tok0:tok0 + TT],
                                    in_=k_fm[:, rc * TT:(rc + 1) * TT])
                                inc16(ins, dhw)
                k_stored = dhw.n
                b01_free = rot_free[1] if rot_free[1][1] >= rot_free[0][1] else rot_free[0]

                for s_i in range(NSUB):
                    sl = s_i % 2
                    ci = tok0 // 128 + s_i

                    @block.sync
                    def _(sy, sl=sl, ci=ci):
                        sy.wait_ge(dhw.h, k_stored)
                        if kv_used[sl]:
                            sy.wait_ge(ts.h, kv_used[sl])
                        ktm_done[sl] = dma_tr(
                            sy, k_tm[:, sl * DM:(sl + 1) * DM],
                            k_sp[:, ci * 128:(ci + 1) * 128], dtr)

                    @block.gpsimd
                    def _(g, sl=sl, ci=ci):
                        if tinv_used[sl]:
                            g.wait_ge(vs.h, tinv_used[sl])
                        ins = g.iota(tinv[:, sl * 128:(sl + 1) * 128], [[1, 128]],
                                     base=ci * 128 + 1, channel_multiplier=0,
                                     allow_small_or_imprecise_dtypes=True)
                        inc1(ins, gs)
                    g_iota = gs.n

                    @block.vector
                    def _(v, sl=sl, g_iota=g_iota):
                        v.wait_ge(gs.h, g_iota)
                        ins = v.reciprocal(tinv[:, sl * 128:(sl + 1) * 128],
                                           tinv[:, sl * 128:(sl + 1) * 128])
                        tinv_done[sl] = inc1(ins, vs)

                    @block.tensor
                    def _(t, s_i=s_i, sl=sl, xo=xo, t_i=t_i):
                        if t_i == 0 and s_i == 0:
                            t.wait_ge(dtr.h, d_wv)
                        fsem, ftgt = b23_free
                        if ftgt:
                            t.wait_ge(fsem.h, ftgt)
                        if kv_used[sl]:
                            t.wait_ge(ts.h, 0)
                        for hh in range(2):
                            po = pPool[:, 1024 + hh * 512:1024 + hh * 512 + 512]
                            for dc in range(8):
                                lhs = xn_fm[:, xo + dc * TT + s_i * 128:
                                            xo + dc * TT + s_i * 128 + 128]
                                rhs = wqkvT[:, dc * 3 * DM + 2 * DM + hh * 512:
                                            dc * 3 * DM + 2 * DM + hh * 512 + 512]
                                ins_last = t.matmul(po, lhs, rhs, start=(dc == 0), stop=(dc == 7),
                                         skip_group_check=True)
                        inc1(ins_last, ts)
                    t_vproj = ts.n

                    @block.scalar
                    def _(s, sl=sl, t_vproj=t_vproj):
                        s.wait_ge(ts.h, t_vproj)
                        if kv_used[sl]:
                            s.wait_ge(ts.h, kv_used[sl])
                        ins = s.activation(v_tm[:, sl * DM:(sl + 1) * DM],
                                           pPool[:, 1024:2048], AF.Sign)
                        v_evac[sl] = inc1(ins, ss)
                    v_ready = ss.n

                    sc_wave_v = 0
                    for w in range(2):
                        @block.tensor
                        def _(t, w=w, s_i=s_i, sc_wave_v=sc_wave_v):
                            if w == 0:
                                fsem, ftgt = b01_free
                                if ftgt:
                                    t.wait_ge(fsem.h, ftgt)
                                if scM_used_prev:
                                    t.wait_ge(ts.h, 0)
                            else:
                                t.wait_ge(vs.h, sc_wave_v)
                            for b in range(2):
                                t.matmul(pPool[:, b * 512:b * 512 + 512],
                                         zk[:, 0:128], zk[:, 0:512],
                                         start=True, stop=False,
                                         skip_group_check=True)
                            ro = 64 * w
                            for hh in range(8):
                                h = 2 * hh + w
                                rc = h // 2
                                lhs = k_fm[ro:ro + 64, rc * TT + s_i * 128:
                                           rc * TT + s_i * 128 + 128]
                                rhs = q_fm[ro:ro + 64, rc * TT + s_i * 128:
                                           rc * TT + s_i * 128 + 128]
                                ins_last = t.matmul(pPool[:, hh * 128:hh * 128 + 128], lhs, rhs,
                                         start=False, stop=(hh % 4 == 3),
                                         skip_group_check=True,
                                         tile_position=(ro, 0))
                            inc1(ins_last, ts)
                        t_sc = ts.n

                        @block.vector
                        def _(v, w=w, t_sc=t_sc):
                            v.wait_ge(ts.h, t_sc)
                            if scM_used_prev:
                                v.wait_ge(ts.h, scM_used_prev)
                            src = pPool[:, 0:1024].rearrange("p (h i) -> p h i", h=8)
                            msk = maskT[:].rearrange("p (o i) -> p o i", o=1
                                                     ).broadcast_to([128, 8, 128])
                            dst = scM[:].rearrange("p (h q) -> p h q", h=8)[
                                :, :, w * 128:(w + 1) * 128]
                            inc1(v.tensor_tensor(dst, src, msk, ALU.mult), vs)
                        sc_wave_v = vs.n
                    sc_done = vs.n
                    b01_free = (vs, sc_done)

                    @block.tensor
                    def _(t, sl=sl, s_i=s_i, sc_done=sc_done):
                        t.wait_ge(vs.h, sc_done)
                        t.wait_ge(ss.h, v_ready)
                        t.wait_ge(ss.h, s_scopy)
                        if ic_evac_prev:
                            t.wait_ge(vs.h, ic_evac_prev)
                        for b in range(2):
                            t.matmul(pPool[:, 1024 + b * 512:1024 + b * 512 + 512],
                                     zk[:, 0:128], zk[:, 0:512],
                                     start=True, stop=False,
                                     skip_group_check=True)
                        for p in range(8):
                            pic = pPool[:, 1024 + p * 128:1024 + p * 128 + 128]
                            for par in range(2):
                                ro = 64 * par
                                sM = scM[:, (2 * p + par) * 128:(2 * p + par) * 128 + 128]
                                t.matmul(pic[ro:ro + 64, :],
                                         v_tm[:, sl * DM + p * 128 + ro:
                                              sl * DM + p * 128 + ro + 64],
                                         sM, start=False, stop=False,
                                         skip_group_check=True, tile_position=(0, ro))
                            ins_last = t.matmul(pic[:, :],
                                     S_sb[:, p * 128:p * 128 + 128],
                                     q_fm[:, p * TT + s_i * 128:
                                          p * TT + s_i * 128 + 128],
                                     start=False, stop=(p % 4 == 3),
                                     skip_group_check=True)
                        inc1(ins_last, ts)
                    t_ic = ts.n
                    scM_used_prev = t_ic

                    @block.tensor
                    def _(t, sl=sl):
                        t.wait_ge(dtr.h, ktm_done[sl])
                        for p in range(8):
                            for par in range(2):
                                ro = 64 * par
                                ins_last = t.matmul(pS[ro:ro + 64, p * 64:p * 64 + 64],
                                         k_tm[:, sl * DM + p * 128 + ro:
                                              sl * DM + p * 128 + ro + 64],
                                         v_tm[:, sl * DM + p * 128 + ro:
                                              sl * DM + p * 128 + ro + 64],
                                         start=False, stop=False,
                                         skip_group_check=True, tile_position=(0, ro))
                        kv_used[sl] = inc1(ins_last, ts)
                    t_ckv = ts.n

                    @block.scalar
                    def _(s, t_ckv=t_ckv):
                        s.wait_ge(ts.h, t_ckv)
                        dste = S_sb[0:64, :].rearrange("p (q c) -> p q c", q=8)[
                            :, :, 0:64]
                        srce = pS[0:64, :].rearrange("p (q c) -> p q c", q=8)
                        s.activation(dste, srce, AF.Identity)
                        dsto = S_sb[64:128, :].rearrange("p (q c) -> p q c", q=8)[
                            :, :, 64:128]
                        srco = pS[64:128, :].rearrange("p (q c) -> p q c", q=8)
                        inc1(s.activation(dsto, srco, AF.Identity), ss)
                    s_scopy = ss.n

                    @block.vector
                    def _(v, s_i=s_i, sl=sl, t_ic=t_ic):
                        v.wait_ge(ts.h, t_ic)
                        v.wait_ge(vs.h, tinv_done[sl])
                        if attn_stored_prev:
                            v.wait_ge(dhw.h, attn_stored_prev)
                        src = pPool[:, 1024:2048].rearrange("p (q i) -> p q i", q=8)
                        tv = tinv[:, sl * 128:(sl + 1) * 128].rearrange(
                            "p (o i) -> p o i", o=1).broadcast_to([128, 8, 128])
                        dst = attn_buf[:].rearrange("p (q c) -> p q c", q=8)[
                            :, :, s_i * 128:s_i * 128 + 128]
                        ins = v.tensor_tensor(dst, src, tv, ALU.mult)
                        tinv_used[sl] = inc1(ins, vs)
                    ic_evac_prev = vs.n
                    b23_free = (vs, vs.n)

                    if t_i + 1 < NT:
                        emit_ln_stats(t_i + 1, s_i)

                @block.sync
                def _(sy, tok0=tok0, ic_evac_prev=ic_evac_prev):
                    sy.wait_ge(vs.h, ic_evac_prev)
                    for p in range(8):
                        ins = sy.dma_start(
                            out=attn_sp[p * 128:(p + 1) * 128, tok0:tok0 + TT],
                            in_=attn_buf[:, p * TT:(p + 1) * TT])
                        inc16(ins, dhw)
                attn_stored_prev = dhw.n

                if t_i + 1 < NT:
                    emit_ln_tail(t_i + 1)

                rot_free = [b01_free, b01_free]
                for s_i in range(NSUB):
                    sl = s_i % 2
                    r0 = tok0 + s_i * 128
                    for eg in range(4):
                        rot = eg % 2
                        pslice = pPool[:, rot * 512:rot * 512 + 512]

                        @block.tensor
                        def _(t, s_i=s_i, eg=eg, rot=rot, pslice=pslice, xo=xo,
                              t_i=t_i):
                            if t_i == 0 and s_i == 0 and eg == 0:
                                t.wait_ge(dtr.h, d_wT)
                            fsem, ftgt = rot_free[rot]
                            if ftgt:
                                t.wait_ge(fsem.h, ftgt)
                            for dc in range(8):
                                lhs = xn_fm[:, xo + dc * TT + s_i * 128:
                                            xo + dc * TT + s_i * 128 + 128]
                                rhs = phiT[:, dc * E + eg * 512:dc * E + eg * 512 + 512]
                                t.matmul(pslice, lhs, rhs, start=(dc == 0), stop=False,
                                         skip_group_check=True)
                            ins_last = t.matmul(pslice, ones1[0:1, 0:128],
                                     phib_sb[0:1, eg * 512:eg * 512 + 512],
                                     start=False, stop=True, skip_group_check=True)
                            inc1(ins_last, ts)
                        t_lg = ts.n

                        @block.scalar
                        def _(s, sl=sl, eg=eg, pslice=pslice, t_lg=t_lg):
                            s.wait_ge(ts.h, t_lg)
                            if eg == 0 and e1_free[sl]:
                                s.wait_ge(dhw.h, e1_free[sl])
                            ins = s.activation(
                                e1b[:, sl * E + eg * 512:sl * E + eg * 512 + 512],
                                pslice, AF.Exp)
                            inc1(ins, ss)
                        rot_free[rot] = (ss, ss.n)
                    e1_ready = ss.n

                    @block.vector
                    def _(v, sl=sl, e1_ready=e1_ready):
                        v.wait_ge(ss.h, e1_ready)
                        e3 = e1b[:, sl * E:(sl + 1) * E].rearrange(
                            "p (n q) -> p n q", n=64, q=32)
                        e3s = e1b[:, sl * E:(sl + 1) * E].rearrange(
                            "p (n q) -> p q n", n=64, q=32)
                        v.tensor_reduce(sC[:].rearrange("p (q o) -> p q o", o=1),
                                        e3s, axis=AX.X, op=ALU.add)
                        v.tensor_reduce(sD[:].rearrange("p (n o) -> p n o", o=1),
                                        e3, axis=AX.X, op=ALU.add)
                        v.reciprocal(rC[:], sC[:])
                        v.reciprocal(rD[:], sD[:])
                        if c_free[sl]:
                            v.wait_ge(dhw.h, c_free[sl])
                        e3 = e1b[:, sl * E:(sl + 1) * E].rearrange(
                            "p (n q) -> p n q", n=64, q=32)
                        co = c_ob[:, sl * E:(sl + 1) * E].rearrange(
                            "p (n q) -> p n q", n=64, q=32)
                        rc_b = rC[:].rearrange("p (o q) -> p o q", o=1
                                               ).broadcast_to([128, 64, 32])
                        v.tensor_tensor(co, e3, rc_b, ALU.mult)
                        rd_b = rD[:].rearrange("p (n o) -> p n o", o=1
                                               ).broadcast_to([128, 64, 32])
                        inc1(v.tensor_tensor(e3, e3, rd_b, ALU.mult), vs)
                    cd_done = vs.n

                    @block.sync
                    def _(sy, sl=sl, r0=r0, cd_done=cd_done):
                        sy.wait_ge(vs.h, cd_done)
                        ins = sy.dma_start(out=c_sp[r0:r0 + 128, :],
                                           in_=c_ob[:, sl * E:(sl + 1) * E])
                        c_free[sl] = inc16(ins, dhw)
                        ins = sy.dma_start(out=d_sp[r0:r0 + 128, :],
                                           in_=e1b[:, sl * E:(sl + 1) * E])
                        e1_free[sl] = inc16(ins, dhw)
                xnfm_used_slot[t_i % 2] = ts.n
                qk_used_prev = scM_used_prev
                b01_free = rot_free[0] if rot_free[0][1] >= rot_free[1][1] else rot_free[1]

            p1_dhw, p1_ts, p1_vs, p1_ss, p1_gs = dhw.n, ts.n, vs.n, ss.n, gs.n

        if DEBUG_STOP == "pass1":
            _debug_dump(nc, block, dhw, dsw, d_sp, out_h, T,
                        [(dhw, p1_dhw), (ts, p1_ts), (vs, p1_vs), (ss, p1_ss), (gs, p1_gs)])
            return nc

        with contextlib.ExitStack() as st2:
            xn_full = st2.enter_context(
                nc.sbuf_tensor("xn_full", [128, NSUBT * DM], BF16))
            d_col = st2.enter_context(
                nc.sbuf_tensor("d_col", [128, 3 * NSUBT * 128], BF16))
            x_stage = st2.enter_context(nc.sbuf_tensor("x_stage", [128, 2 * 2 * DM], F32))

            @block.sync
            def _(sy):
                sy.wait_ge(dhw.h, p1_dhw)
                sy.wait_ge(ts.h, p1_ts)
                sy.wait_ge(vs.h, p1_vs)
                sy.wait_ge(ss.h, p1_ss)
                sy.wait_ge(gs.h, p1_gs)
                nq = min(8, T // 128)
                rows = T // nq
                for i in range(nq):
                    src = xn_sp[i * rows:(i + 1) * rows, :].rearrange(
                        "(s p) d -> p s d", p=128)
                    dst = xn_full[:, i * (rows // 128) * DM:
                                  (i + 1) * (rows // 128) * DM].rearrange(
                        "p (s d) -> p s d", d=DM)
                    inc16(sy.dma_start(out=dst, in_=src), dhw)
            xnfull_done = dhw.n

            dcol_done = [0, 0, 0]
            dcol_used = [0, 0, 0]
            xev = [0, 0, 0, 0]
            xst_t = [0, 0]
            for ec in range(16):
                sl = ec % 3

                @block.sync
                def _(sy, ec=ec, sl=sl):
                    if dcol_used[sl]:
                        sy.wait_ge(ts.h, dcol_used[sl])
                    nq = max(1, T // 8192)
                    rows = T // nq
                    for i in range(nq):
                        src = d_sp[i * rows:(i + 1) * rows, ec * 128:(ec + 1) * 128
                                   ].rearrange("(s p) e -> p s e", p=128)
                        dst = d_col[:, sl * NSUBT * 128 + i * rows:
                                    sl * NSUBT * 128 + (i + 1) * rows
                                    ].rearrange("p (s e) -> p s e", e=128)
                        inc16(sy.dma_start(out=dst, in_=src), dhw)
                    dcol_done[sl] = dhw.n

                for dh_i in range(2):
                    bk = (ec % 2) * 2 + dh_i
                    pslice = pPool[:, bk * 512:bk * 512 + 512]

                    @block.tensor
                    def _(t, ec=ec, sl=sl, dh_i=dh_i, pslice=pslice):
                        if ec == 0 and dh_i == 0:
                            t.wait_ge(dhw.h, xnfull_done)
                        t.wait_ge(dhw.h, dcol_done[sl])
                        if xev[(ec % 2) * 2 + dh_i]:
                            t.wait_ge(vs.h, xev[(ec % 2) * 2 + dh_i])
                        for sub in range(NSUBT):
                            lhs = d_col[:, sl * NSUBT * 128 + sub * 128:
                                        sl * NSUBT * 128 + sub * 128 + 128]
                            rhs = xn_full[:, sub * DM + dh_i * 512:
                                          sub * DM + dh_i * 512 + 512]
                            ins_last = t.matmul(pslice, lhs, rhs, start=(sub == 0),
                                     stop=(sub == NSUBT - 1), skip_group_check=True)
                        if dh_i == 1:
                            dcol_used[sl] = ts.n + 1
                        inc1(ins_last, ts)
                    t_x = ts.n

                    @block.vector
                    def _(v, ec=ec, dh_i=dh_i, pslice=pslice, t_x=t_x):
                        v.wait_ge(ts.h, t_x)
                        if xst_t[ec % 2]:
                            v.wait_ge(dsc.h, xst_t[ec % 2])
                        so_ = (ec % 2) * 2 * 512
                        ins = v.tensor_copy(
                            x_stage[:, so_ + dh_i * 512:so_ + dh_i * 512 + 512],
                            pslice)
                        xev[(ec % 2) * 2 + dh_i] = inc1(ins, vs)
                v_x = vs.n

                @block.scalar
                def _(sy, ec=ec, v_x=v_x):
                    sy.wait_ge(vs.h, v_x)
                    so_ = (ec % 2) * DM
                    ins = sy.dma_start(out=x_bounce[ec * 128:(ec + 1) * 128, :],
                                       in_=x_stage[:, so_:so_ + DM])
                    xst_t[ec % 2] = inc16(ins, dsc)
            xb_done = dsc.n

            if DEBUG_STOP == "xphase":
                _debug_dump(nc, block, dhw, dsw, d_sp, out_h, T,
                            [(dhw, xb_done), (ts, ts.n), (vs, vs.n)])
                return nc

            @block.gpsimd
            def _(g):
                g.wait_ge(dsc.h, xb_done)
                ins = g.collective_compute(
                    "ReduceScatter", ALU.add,
                    replica_groups=[list(range(NCORES))],
                    ins=[x_bounce.ap().opt()], outs=[x_rs.ap().opt()])
                inc1(ins, cc)
            rs_done = cc.n

            if DEBUG_STOP == "exchange":
                _debug_dump(nc, block, dhw, dsw, d_sp, out_h, T, [(cc, rs_done)])
                return nc

        with contextlib.ExitStack() as st3:
            xt_f = st3.enter_context(nc.sbuf_tensor("xt_f", [128, 2 * DM], F32))
            xt_b = st3.enter_context(nc.sbuf_tensor("xt_b", [128, 2 * DM], BF16))
            xtT = st3.enter_context(nc.sbuf_tensor("xtT", [128, 8 * ESL], BF16))
            ew_sb = st3.enter_context(nc.sbuf_tensor("ew_sb", [128, 2 * 8 * HID], BF16))
            eb_sb = st3.enter_context(nc.sbuf_tensor("eb_sb", [1, EPC * HID], BF16))
            yt_st = st3.enter_context(nc.sbuf_tensor("yt_st", [128, 2 * HID], BF16))
            sqtb = st3.enter_context(nc.sbuf_tensor("sqtb", [128, 1024], F32))
            s1b = st3.enter_context(nc.sbuf_tensor("s1b", [128, 1], F32))
            s2b = st3.enter_context(nc.sbuf_tensor("s2b", [128, 1], F32))
            m_sbb = st3.enter_context(nc.sbuf_tensor("m_sbb", [128, 1], F32))
            vnegb = st3.enter_context(nc.sbuf_tensor("vnegb", [128, 1], F32))
            junkb = st3.enter_context(nc.sbuf_tensor("junkb", [128, 1], F32))
            rstdb = st3.enter_context(nc.sbuf_tensor("rstdb", [128, 1], F32))

            ew_done = {}
            ew_used = [0, 0]

            @block.gpsimd
            def _(g):
                g.wait_ge(dsc.h, xb_done)
                inc16(g.dma_start(out=eb_sb[:, :], in_=eb_h[:, :]), dsw)
                for n in range(2):
                    src = ew_h[n, :, :].rearrange("(c p) h -> p c h", p=128)
                    dst = ew_sb[:, n * 8 * HID:(n + 1) * 8 * HID].rearrange(
                        "p (c h) -> p c h", h=HID)
                    ew_done[n] = inc16(g.dma_start(out=dst, in_=src), dsw)
            eb_done = dsw.n

            @block.sync
            def _(sy):
                sy.wait_ge(cc.h, rs_done)
                for i in range(2):
                    ins = sy.dma_start(out=xt_f[:, i * DM:(i + 1) * DM],
                                       in_=x_rs[i * 128:(i + 1) * 128, :])
                    inc16(ins, dhw)
            xrs_loaded = dhw.n

            for i in range(2):
                @block.vector
                def _(v, i=i):
                    v.wait_ge(dhw.h, xrs_loaded)
                    xs = xt_f[:, i * DM:(i + 1) * DM]
                    v.tensor_scalar(xs, xs, CLIP_LO, CLIP_HI, ALU.max, ALU.min)
                    v.tensor_copy(junkb[:], eps_sb[:])
                    v.tensor_mul(sqtb[:], xs, xs)
                    v.reduce_sum(s1b[:], xs, axis=AX.X)
                    v.reduce_sum(s2b[:], sqtb[:], axis=AX.X)
                    v.tensor_scalar_mul(m_sbb[:], s1b[:], 1.0 / DM)
                    v.tensor_copy(junkb[:], s1b[:])
                    ins = v.scalar_tensor_tensor(vnegb[:], s1b[:], m_sbb[:, 0:1],
                                                 s2b[:], ALU.mult, ALU.subtract)
                    inc1(ins, vs)
                v_st = vs.n

                @block.scalar
                def _(s, v_st=v_st):
                    s.wait_ge(vs.h, v_st)
                    ins = s.activation(rstdb[:], vnegb[:], AF.Sqrt,
                                       bias=eps_sb[:, 0:1], scale=-1.0 / DM)
                    inc1(ins, ss)
                s_sq = ss.n

                @block.vector
                def _(v, i=i, s_sq=s_sq):
                    v.wait_ge(ss.h, s_sq)
                    v.reciprocal(rstdb[:], rstdb[:])
                    v.tensor_copy(junkb[:], m_sbb[:])
                    xs = xt_f[:, i * DM:(i + 1) * DM]
                    ins = v.tensor_scalar(xs, xs, m_sbb[:, 0:1], rstdb[:, 0:1],
                                          ALU.subtract, ALU.mult)
                    inc1(ins, vs)
            v_xt = vs.n

            @block.scalar
            def _(s):
                s.wait_ge(vs.h, v_xt)
                inc1(s.activation(xt_b[:], xt_f[:], AF.Tanh), ss)
            s_tanh = ss.n

            xtT_prev = [0]
            for i in range(2):
                for dc in range(8):
                    @block.tensor
                    def _(t, i=i, dc=dc):
                        if i == 0 and dc == 0:
                            t.wait_ge(ss.h, s_tanh)
                        else:
                            t.wait_ge(ss.h, xtT_prev[0])
                        ins = t.transpose(
                            pT[:, 0:128],
                            xt_b[:, i * DM + dc * 128:i * DM + dc * 128 + 128],
                            ident[:])
                        inc1(ins, ts)
                    t_tr = ts.n

                    @block.scalar
                    def _(s, i=i, dc=dc, t_tr=t_tr):
                        s.wait_ge(ts.h, t_tr)
                        ins = s.activation(
                            xtT[:, dc * ESL + i * 128:dc * ESL + i * 128 + 128],
                            pT[:, 0:128], AF.Identity)
                        inc1(ins, ss)
                    xtT_prev[0] = ss.n
            s_xtT = ss.n

            yt_ev = [0]
            for n in range(EPC):
                sl = n % 2
                if n >= 2:
                    @block.gpsimd
                    def _(g, n=n, sl=sl):
                        g.wait_ge(ts.h, ew_used[sl])
                        src = ew_h[n, :, :].rearrange("(c p) h -> p c h", p=128)
                        dst = ew_sb[:, sl * 8 * HID:(sl + 1) * 8 * HID].rearrange(
                            "p (c h) -> p c h", h=HID)
                        ew_done[n] = inc16(g.dma_start(out=dst, in_=src), dsw)

                @block.tensor
                def _(t, n=n, sl=sl):
                    if n == 0:
                        t.wait_ge(ss.h, s_xtT)
                        t.wait_ge(dsw.h, eb_done)
                    t.wait_ge(dsw.h, ew_done[n])
                    if yt_ev[0]:
                        t.wait_ge(ss.h, yt_ev[0])
                    for hh in range(2):
                        po = pPool[0:32, hh * 512:hh * 512 + 512]
                        for dc in range(8):
                            lhs = xtT[:, dc * ESL + n * 32:dc * ESL + n * 32 + 32]
                            rhs = ew_sb[:, sl * 8 * HID + dc * HID + hh * 512:
                                        sl * 8 * HID + dc * HID + hh * 512 + 512]
                            t.matmul(po, lhs, rhs, start=(dc == 0), stop=False,
                                     skip_group_check=True)
                        ins_last = t.matmul(po, ones1[0:1, 0:32],
                                 eb_sb[0:1, n * HID + hh * 512:n * HID + hh * 512 + 512],
                                 start=False, stop=True, skip_group_check=True)
                    ew_used[sl] = inc1(ins_last, ts)
                t_exp = ts.n

                @block.scalar
                def _(s, n=n, t_exp=t_exp):
                    s.wait_ge(ts.h, t_exp)
                    half = n // 4
                    row = 32 * (n % 4)
                    ins = s.activation(
                        yt_st[row:row + 32, half * HID:(half + 1) * HID],
                        pPool[0:32, 0:1024], AF.Identity)
                    yt_ev[0] = inc1(ins, ss)
            s_yt = ss.n

            @block.sync
            def _(sy):
                sy.wait_ge(ss.h, s_yt)
                for half in range(2):
                    ins = sy.dma_start(
                        out=yt_slice[half * 128:(half + 1) * 128, :],
                        in_=yt_st[:, half * HID:(half + 1) * HID])
                    inc16(ins, dhw)
            yts_done = dhw.n

            @block.gpsimd
            def _(g):
                g.wait_ge(dhw.h, yts_done)
                ins = g.collective_compute(
                    "AllGather", ALU.bypass,
                    replica_groups=[list(range(NCORES))],
                    ins=[yt_slice.ap().opt()], outs=[yt_full.ap().opt()])
                inc1(ins, cc)
            ag_done = cc.n
            exp_ts, exp_ss, exp_vs = ts.n, ss.n, vs.n

            if DEBUG_STOP == "experts":
                _debug_dump(nc, block, dhw, dsw, d_sp, out_h, T,
                            [(cc, ag_done), (ts, exp_ts), (ss, exp_ss)])
                return nc

        with contextlib.ExitStack() as st4:
            woT = st4.enter_context(nc.sbuf_tensor("woT", [128, 8 * DM], BF16))
            yt_sb = st4.enter_context(nc.sbuf_tensor("yt_sb", [128, 16 * HID], BF16))
            attn_t = st4.enter_context(nc.sbuf_tensor("attn_t", [128, 2 * 8 * TT], BF16))
            c_t = st4.enter_context(nc.sbuf_tensor("c_t", [128, 2 * 16 * TT], BF16))
            x2 = st4.enter_context(nc.sbuf_tensor("x2", [128, 2 * NSUB * DM], F32))
            o_st = st4.enter_context(nc.sbuf_tensor("o_st", [128, 2 * DM], F32))

            @block.sync
            def _(sy):
                sy.wait_ge(ts.h, exp_ts)
                sy.wait_ge(ss.h, exp_ss)
                for rc in range(8):
                    dma_tr(sy, woT[:, rc * DM:(rc + 1) * DM],
                           wo_bs[:, rc * 128:(rc + 1) * 128], dtr)
            p2_w = dhw.n
            p2_w_tr = dtr.n

            tile_loaded = [0, 0]
            tile_loaded_tr = [0, 0]
            tile_loaded_sc = [0, 0]
            yt_loaded = [0]
            tile_used = [0, 0]
            ost_done = [0, 0]
            p2_ev = [0, 0]
            for t_i in range(NT):
                tok0 = t_i * TT
                tl = t_i % 2


                @block.sync
                def _(sy, tok0=tok0, tl=tl, t_i=t_i):
                    if t_i == 0:
                        sy.wait_ge(dhw.h, p2_w)
                        sy.wait_ge(dtr.h, p2_w_tr)
                        sy.wait_ge(vs.h, exp_vs)
                    if tile_used[tl]:
                        sy.wait_ge(ts.h, tile_used[tl])
                    for rc in range(8):
                        ins = sy.dma_start(
                            out=attn_t[:, tl * 8 * TT + rc * TT:
                                       tl * 8 * TT + (rc + 1) * TT],
                            in_=attn_sp[rc * 128:(rc + 1) * 128, tok0:tok0 + TT])
                        inc16(ins, dhw)
                    for npc in range(16):
                        ins = sy.dma_start(
                            out=c_t[:, tl * 16 * TT + npc * TT:
                                    tl * 16 * TT + (npc + 1) * TT],
                            in_=c_sp[tok0:tok0 + TT, npc * 128:(npc + 1) * 128],
                            transpose=True)
                        inc16(ins, dtr)
                    src = x_h[tok0:tok0 + TT, :].rearrange("(s p) d -> p s d", p=128)
                    dst = x2[:, tl * NSUB * DM:(tl + 1) * NSUB * DM].rearrange(
                        "p (s d) -> p s d", d=DM)
                    tile_loaded[tl] = inc16(sy.dma_start(out=dst, in_=src), dhw)
                    tile_loaded_tr[tl] = dtr.n
                    if t_i == 0:
                        sy.wait_ge(cc.h, ag_done)
                        srcy = yt_full[:, :].rearrange("(c p) h -> p c h", p=128)
                        dsty = yt_sb[:].rearrange("p (c h) -> p c h", h=HID)
                        yt_loaded[0] = inc16(sy.dma_start(out=dsty, in_=srcy), dhw)

                for s_i in range(NSUB):
                    rot = s_i % 2
                    pslice = pPool[:, rot * 1024:rot * 1024 + 1024]

                    @block.tensor
                    def _(t, s_i=s_i, tl=tl, rot=rot, pslice=pslice):
                        t.wait_ge(dhw.h, tile_loaded[tl])
                        t.wait_ge(dtr.h, tile_loaded_tr[tl])
                        if p2_ev[rot]:
                            t.wait_ge(vs.h, p2_ev[rot])
                        for mh in range(2):
                            po = pslice[:, mh * 512:mh * 512 + 512]
                            for rc in range(8):
                                lhs = attn_t[:, tl * 8 * TT + rc * TT + s_i * 128:
                                             tl * 8 * TT + rc * TT + s_i * 128 + 128]
                                rhs = woT[:, rc * DM + mh * 512:rc * DM + mh * 512 + 512]
                                t.matmul(po, lhs, rhs, start=(rc == 0), stop=False,
                                         skip_group_check=True)
                            if t_i == 0 and s_i == 0 and mh == 0:
                                t.wait_ge(dhw.h, yt_loaded[0])
                            for npc in range(16):
                                lhs = c_t[:, tl * 16 * TT + npc * TT + s_i * 128:
                                          tl * 16 * TT + npc * TT + s_i * 128 + 128]
                                rhs = yt_sb[:, npc * HID + mh * 512:
                                            npc * HID + mh * 512 + 512]
                                ins_last = t.matmul(po, lhs, rhs, start=False, stop=(npc == 15),
                                         skip_group_check=True)
                        tgt = inc1(ins_last, ts)
                        if s_i == NSUB - 1:
                            tile_used[tl] = tgt
                    t_p2 = ts.n

                    @block.vector
                    def _(v, s_i=s_i, tl=tl, rot=rot, pslice=pslice, t_p2=t_p2):
                        v.wait_ge(ts.h, t_p2)
                        if ost_done[rot]:
                            v.wait_ge(dsc.h, ost_done[rot])
                        ins = v.tensor_tensor(
                            o_st[:, rot * DM:(rot + 1) * DM],
                            x2[:, tl * NSUB * DM + s_i * DM:
                               tl * NSUB * DM + (s_i + 1) * DM],
                            pslice, ALU.add)
                        p2_ev[rot] = inc1(ins, vs)
                    v_p2 = vs.n

                    @block.scalar
                    def _(s, tok0=tok0, s_i=s_i, rot=rot, v_p2=v_p2):
                        s.wait_ge(vs.h, v_p2)
                        r0 = tok0 + s_i * 128
                        ins = s.dma_start(out=out_h[r0:r0 + 128, :],
                                          in_=o_st[:, rot * DM:(rot + 1) * DM])
                        ost_done[rot] = inc16(ins, dsc)

            @block.sync
            def _(sy):
                sy.wait_ge(dhw.h, dhw.n)
                sy.wait_ge(dsc.h, dsc.n)
                sy.wait_ge(dtr.h, dtr.n)

        if DEBUG_SCRATCH:
            dbg = {
                "xn_d": (xn_sp, [T, DM], BF16),
                "k_d": (k_sp, [DM, T], BF16),
                "attn_d": (attn_sp, [DM, T], BF16),
                "c_d": (c_sp, [T, E], BF16),
                "dd_d": (d_sp, [T, E], BF16),
                "xb_d": (x_bounce, [E, DM], F32),
            }

            @block.gpsimd
            def _(g):
                g.wait_ge(dhw.h, dhw.n)
                for nm, (srct, shp, dt) in dbg.items():
                    oh = nc.dram_tensor(nm, shp, dt, kind="ExternalOutput")
                    inc16(g.dma_start(out=oh[:, :], in_=srct[:, :]), dsw)
                g.wait_ge(dsw.h, dsw.n)

    return nc


def kernel(L0_attn_x, L0_attn_W_qkv, L0_attn_W_o, L0_ffn_phi_w, L0_ffn_phi_b,
           L0_ffn_expert_w, L0_ffn_expert_b):
    x = np.asarray(L0_attn_x)
    B, T, D = x.shape
    nc = build_program(T=T)
    in_maps = []
    for c in range(NCORES):
        in_maps.append({
            "x": np.ascontiguousarray(x[c]),
            "wqkv": np.ascontiguousarray(L0_attn_W_qkv),
            "wo": np.ascontiguousarray(L0_attn_W_o),
            "phiw": np.ascontiguousarray(L0_ffn_phi_w),
            "phib": np.ascontiguousarray(np.asarray(L0_ffn_phi_b).reshape(1, E)),
            "ew": np.ascontiguousarray(L0_ffn_expert_w[c * EPC:(c + 1) * EPC]),
            "eb": np.ascontiguousarray(
                np.asarray(L0_ffn_expert_b)[c * EPC:(c + 1) * EPC].reshape(1, EPC * HID)),
        })
    res = run_bass_kernel_spmd(nc, in_maps, core_ids=list(range(NCORES)))
    return np.stack([r["out"] for r in res.results], axis=0)


# version 12
# speedup vs baseline: 1.0700x; 1.0154x over previous
import numpy as np
import contextlib
import concourse.bass as bass
import concourse.mybir as mybir
from concourse.bass_utils import run_bass_kernel_spmd

F32 = mybir.dt.float32
BF16 = mybir.dt.bfloat16
AF = mybir.ActivationFunctionType
ALU = mybir.AluOpType
AX = mybir.AxisListType

NCORES = 8
DM = 1024
H = 16
DH = 64
E = 2048
NEXP = 64
PSLOT = 32
HID = 1024
TT = 512
LN_EPS = 1e-5
QSCALE = 0.125
CLIP_LO = -33000.0
CLIP_HI = 65000.0
EPC = NEXP // NCORES
DEBUG_STOP = None
DEBUG_SCRATCH = False
ESL = E // NCORES


class Sem:
    __slots__ = ("h", "n")

    def __init__(self, h):
        self.h = h
        self.n = 0


def inc1(ins, sem):
    sem.n += 1
    ins.then_inc(sem.h, 1)
    return sem.n


def inc16(ins, sem):
    sem.n += 16
    ins.then_inc(sem.h, 16)
    return sem.n


def dma_tr(sy, out_ap, in_ap, sem):
    F = out_ap.shape[1]
    tgt = sem.n
    for f0 in range(0, F, 512):
        f1 = min(f0 + 512, F)
        ins = sy.dma_start(out=out_ap[:, f0:f1], in_=in_ap[f0:f1, :], transpose=True)
        tgt = inc16(ins, sem)
    return tgt




def _debug_dump(nc, block, dhw, dsw, d_sp, out_h, T, waits):
    @block.gpsimd
    def _(g):
        for s, tgt in waits:
            g.wait_ge(s.h, tgt)
        g.dma_start(out=out_h[:, :], in_=d_sp[0:T, 0:DM]).then_inc(dsw.h, 16)
        dsw.n += 16
        g.wait_ge(dsw.h, dsw.n)

def build_program(T=8192):
    assert T % TT == 0
    NT = T // TT
    NSUB = TT // 128
    NSUBT = T // 128

    nc = bass.Bass("TRN2", target_bir_lowering=False, detect_race_conditions=False)

    x_h = nc.dram_tensor("x", [T, DM], F32, kind="ExternalInput")
    wqkv_h = nc.dram_tensor("wqkv", [3 * DM, DM], F32, kind="ExternalInput")
    wo_h = nc.dram_tensor("wo", [DM, DM], F32, kind="ExternalInput")
    phiw_h = nc.dram_tensor("phiw", [E, DM], F32, kind="ExternalInput")
    phib_h = nc.dram_tensor("phib", [1, E], F32, kind="ExternalInput")
    ew_h = nc.dram_tensor("ew", [EPC, DM, HID], F32, kind="ExternalInput")
    eb_h = nc.dram_tensor("eb", [1, EPC * HID], F32, kind="ExternalInput")
    out_h = nc.dram_tensor("out", [T, DM], F32, kind="ExternalOutput")

    wqkv_bs = nc.dram_tensor("wqkv_bs", [3 * DM, DM], BF16)
    phiw_bs = nc.dram_tensor("phiw_bs", [E, DM], BF16)
    wo_bs = nc.dram_tensor("wo_bs", [DM, DM], BF16)
    xn_sp = nc.dram_tensor("xn_sp", [T, DM], BF16)
    k_sp = nc.dram_tensor("k_sp", [DM, T], BF16)
    attn_sp = nc.dram_tensor("attn_sp", [DM, T], BF16)
    c_sp = nc.dram_tensor("c_sp", [T, E], BF16)
    d_sp = nc.dram_tensor("d_sp", [T, E], BF16)
    x_bounce = nc.dram_tensor("x_bounce", [E, DM], F32)
    x_rs = nc.dram_tensor("x_rs", [ESL, DM], F32)
    yt_slice = nc.dram_tensor("yt_slice", [ESL, HID], BF16)
    yt_full = nc.dram_tensor("yt_full", [E, HID], BF16)

    st = contextlib.ExitStack()
    with st:
        dhw = Sem(st.enter_context(nc.semaphore("dhw")))
        dtr = Sem(st.enter_context(nc.semaphore("dtr")))
        dsc = Sem(st.enter_context(nc.semaphore("dsc")))
        dsw = Sem(st.enter_context(nc.semaphore("dsw")))
        cc = Sem(st.enter_context(nc.semaphore("cc")))
        ts = Sem(st.enter_context(nc.semaphore("ts")))
        vs = Sem(st.enter_context(nc.semaphore("vs")))
        ss = Sem(st.enter_context(nc.semaphore("ss")))
        gs = Sem(st.enter_context(nc.semaphore("gs")))

        pS = st.enter_context(nc.psum_tensor("pS", [128, 512], F32))
        pPool = st.enter_context(nc.psum_tensor("pPool", [128, 2560], F32))
        pT = st.enter_context(nc.psum_tensor("pT", [128, 1024], BF16))

        S_sb = st.enter_context(nc.sbuf_tensor("S_sb", [128, 1024], BF16))
        maskT = st.enter_context(nc.sbuf_tensor("maskT", [128, 128], BF16))
        ident = st.enter_context(nc.sbuf_tensor("ident", [128, 128], BF16))
        ones1 = st.enter_context(nc.sbuf_tensor("ones1", [1, 128], BF16))
        zeros = st.enter_context(nc.sbuf_tensor("zerosb", [1, 512], BF16))
        zk = st.enter_context(nc.sbuf_tensor("zk", [128, 512], BF16))
        phib_sb = st.enter_context(nc.sbuf_tensor("phib_sb", [1, E], BF16))
        eps_sb = st.enter_context(nc.sbuf_tensor("eps_sb", [128, 1], F32))

        block = st.enter_context(nc.Block())

        b01_free = (ss, 0)
        b23_free = (vs, 0)

        @block.gpsimd
        def _(g):
            g.memset(maskT[:], 1.0)
            g.memset(ident[:], 0.0)
            g.memset(ones1[:], 1.0)
            g.memset(zeros[:], 0.0)
            g.memset(zk[:], 0.0)
            g.memset(S_sb[:], 0.0)
            g.memset(eps_sb[:], LN_EPS)
            g.affine_select(out=maskT[:], in_=maskT[:], compare_op=ALU.is_ge,
                            fill=0.0, base=0, pattern=[[1, 128]], channel_multiplier=-1)
            ins_last = g.affine_select(out=ident[:], in_=ident[:], compare_op=ALU.not_equal,
                            fill=1.0, base=0, pattern=[[-1, 128]], channel_multiplier=1)
            g_consts = inc1(ins_last, gs)
            for pc in range(6):
                g.dma_start(out=wqkv_bs[pc * 512:(pc + 1) * 512, :],
                            in_=wqkv_h[pc * 512:(pc + 1) * 512, :]).then_inc(dsw.h, 16)
            g.dma_start(out=phib_sb[:, :], in_=phib_h[:, :]).then_inc(dsw.h, 16)
            for pc in range(4):
                g.dma_start(out=phiw_bs[pc * 512:(pc + 1) * 512, :],
                            in_=phiw_h[pc * 512:(pc + 1) * 512, :]).then_inc(dsw.h, 16)
            for pc in range(2):
                g.dma_start(out=wo_bs[pc * 512:(pc + 1) * 512, :],
                            in_=wo_h[pc * 512:(pc + 1) * 512, :]).then_inc(dsw.h, 16)
            dsw.n += 13 * 16
        g_consts = gs.n
        d_wo = dsw.n
        d_wcast = d_wo - 2 * 16
        d_cq = d_wo - 11 * 16
        d_ck = d_wo - 9 * 16
        d_cv = d_wo - 7 * 16

        @block.tensor
        def _(t):
            t.wait_ge(gs.h, g_consts)
            t.matmul(pS[:, :], zk[:, 0:128], zk[:, 0:512],
                     start=True, stop=False, skip_group_check=True)

        with contextlib.ExitStack() as st1:
            wqkvT = st1.enter_context(nc.sbuf_tensor("wqkvT", [128, 8 * 3 * DM], BF16))
            phiT = st1.enter_context(nc.sbuf_tensor("phiT", [128, 8 * E], BF16))
            attn_buf = st1.enter_context(nc.sbuf_tensor("attn_buf", [128, 8 * TT], BF16))
            x_sub = st1.enter_context(nc.sbuf_tensor("x_sub", [128, 4 * DM], F32))

            d_parts = [0, 0, 0]

            @block.sync
            def _(sy):
                for sec in range(3):
                    sy.wait_ge(dsw.h, (d_cq, d_ck, d_cv)[sec])
                    for dc in range(8):
                        for f0 in (sec * DM, sec * DM + 512):
                            ins = sy.dma_start(
                                out=wqkvT[:, dc * 3 * DM + f0:
                                          dc * 3 * DM + f0 + 512],
                                in_=wqkv_bs[f0:f0 + 512,
                                            dc * 128:(dc + 1) * 128],
                                transpose=True)
                            inc16(ins, dtr)
                    d_parts[sec] = dtr.n
                sy.wait_ge(dsw.h, d_wcast)
                for dc in range(8):
                    dma_tr(sy, phiT[:, dc * E:(dc + 1) * E],
                           phiw_bs[:, dc * 128:(dc + 1) * 128], dtr)
            d_wT = dtr.n
            d_wq, d_wk, d_wv = d_parts
            xn_st2 = st1.enter_context(nc.sbuf_tensor("xn_st2", [128, 4 * DM], BF16))
            sqt = st1.enter_context(nc.sbuf_tensor("sqt", [128, 1024], F32))
            s1 = st1.enter_context(nc.sbuf_tensor("s1", [128, 4], F32))
            s2 = st1.enter_context(nc.sbuf_tensor("s2", [128, 4], F32))
            m_sb = st1.enter_context(nc.sbuf_tensor("m_sb", [128, 4], F32))
            vneg = st1.enter_context(nc.sbuf_tensor("vneg", [128, 4], F32))
            junk1 = st1.enter_context(nc.sbuf_tensor("junk1", [128, 1], F32))
            rstd = st1.enter_context(nc.sbuf_tensor("rstd", [128, 4], F32))
            xn_fm = st1.enter_context(nc.sbuf_tensor("xn_fm", [128, 2 * 8 * TT], BF16))
            q_fm = st1.enter_context(nc.sbuf_tensor("q_fm", [128, 8 * TT], BF16))
            k_fm = st1.enter_context(nc.sbuf_tensor("k_fm", [128, 8 * TT], BF16))
            k_tm = st1.enter_context(nc.sbuf_tensor("k_tm", [128, 2 * DM], BF16))
            v_tm = st1.enter_context(nc.sbuf_tensor("v_tm", [128, 2 * DM], BF16))
            scM = st1.enter_context(nc.sbuf_tensor("scM", [128, 16 * 128], BF16))
            tinv = st1.enter_context(nc.sbuf_tensor("tinv", [128, 2 * 128], F32))
            e1b = st1.enter_context(nc.sbuf_tensor("e1b", [128, 2 * E], BF16))
            c_ob = st1.enter_context(nc.sbuf_tensor("c_ob", [128, 2 * E], BF16))
            sD = st1.enter_context(nc.sbuf_tensor("sD", [128, 64], F32))
            sC = st1.enter_context(nc.sbuf_tensor("sC", [128, 32], F32))
            rD = st1.enter_context(nc.sbuf_tensor("rD", [128, 64], F32))
            rC = st1.enter_context(nc.sbuf_tensor("rC", [128, 32], F32))

            x_done = [0, 0, 0, 0]
            x_used = [0]
            xst_done = [0]
            e1_free = [0, 0]
            c_free = [0, 0]
            kv_used = [0, 0]
            v_evac = [0, 0]
            ktm_done = [0, 0]
            tinv_done = [0, 0]
            tinv_used = [0, 0]
            ic_evac_prev = 0
            scM_used_prev = 0
            attn_stored_prev = 0
            xnfm_used_slot = [0, 0]
            xnfm_done_slot = [0, 0]
            qk_used_prev = 0
            s_scopy = 0

            def emit_ln_stats(lt, s_i):
                tok0 = lt * TT
                r0 = tok0 + s_i * 128

                @block.sync
                def _(sy, s_i=s_i, r0=r0):
                    if x_used[0]:
                        sy.wait_ge(vs.h, x_used[0])
                    ins = sy.dma_start(out=x_sub[:, s_i * DM:(s_i + 1) * DM],
                                       in_=x_h[r0:r0 + 128, :])
                    x_done[s_i] = inc16(ins, dhw)

                @block.vector
                def _(v, s_i=s_i):
                    v.wait_ge(dhw.h, x_done[s_i])
                    xs = x_sub[:, s_i * DM:(s_i + 1) * DM]
                    v.tensor_mul(sqt[:], xs, xs)
                    v.reduce_sum(s1[:, s_i:s_i + 1], xs, axis=AX.X)
                    v.reduce_sum(s2[:, s_i:s_i + 1], sqt[:], axis=AX.X)
                    v.tensor_scalar_mul(m_sb[:, s_i:s_i + 1], s1[:, s_i:s_i + 1],
                                        1.0 / DM)
                    v.tensor_copy(junk1[:], s1[:, 0:1])
                    ins = v.scalar_tensor_tensor(vneg[:, s_i:s_i + 1],
                                                 s1[:, s_i:s_i + 1],
                                                 m_sb[:, s_i:s_i + 1],
                                                 s2[:, s_i:s_i + 1],
                                                 ALU.mult, ALU.subtract)
                    inc1(ins, vs)

            def emit_ln_tail(lt):
                tok0 = lt * TT
                xo = (lt % 2) * 8 * TT
                v_stats = vs.n

                @block.scalar
                def _(s, v_stats=v_stats):
                    s.wait_ge(vs.h, v_stats)
                    ins = s.activation(rstd[:], vneg[:], AF.Sqrt,
                                       bias=eps_sb[:, 0:1], scale=-1.0 / DM)
                    inc1(ins, ss)
                s_sqrt = ss.n

                for s_i in range(NSUB):
                    @block.vector
                    def _(v, s_i=s_i, s_sqrt=s_sqrt):
                        if s_i == 0:
                            v.wait_ge(ss.h, s_sqrt)
                            if xst_done[0]:
                                v.wait_ge(dhw.h, xst_done[0])
                            v.reciprocal(rstd[:], rstd[:])
                            v.tensor_copy(junk1[:], m_sb[:, 0:1])
                        xs = x_sub[:, s_i * DM:(s_i + 1) * DM]
                        ins = v.tensor_scalar(xn_st2[:, s_i * DM:(s_i + 1) * DM], xs,
                                              m_sb[:, s_i:s_i + 1],
                                              rstd[:, s_i:s_i + 1],
                                              ALU.subtract, ALU.mult)
                        if s_i == NSUB - 1:
                            x_used[0] = inc1(ins, vs)
                        else:
                            inc1(ins, vs)
                    v_n = vs.n

                    @block.sync
                    def _(sy, s_i=s_i, v_n=v_n, tok0=tok0):
                        sy.wait_ge(vs.h, v_n)
                        r0 = tok0 + s_i * 128
                        ins = sy.dma_start(
                            out=xn_sp[r0:r0 + 128, :],
                            in_=xn_st2[:, s_i * DM:(s_i + 1) * DM])
                        xst_done[0] = inc16(ins, dhw)
                xn_all_stored = dhw.n

                @block.sync
                def _(sy, tok0=tok0, lt=lt, xo=xo, xn_all_stored=xn_all_stored):
                    sy.wait_ge(dhw.h, xn_all_stored)
                    if xnfm_used_slot[lt % 2]:
                        sy.wait_ge(ts.h, xnfm_used_slot[lt % 2])
                    for dc in range(8):
                        ins = sy.dma_start(
                            out=xn_fm[:, xo + dc * TT:xo + (dc + 1) * TT],
                            in_=xn_sp[tok0:tok0 + TT, dc * 128:(dc + 1) * 128],
                            transpose=True)
                        inc16(ins, dtr)
                xnfm_done_slot[lt % 2] = dtr.n

            def emit_ln(lt):
                for s_i in range(NSUB):
                    emit_ln_stats(lt, s_i)
                emit_ln_tail(lt)

            emit_ln(0)
            for t_i in range(NT):
                tok0 = t_i * TT
                xo = (t_i % 2) * 8 * TT
                xnfm_done = xnfm_done_slot[t_i % 2]

                rot_free = [b01_free, b01_free]
                for which in range(2):
                    sec0 = which * DM
                    for rc in range(8):
                        rot = rc % 2
                        pslice = pPool[:, rot * 512:rot * 512 + 512]

                        @block.tensor
                        def _(t, which=which, rc=rc, rot=rot, pslice=pslice, sec0=sec0, xo=xo):
                            if which == 0 and rc == 0:
                                t.wait_ge(dtr.h, xnfm_done)
                                if t_i == 0:
                                    t.wait_ge(dtr.h, d_wq)
                            if which == 1 and rc == 0 and t_i == 0:
                                t.wait_ge(dtr.h, d_wk)
                                if qk_used_prev:
                                    t.wait_ge(ts.h, qk_used_prev)
                            fsem, ftgt = rot_free[rot]
                            if ftgt:
                                t.wait_ge(fsem.h, ftgt)
                            for dc in range(8):
                                lhs = wqkvT[:, dc * 3 * DM + sec0 + rc * 128:
                                            dc * 3 * DM + sec0 + rc * 128 + 128]
                                rhs = xn_fm[:, xo + dc * TT:xo + dc * TT + TT]
                                ins_last = t.matmul(pslice, lhs, rhs,
                                         start=(dc == 0), stop=(dc == 7),
                                         skip_group_check=True)
                            inc1(ins_last, ts)
                        t_proj = ts.n

                        @block.scalar
                        def _(s, which=which, rc=rc, pslice=pslice, t_proj=t_proj):
                            s.wait_ge(ts.h, t_proj)
                            if which == 0:
                                ins = s.activation(q_fm[:, rc * TT:(rc + 1) * TT],
                                                   pslice, AF.Identity, scale=QSCALE)
                            else:
                                ins = s.activation(k_fm[:, rc * TT:(rc + 1) * TT],
                                                   pslice, AF.Sign)
                            inc1(ins, ss)
                        rot_free[rot] = (ss, ss.n)

                        if which == 1:
                            k_ev = ss.n

                            @block.sync
                            def _(sy, rc=rc, tok0=tok0, k_ev=k_ev):
                                sy.wait_ge(ss.h, k_ev)
                                ins = sy.dma_start(
                                    out=k_sp[rc * 128:(rc + 1) * 128, tok0:tok0 + TT],
                                    in_=k_fm[:, rc * TT:(rc + 1) * TT])
                                inc16(ins, dhw)
                k_stored = dhw.n
                b01_free = rot_free[1] if rot_free[1][1] >= rot_free[0][1] else rot_free[0]

                for s_i in range(NSUB):
                    sl = s_i % 2
                    ci = tok0 // 128 + s_i

                    @block.sync
                    def _(sy, sl=sl, ci=ci):
                        sy.wait_ge(dhw.h, k_stored)
                        if kv_used[sl]:
                            sy.wait_ge(ts.h, kv_used[sl])
                        ktm_done[sl] = dma_tr(
                            sy, k_tm[:, sl * DM:(sl + 1) * DM],
                            k_sp[:, ci * 128:(ci + 1) * 128], dtr)

                    @block.gpsimd
                    def _(g, sl=sl, ci=ci):
                        if tinv_used[sl]:
                            g.wait_ge(vs.h, tinv_used[sl])
                        ins = g.iota(tinv[:, sl * 128:(sl + 1) * 128], [[1, 128]],
                                     base=ci * 128 + 1, channel_multiplier=0,
                                     allow_small_or_imprecise_dtypes=True)
                        inc1(ins, gs)
                    g_iota = gs.n

                    @block.vector
                    def _(v, sl=sl, g_iota=g_iota):
                        v.wait_ge(gs.h, g_iota)
                        ins = v.reciprocal(tinv[:, sl * 128:(sl + 1) * 128],
                                           tinv[:, sl * 128:(sl + 1) * 128])
                        tinv_done[sl] = inc1(ins, vs)

                    @block.tensor
                    def _(t, s_i=s_i, sl=sl, xo=xo, t_i=t_i):
                        if t_i == 0 and s_i == 0:
                            t.wait_ge(dtr.h, d_wv)
                        fsem, ftgt = b23_free
                        if ftgt:
                            t.wait_ge(fsem.h, ftgt)
                        if kv_used[sl]:
                            t.wait_ge(ts.h, 0)
                        for hh in range(2):
                            po = pPool[:, 1024 + hh * 512:1024 + hh * 512 + 512]
                            for dc in range(8):
                                lhs = xn_fm[:, xo + dc * TT + s_i * 128:
                                            xo + dc * TT + s_i * 128 + 128]
                                rhs = wqkvT[:, dc * 3 * DM + 2 * DM + hh * 512:
                                            dc * 3 * DM + 2 * DM + hh * 512 + 512]
                                ins_last = t.matmul(po, lhs, rhs, start=(dc == 0), stop=(dc == 7),
                                         skip_group_check=True)
                        inc1(ins_last, ts)
                    t_vproj = ts.n

                    @block.scalar
                    def _(s, sl=sl, t_vproj=t_vproj):
                        s.wait_ge(ts.h, t_vproj)
                        if kv_used[sl]:
                            s.wait_ge(ts.h, kv_used[sl])
                        ins = s.activation(v_tm[:, sl * DM:(sl + 1) * DM],
                                           pPool[:, 1024:2048], AF.Sign)
                        v_evac[sl] = inc1(ins, ss)
                    v_ready = ss.n

                    sc_wave_v = 0
                    for w in range(2):
                        @block.tensor
                        def _(t, w=w, s_i=s_i, sc_wave_v=sc_wave_v):
                            if w == 0:
                                fsem, ftgt = b01_free
                                if ftgt:
                                    t.wait_ge(fsem.h, ftgt)
                                if scM_used_prev:
                                    t.wait_ge(ts.h, 0)
                            else:
                                t.wait_ge(vs.h, sc_wave_v)
                            for b in range(2):
                                t.matmul(pPool[:, b * 512:b * 512 + 512],
                                         zk[:, 0:128], zk[:, 0:512],
                                         start=True, stop=False,
                                         skip_group_check=True)
                            ro = 64 * w
                            for hh in range(8):
                                h = 2 * hh + w
                                rc = h // 2
                                lhs = k_fm[ro:ro + 64, rc * TT + s_i * 128:
                                           rc * TT + s_i * 128 + 128]
                                rhs = q_fm[ro:ro + 64, rc * TT + s_i * 128:
                                           rc * TT + s_i * 128 + 128]
                                ins_last = t.matmul(pPool[:, hh * 128:hh * 128 + 128], lhs, rhs,
                                         start=False, stop=(hh % 4 == 3),
                                         skip_group_check=True,
                                         tile_position=(ro, 0))
                            inc1(ins_last, ts)
                        t_sc = ts.n

                        @block.vector
                        def _(v, w=w, t_sc=t_sc):
                            v.wait_ge(ts.h, t_sc)
                            if scM_used_prev:
                                v.wait_ge(ts.h, scM_used_prev)
                            src = pPool[:, 0:1024].rearrange("p (h i) -> p h i", h=8)
                            msk = maskT[:].rearrange("p (o i) -> p o i", o=1
                                                     ).broadcast_to([128, 8, 128])
                            dst = scM[:].rearrange("p (h q) -> p h q", h=8)[
                                :, :, w * 128:(w + 1) * 128]
                            inc1(v.tensor_tensor(dst, src, msk, ALU.mult), vs)
                        sc_wave_v = vs.n
                    sc_done = vs.n
                    b01_free = (vs, sc_done)

                    @block.tensor
                    def _(t, sl=sl, s_i=s_i, sc_done=sc_done):
                        t.wait_ge(vs.h, sc_done)
                        t.wait_ge(ss.h, v_ready)
                        t.wait_ge(ss.h, s_scopy)
                        if ic_evac_prev:
                            t.wait_ge(vs.h, ic_evac_prev)
                        for b in range(2):
                            t.matmul(pPool[:, 1024 + b * 512:1024 + b * 512 + 512],
                                     zk[:, 0:128], zk[:, 0:512],
                                     start=True, stop=False,
                                     skip_group_check=True)
                        for p in range(8):
                            pic = pPool[:, 1024 + p * 128:1024 + p * 128 + 128]
                            for par in range(2):
                                ro = 64 * par
                                sM = scM[:, (2 * p + par) * 128:(2 * p + par) * 128 + 128]
                                t.matmul(pic[ro:ro + 64, :],
                                         v_tm[:, sl * DM + p * 128 + ro:
                                              sl * DM + p * 128 + ro + 64],
                                         sM, start=False, stop=False,
                                         skip_group_check=True, tile_position=(0, ro))
                            ins_last = t.matmul(pic[:, :],
                                     S_sb[:, p * 128:p * 128 + 128],
                                     q_fm[:, p * TT + s_i * 128:
                                          p * TT + s_i * 128 + 128],
                                     start=False, stop=(p % 4 == 3),
                                     skip_group_check=True)
                        inc1(ins_last, ts)
                    t_ic = ts.n
                    scM_used_prev = t_ic

                    @block.tensor
                    def _(t, sl=sl):
                        t.wait_ge(dtr.h, ktm_done[sl])
                        for p in range(8):
                            for par in range(2):
                                ro = 64 * par
                                ins_last = t.matmul(pS[ro:ro + 64, p * 64:p * 64 + 64],
                                         k_tm[:, sl * DM + p * 128 + ro:
                                              sl * DM + p * 128 + ro + 64],
                                         v_tm[:, sl * DM + p * 128 + ro:
                                              sl * DM + p * 128 + ro + 64],
                                         start=False, stop=False,
                                         skip_group_check=True, tile_position=(0, ro))
                        kv_used[sl] = inc1(ins_last, ts)
                    t_ckv = ts.n

                    @block.scalar
                    def _(s, t_ckv=t_ckv):
                        s.wait_ge(ts.h, t_ckv)
                        dste = S_sb[0:64, :].rearrange("p (q c) -> p q c", q=8)[
                            :, :, 0:64]
                        srce = pS[0:64, :].rearrange("p (q c) -> p q c", q=8)
                        s.activation(dste, srce, AF.Identity)
                        dsto = S_sb[64:128, :].rearrange("p (q c) -> p q c", q=8)[
                            :, :, 64:128]
                        srco = pS[64:128, :].rearrange("p (q c) -> p q c", q=8)
                        inc1(s.activation(dsto, srco, AF.Identity), ss)
                    s_scopy = ss.n

                    @block.vector
                    def _(v, s_i=s_i, sl=sl, t_ic=t_ic):
                        v.wait_ge(ts.h, t_ic)
                        v.wait_ge(vs.h, tinv_done[sl])
                        if attn_stored_prev:
                            v.wait_ge(dhw.h, attn_stored_prev)
                        src = pPool[:, 1024:2048].rearrange("p (q i) -> p q i", q=8)
                        tv = tinv[:, sl * 128:(sl + 1) * 128].rearrange(
                            "p (o i) -> p o i", o=1).broadcast_to([128, 8, 128])
                        dst = attn_buf[:].rearrange("p (q c) -> p q c", q=8)[
                            :, :, s_i * 128:s_i * 128 + 128]
                        ins = v.tensor_tensor(dst, src, tv, ALU.mult)
                        tinv_used[sl] = inc1(ins, vs)
                    ic_evac_prev = vs.n
                    b23_free = (vs, vs.n)

                    if t_i + 1 < NT:
                        emit_ln_stats(t_i + 1, s_i)

                @block.sync
                def _(sy, tok0=tok0, ic_evac_prev=ic_evac_prev):
                    sy.wait_ge(vs.h, ic_evac_prev)
                    for p in range(8):
                        ins = sy.dma_start(
                            out=attn_sp[p * 128:(p + 1) * 128, tok0:tok0 + TT],
                            in_=attn_buf[:, p * TT:(p + 1) * TT])
                        inc16(ins, dhw)
                attn_stored_prev = dhw.n

                if t_i + 1 < NT:
                    emit_ln_tail(t_i + 1)

                rot_free = [b01_free, b01_free]
                for s_i in range(NSUB):
                    sl = s_i % 2
                    r0 = tok0 + s_i * 128
                    for eg in range(4):
                        rot = eg % 2
                        pslice = pPool[:, rot * 512:rot * 512 + 512]

                        @block.tensor
                        def _(t, s_i=s_i, eg=eg, rot=rot, pslice=pslice, xo=xo,
                              t_i=t_i):
                            if t_i == 0 and s_i == 0 and eg == 0:
                                t.wait_ge(dtr.h, d_wT)
                            fsem, ftgt = rot_free[rot]
                            if ftgt:
                                t.wait_ge(fsem.h, ftgt)
                            for dc in range(8):
                                lhs = xn_fm[:, xo + dc * TT + s_i * 128:
                                            xo + dc * TT + s_i * 128 + 128]
                                rhs = phiT[:, dc * E + eg * 512:dc * E + eg * 512 + 512]
                                t.matmul(pslice, lhs, rhs, start=(dc == 0), stop=False,
                                         skip_group_check=True)
                            ins_last = t.matmul(pslice, ones1[0:1, 0:128],
                                     phib_sb[0:1, eg * 512:eg * 512 + 512],
                                     start=False, stop=True, skip_group_check=True)
                            inc1(ins_last, ts)
                        t_lg = ts.n

                        @block.scalar
                        def _(s, sl=sl, eg=eg, pslice=pslice, t_lg=t_lg):
                            s.wait_ge(ts.h, t_lg)
                            if eg == 0 and e1_free[sl]:
                                s.wait_ge(dhw.h, e1_free[sl])
                            ins = s.activation(
                                e1b[:, sl * E + eg * 512:sl * E + eg * 512 + 512],
                                pslice, AF.Exp)
                            inc1(ins, ss)
                        rot_free[rot] = (ss, ss.n)
                    e1_ready = ss.n

                    @block.vector
                    def _(v, sl=sl, e1_ready=e1_ready):
                        v.wait_ge(ss.h, e1_ready)
                        e3 = e1b[:, sl * E:(sl + 1) * E].rearrange(
                            "p (n q) -> p n q", n=64, q=32)
                        e3s = e1b[:, sl * E:(sl + 1) * E].rearrange(
                            "p (n q) -> p q n", n=64, q=32)
                        v.tensor_reduce(sC[:].rearrange("p (q o) -> p q o", o=1),
                                        e3s, axis=AX.X, op=ALU.add)
                        v.tensor_reduce(sD[:].rearrange("p (n o) -> p n o", o=1),
                                        e3, axis=AX.X, op=ALU.add)
                        v.reciprocal(rC[:], sC[:])
                        v.reciprocal(rD[:], sD[:])
                        if c_free[sl]:
                            v.wait_ge(dhw.h, c_free[sl])
                        e3 = e1b[:, sl * E:(sl + 1) * E].rearrange(
                            "p (n q) -> p n q", n=64, q=32)
                        co = c_ob[:, sl * E:(sl + 1) * E].rearrange(
                            "p (n q) -> p n q", n=64, q=32)
                        rc_b = rC[:].rearrange("p (o q) -> p o q", o=1
                                               ).broadcast_to([128, 64, 32])
                        v.tensor_tensor(co, e3, rc_b, ALU.mult)
                        rd_b = rD[:].rearrange("p (n o) -> p n o", o=1
                                               ).broadcast_to([128, 64, 32])
                        inc1(v.tensor_tensor(e3, e3, rd_b, ALU.mult), vs)
                    cd_done = vs.n

                    @block.sync
                    def _(sy, sl=sl, r0=r0, cd_done=cd_done):
                        sy.wait_ge(vs.h, cd_done)
                        ins = sy.dma_start(out=c_sp[r0:r0 + 128, :],
                                           in_=c_ob[:, sl * E:(sl + 1) * E])
                        c_free[sl] = inc16(ins, dhw)
                        ins = sy.dma_start(out=d_sp[r0:r0 + 128, :],
                                           in_=e1b[:, sl * E:(sl + 1) * E])
                        e1_free[sl] = inc16(ins, dhw)
                xnfm_used_slot[t_i % 2] = ts.n
                qk_used_prev = scM_used_prev
                b01_free = rot_free[0] if rot_free[0][1] >= rot_free[1][1] else rot_free[1]

            p1_dhw, p1_ts, p1_vs, p1_ss, p1_gs = dhw.n, ts.n, vs.n, ss.n, gs.n

        if DEBUG_STOP == "pass1":
            _debug_dump(nc, block, dhw, dsw, d_sp, out_h, T,
                        [(dhw, p1_dhw), (ts, p1_ts), (vs, p1_vs), (ss, p1_ss), (gs, p1_gs)])
            return nc

        with contextlib.ExitStack() as st2:
            xn_full = st2.enter_context(
                nc.sbuf_tensor("xn_full", [128, NSUBT * DM], BF16))
            d_col = st2.enter_context(
                nc.sbuf_tensor("d_col", [128, 3 * NSUBT * 128], BF16))
            x_stage = st2.enter_context(nc.sbuf_tensor("x_stage", [128, 2 * 2 * DM], F32))

            @block.sync
            def _(sy):
                sy.wait_ge(dhw.h, p1_dhw)
                sy.wait_ge(ts.h, p1_ts)
                sy.wait_ge(vs.h, p1_vs)
                sy.wait_ge(ss.h, p1_ss)
                sy.wait_ge(gs.h, p1_gs)
                nq = min(8, T // 128)
                rows = T // nq
                for i in range(nq):
                    src = xn_sp[i * rows:(i + 1) * rows, :].rearrange(
                        "(s p) d -> p s d", p=128)
                    dst = xn_full[:, i * (rows // 128) * DM:
                                  (i + 1) * (rows // 128) * DM].rearrange(
                        "p (s d) -> p s d", d=DM)
                    inc16(sy.dma_start(out=dst, in_=src), dhw)
            xnfull_done = dhw.n

            dcol_done = [0, 0, 0]
            dcol_used = [0, 0, 0]
            xev = [0, 0, 0, 0]
            xst_t = [0, 0]
            for ec in range(16):
                sl = ec % 3

                @block.sync
                def _(sy, ec=ec, sl=sl):
                    if dcol_used[sl]:
                        sy.wait_ge(ts.h, dcol_used[sl])
                    nq = max(1, T // 8192)
                    rows = T // nq
                    for i in range(nq):
                        src = d_sp[i * rows:(i + 1) * rows, ec * 128:(ec + 1) * 128
                                   ].rearrange("(s p) e -> p s e", p=128)
                        dst = d_col[:, sl * NSUBT * 128 + i * rows:
                                    sl * NSUBT * 128 + (i + 1) * rows
                                    ].rearrange("p (s e) -> p s e", e=128)
                        inc16(sy.dma_start(out=dst, in_=src), dhw)
                    dcol_done[sl] = dhw.n

                for dh_i in range(2):
                    bk = (ec % 2) * 2 + dh_i
                    pslice = pPool[:, bk * 512:bk * 512 + 512]

                    @block.tensor
                    def _(t, ec=ec, sl=sl, dh_i=dh_i, pslice=pslice):
                        if ec == 0 and dh_i == 0:
                            t.wait_ge(dhw.h, xnfull_done)
                        t.wait_ge(dhw.h, dcol_done[sl])
                        if xev[(ec % 2) * 2 + dh_i]:
                            t.wait_ge(vs.h, xev[(ec % 2) * 2 + dh_i])
                        for sub in range(NSUBT):
                            lhs = d_col[:, sl * NSUBT * 128 + sub * 128:
                                        sl * NSUBT * 128 + sub * 128 + 128]
                            rhs = xn_full[:, sub * DM + dh_i * 512:
                                          sub * DM + dh_i * 512 + 512]
                            ins_last = t.matmul(pslice, lhs, rhs, start=(sub == 0),
                                     stop=(sub == NSUBT - 1), skip_group_check=True)
                        if dh_i == 1:
                            dcol_used[sl] = ts.n + 1
                        inc1(ins_last, ts)
                    t_x = ts.n

                    @block.vector
                    def _(v, ec=ec, dh_i=dh_i, pslice=pslice, t_x=t_x):
                        v.wait_ge(ts.h, t_x)
                        if xst_t[ec % 2]:
                            v.wait_ge(dsc.h, xst_t[ec % 2])
                        so_ = (ec % 2) * 2 * 512
                        ins = v.tensor_copy(
                            x_stage[:, so_ + dh_i * 512:so_ + dh_i * 512 + 512],
                            pslice)
                        xev[(ec % 2) * 2 + dh_i] = inc1(ins, vs)
                v_x = vs.n

                @block.scalar
                def _(sy, ec=ec, v_x=v_x):
                    sy.wait_ge(vs.h, v_x)
                    so_ = (ec % 2) * DM
                    ins = sy.dma_start(out=x_bounce[ec * 128:(ec + 1) * 128, :],
                                       in_=x_stage[:, so_:so_ + DM])
                    xst_t[ec % 2] = inc16(ins, dsc)
            xb_done = dsc.n

            if DEBUG_STOP == "xphase":
                _debug_dump(nc, block, dhw, dsw, d_sp, out_h, T,
                            [(dhw, xb_done), (ts, ts.n), (vs, vs.n)])
                return nc

            @block.gpsimd
            def _(g):
                g.wait_ge(dsc.h, xb_done)
                ins = g.collective_compute(
                    "ReduceScatter", ALU.add,
                    replica_groups=[list(range(NCORES))],
                    ins=[x_bounce.ap().opt()], outs=[x_rs.ap().opt()])
                inc1(ins, cc)
            rs_done = cc.n

            if DEBUG_STOP == "exchange":
                _debug_dump(nc, block, dhw, dsw, d_sp, out_h, T, [(cc, rs_done)])
                return nc

        with contextlib.ExitStack() as st3:
            xt_f = st3.enter_context(nc.sbuf_tensor("xt_f", [128, 2 * DM], F32))
            xt_b = st3.enter_context(nc.sbuf_tensor("xt_b", [128, 2 * DM], BF16))
            xtT = st3.enter_context(nc.sbuf_tensor("xtT", [128, 8 * ESL], BF16))
            ew_sb = st3.enter_context(nc.sbuf_tensor("ew_sb", [128, 2 * 8 * HID], BF16))
            eb_sb = st3.enter_context(nc.sbuf_tensor("eb_sb", [1, EPC * HID], BF16))
            yt_st = st3.enter_context(nc.sbuf_tensor("yt_st", [128, 2 * HID], BF16))
            sqtb = st3.enter_context(nc.sbuf_tensor("sqtb", [128, 1024], F32))
            s1b = st3.enter_context(nc.sbuf_tensor("s1b", [128, 1], F32))
            s2b = st3.enter_context(nc.sbuf_tensor("s2b", [128, 1], F32))
            m_sbb = st3.enter_context(nc.sbuf_tensor("m_sbb", [128, 1], F32))
            vnegb = st3.enter_context(nc.sbuf_tensor("vnegb", [128, 1], F32))
            junkb = st3.enter_context(nc.sbuf_tensor("junkb", [128, 1], F32))
            rstdb = st3.enter_context(nc.sbuf_tensor("rstdb", [128, 1], F32))

            ew_done = {}
            ew_used = [0, 0]

            @block.gpsimd
            def _(g):
                g.wait_ge(dsc.h, xb_done)
                inc16(g.dma_start(out=eb_sb[:, :], in_=eb_h[:, :]), dsw)
                for n in range(2):
                    src = ew_h[n, :, :].rearrange("(c p) h -> p c h", p=128)
                    dst = ew_sb[:, n * 8 * HID:(n + 1) * 8 * HID].rearrange(
                        "p (c h) -> p c h", h=HID)
                    ew_done[n] = inc16(g.dma_start(out=dst, in_=src), dsw)
            eb_done = dsw.n

            @block.sync
            def _(sy):
                sy.wait_ge(cc.h, rs_done)
                for i in range(2):
                    ins = sy.dma_start(out=xt_f[:, i * DM:(i + 1) * DM],
                                       in_=x_rs[i * 128:(i + 1) * 128, :])
                    inc16(ins, dhw)
            xrs_loaded = dhw.n

            for i in range(2):
                @block.vector
                def _(v, i=i):
                    v.wait_ge(dhw.h, xrs_loaded)
                    xs = xt_f[:, i * DM:(i + 1) * DM]
                    v.tensor_scalar(xs, xs, CLIP_LO, CLIP_HI, ALU.max, ALU.min)
                    v.tensor_copy(junkb[:], eps_sb[:])
                    v.tensor_mul(sqtb[:], xs, xs)
                    v.reduce_sum(s1b[:], xs, axis=AX.X)
                    v.reduce_sum(s2b[:], sqtb[:], axis=AX.X)
                    v.tensor_scalar_mul(m_sbb[:], s1b[:], 1.0 / DM)
                    v.tensor_copy(junkb[:], s1b[:])
                    ins = v.scalar_tensor_tensor(vnegb[:], s1b[:], m_sbb[:, 0:1],
                                                 s2b[:], ALU.mult, ALU.subtract)
                    inc1(ins, vs)
                v_st = vs.n

                @block.scalar
                def _(s, v_st=v_st):
                    s.wait_ge(vs.h, v_st)
                    ins = s.activation(rstdb[:], vnegb[:], AF.Sqrt,
                                       bias=eps_sb[:, 0:1], scale=-1.0 / DM)
                    inc1(ins, ss)
                s_sq = ss.n

                @block.vector
                def _(v, i=i, s_sq=s_sq):
                    v.wait_ge(ss.h, s_sq)
                    v.reciprocal(rstdb[:], rstdb[:])
                    v.tensor_copy(junkb[:], m_sbb[:])
                    xs = xt_f[:, i * DM:(i + 1) * DM]
                    ins = v.tensor_scalar(xs, xs, m_sbb[:, 0:1], rstdb[:, 0:1],
                                          ALU.subtract, ALU.mult)
                    inc1(ins, vs)
            v_xt = vs.n

            @block.scalar
            def _(s):
                s.wait_ge(vs.h, v_xt)
                inc1(s.activation(xt_b[:], xt_f[:], AF.Tanh), ss)
            s_tanh = ss.n

            xtT_prev = [0]
            for i in range(2):
                for dc in range(8):
                    @block.tensor
                    def _(t, i=i, dc=dc):
                        if i == 0 and dc == 0:
                            t.wait_ge(ss.h, s_tanh)
                        else:
                            t.wait_ge(ss.h, xtT_prev[0])
                        ins = t.transpose(
                            pT[:, 0:128],
                            xt_b[:, i * DM + dc * 128:i * DM + dc * 128 + 128],
                            ident[:])
                        inc1(ins, ts)
                    t_tr = ts.n

                    @block.scalar
                    def _(s, i=i, dc=dc, t_tr=t_tr):
                        s.wait_ge(ts.h, t_tr)
                        ins = s.activation(
                            xtT[:, dc * ESL + i * 128:dc * ESL + i * 128 + 128],
                            pT[:, 0:128], AF.Identity)
                        inc1(ins, ss)
                    xtT_prev[0] = ss.n
            s_xtT = ss.n

            yt_ev = [0]
            for n in range(EPC):
                sl = n % 2
                if n >= 2:
                    @block.gpsimd
                    def _(g, n=n, sl=sl):
                        g.wait_ge(ts.h, ew_used[sl])
                        src = ew_h[n, :, :].rearrange("(c p) h -> p c h", p=128)
                        dst = ew_sb[:, sl * 8 * HID:(sl + 1) * 8 * HID].rearrange(
                            "p (c h) -> p c h", h=HID)
                        ew_done[n] = inc16(g.dma_start(out=dst, in_=src), dsw)

                @block.tensor
                def _(t, n=n, sl=sl):
                    if n == 0:
                        t.wait_ge(ss.h, s_xtT)
                        t.wait_ge(dsw.h, eb_done)
                    t.wait_ge(dsw.h, ew_done[n])
                    if yt_ev[0]:
                        t.wait_ge(ss.h, yt_ev[0])
                    for hh in range(2):
                        po = pPool[0:32, hh * 512:hh * 512 + 512]
                        for dc in range(8):
                            lhs = xtT[:, dc * ESL + n * 32:dc * ESL + n * 32 + 32]
                            rhs = ew_sb[:, sl * 8 * HID + dc * HID + hh * 512:
                                        sl * 8 * HID + dc * HID + hh * 512 + 512]
                            t.matmul(po, lhs, rhs, start=(dc == 0), stop=False,
                                     skip_group_check=True)
                        ins_last = t.matmul(po, ones1[0:1, 0:32],
                                 eb_sb[0:1, n * HID + hh * 512:n * HID + hh * 512 + 512],
                                 start=False, stop=True, skip_group_check=True)
                    ew_used[sl] = inc1(ins_last, ts)
                t_exp = ts.n

                @block.scalar
                def _(s, n=n, t_exp=t_exp):
                    s.wait_ge(ts.h, t_exp)
                    half = n // 4
                    row = 32 * (n % 4)
                    ins = s.activation(
                        yt_st[row:row + 32, half * HID:(half + 1) * HID],
                        pPool[0:32, 0:1024], AF.Identity)
                    yt_ev[0] = inc1(ins, ss)
            s_yt = ss.n

            @block.sync
            def _(sy):
                sy.wait_ge(ss.h, s_yt)
                for half in range(2):
                    ins = sy.dma_start(
                        out=yt_slice[half * 128:(half + 1) * 128, :],
                        in_=yt_st[:, half * HID:(half + 1) * HID])
                    inc16(ins, dhw)
            yts_done = dhw.n

            @block.gpsimd
            def _(g):
                g.wait_ge(dhw.h, yts_done)
                ins = g.collective_compute(
                    "AllGather", ALU.bypass,
                    replica_groups=[list(range(NCORES))],
                    ins=[yt_slice.ap().opt()], outs=[yt_full.ap().opt()])
                inc1(ins, cc)
            ag_done = cc.n
            exp_ts, exp_ss, exp_vs = ts.n, ss.n, vs.n

            if DEBUG_STOP == "experts":
                _debug_dump(nc, block, dhw, dsw, d_sp, out_h, T,
                            [(cc, ag_done), (ts, exp_ts), (ss, exp_ss)])
                return nc

        with contextlib.ExitStack() as st4:
            woT = st4.enter_context(nc.sbuf_tensor("woT", [128, 8 * DM], BF16))
            yt_sb = st4.enter_context(nc.sbuf_tensor("yt_sb", [128, 16 * HID], BF16))
            attn_t = st4.enter_context(nc.sbuf_tensor("attn_t", [128, 2 * 8 * TT], BF16))
            c_t = st4.enter_context(nc.sbuf_tensor("c_t", [128, 2 * 16 * TT], BF16))
            x2 = st4.enter_context(nc.sbuf_tensor("x2", [128, 2 * NSUB * DM], F32))
            o_st = st4.enter_context(nc.sbuf_tensor("o_st", [128, 2 * DM], F32))

            @block.sync
            def _(sy):
                sy.wait_ge(ts.h, exp_ts)
                sy.wait_ge(ss.h, exp_ss)
                sy.wait_ge(dsw.h, d_wo)
                for rc in range(8):
                    dma_tr(sy, woT[:, rc * DM:(rc + 1) * DM],
                           wo_bs[:, rc * 128:(rc + 1) * 128], dtr)
            p2_w = dhw.n
            p2_w_tr = dtr.n

            tile_loaded = [0, 0]
            tile_loaded_tr = [0, 0]
            tile_loaded_sc = [0, 0]
            yt_loaded = [0]
            tile_used = [0, 0]
            ost_done = [0, 0]
            p2_ev = [0, 0]
            for t_i in range(NT):
                tok0 = t_i * TT
                tl = t_i % 2


                @block.sync
                def _(sy, tok0=tok0, tl=tl, t_i=t_i):
                    if t_i == 0:
                        sy.wait_ge(dhw.h, p2_w)
                        sy.wait_ge(dtr.h, p2_w_tr)
                        sy.wait_ge(vs.h, exp_vs)
                    if tile_used[tl]:
                        sy.wait_ge(ts.h, tile_used[tl])
                    for rc in range(8):
                        ins = sy.dma_start(
                            out=attn_t[:, tl * 8 * TT + rc * TT:
                                       tl * 8 * TT + (rc + 1) * TT],
                            in_=attn_sp[rc * 128:(rc + 1) * 128, tok0:tok0 + TT])
                        inc16(ins, dhw)
                    for npc in range(16):
                        ins = sy.dma_start(
                            out=c_t[:, tl * 16 * TT + npc * TT:
                                    tl * 16 * TT + (npc + 1) * TT],
                            in_=c_sp[tok0:tok0 + TT, npc * 128:(npc + 1) * 128],
                            transpose=True)
                        inc16(ins, dtr)
                    src = x_h[tok0:tok0 + TT, :].rearrange("(s p) d -> p s d", p=128)
                    dst = x2[:, tl * NSUB * DM:(tl + 1) * NSUB * DM].rearrange(
                        "p (s d) -> p s d", d=DM)
                    tile_loaded[tl] = inc16(sy.dma_start(out=dst, in_=src), dhw)
                    tile_loaded_tr[tl] = dtr.n
                    if t_i == 0:
                        sy.wait_ge(cc.h, ag_done)
                        srcy = yt_full[:, :].rearrange("(c p) h -> p c h", p=128)
                        dsty = yt_sb[:].rearrange("p (c h) -> p c h", h=HID)
                        yt_loaded[0] = inc16(sy.dma_start(out=dsty, in_=srcy), dhw)

                for s_i in range(NSUB):
                    rot = s_i % 2
                    pslice = pPool[:, rot * 1024:rot * 1024 + 1024]

                    @block.tensor
                    def _(t, s_i=s_i, tl=tl, rot=rot, pslice=pslice):
                        t.wait_ge(dhw.h, tile_loaded[tl])
                        t.wait_ge(dtr.h, tile_loaded_tr[tl])
                        if p2_ev[rot]:
                            t.wait_ge(vs.h, p2_ev[rot])
                        for mh in range(2):
                            po = pslice[:, mh * 512:mh * 512 + 512]
                            for rc in range(8):
                                lhs = attn_t[:, tl * 8 * TT + rc * TT + s_i * 128:
                                             tl * 8 * TT + rc * TT + s_i * 128 + 128]
                                rhs = woT[:, rc * DM + mh * 512:rc * DM + mh * 512 + 512]
                                t.matmul(po, lhs, rhs, start=(rc == 0), stop=False,
                                         skip_group_check=True)
                            if t_i == 0 and s_i == 0 and mh == 0:
                                t.wait_ge(dhw.h, yt_loaded[0])
                            for npc in range(16):
                                lhs = c_t[:, tl * 16 * TT + npc * TT + s_i * 128:
                                          tl * 16 * TT + npc * TT + s_i * 128 + 128]
                                rhs = yt_sb[:, npc * HID + mh * 512:
                                            npc * HID + mh * 512 + 512]
                                ins_last = t.matmul(po, lhs, rhs, start=False, stop=(npc == 15),
                                         skip_group_check=True)
                        tgt = inc1(ins_last, ts)
                        if s_i == NSUB - 1:
                            tile_used[tl] = tgt
                    t_p2 = ts.n

                    @block.vector
                    def _(v, s_i=s_i, tl=tl, rot=rot, pslice=pslice, t_p2=t_p2):
                        v.wait_ge(ts.h, t_p2)
                        if ost_done[rot]:
                            v.wait_ge(dsc.h, ost_done[rot])
                        ins = v.tensor_tensor(
                            o_st[:, rot * DM:(rot + 1) * DM],
                            x2[:, tl * NSUB * DM + s_i * DM:
                               tl * NSUB * DM + (s_i + 1) * DM],
                            pslice, ALU.add)
                        p2_ev[rot] = inc1(ins, vs)
                    v_p2 = vs.n

                    @block.scalar
                    def _(s, tok0=tok0, s_i=s_i, rot=rot, v_p2=v_p2):
                        s.wait_ge(vs.h, v_p2)
                        r0 = tok0 + s_i * 128
                        ins = s.dma_start(out=out_h[r0:r0 + 128, :],
                                          in_=o_st[:, rot * DM:(rot + 1) * DM])
                        ost_done[rot] = inc16(ins, dsc)

            @block.sync
            def _(sy):
                sy.wait_ge(dhw.h, dhw.n)
                sy.wait_ge(dsc.h, dsc.n)
                sy.wait_ge(dtr.h, dtr.n)

        if DEBUG_SCRATCH:
            dbg = {
                "xn_d": (xn_sp, [T, DM], BF16),
                "k_d": (k_sp, [DM, T], BF16),
                "attn_d": (attn_sp, [DM, T], BF16),
                "c_d": (c_sp, [T, E], BF16),
                "dd_d": (d_sp, [T, E], BF16),
                "xb_d": (x_bounce, [E, DM], F32),
            }

            @block.gpsimd
            def _(g):
                g.wait_ge(dhw.h, dhw.n)
                for nm, (srct, shp, dt) in dbg.items():
                    oh = nc.dram_tensor(nm, shp, dt, kind="ExternalOutput")
                    inc16(g.dma_start(out=oh[:, :], in_=srct[:, :]), dsw)
                g.wait_ge(dsw.h, dsw.n)

    return nc


def kernel(L0_attn_x, L0_attn_W_qkv, L0_attn_W_o, L0_ffn_phi_w, L0_ffn_phi_b,
           L0_ffn_expert_w, L0_ffn_expert_b):
    x = np.asarray(L0_attn_x)
    B, T, D = x.shape
    nc = build_program(T=T)
    in_maps = []
    for c in range(NCORES):
        in_maps.append({
            "x": np.ascontiguousarray(x[c]),
            "wqkv": np.ascontiguousarray(L0_attn_W_qkv),
            "wo": np.ascontiguousarray(L0_attn_W_o),
            "phiw": np.ascontiguousarray(L0_ffn_phi_w),
            "phib": np.ascontiguousarray(np.asarray(L0_ffn_phi_b).reshape(1, E)),
            "ew": np.ascontiguousarray(L0_ffn_expert_w[c * EPC:(c + 1) * EPC]),
            "eb": np.ascontiguousarray(
                np.asarray(L0_ffn_expert_b)[c * EPC:(c + 1) * EPC].reshape(1, EPC * HID)),
        })
    res = run_bass_kernel_spmd(nc, in_maps, core_ids=list(range(NCORES)))
    return np.stack([r["out"] for r in res.results], axis=0)
